# version 06
# baseline (speedup 1.0000x reference)
import numpy as np
import ml_dtypes
import concourse.bass as bass
import concourse.mybir as mybir

F32 = mybir.dt.float32
BF16 = mybir.dt.bfloat16
AF = mybir.ActivationFunctionType
ALU = mybir.AluOpType
AX = mybir.AxisListType
bfd = ml_dtypes.bfloat16
U16 = mybir.dt.uint16

B, S, D = 4, 4096, 1024
H, HD = 16, 64
TOK, NT, QC, NG = 2048, 16, 4, 8
DCH = 8
MOE_D, MOE_E = 2048, 8
FF_H, FCH = 4096, 32
L1_NJ = 21
VCOL = 65
PH_ALL = ("l0qkv", "l0attn", "moe", "l1qkv", "l1attn", "l1ffn")


class Sem:
    def __init__(self, nc, name):
        self.h = nc.alloc_semaphore(name)
        self.val = 0

    def inc(self, ins, n=1):
        ins.then_inc(self.h, n)
        self.val += n
        return ins


def build(phases=PH_ALL, dbg=()):
    phases = list(phases)
    dbg = set(dbg)
    nc = bass.Bass("TRN2", target_bir_lowering=False, detect_race_conditions=False)
    inputs = {}
    dbg_names = []

    def dram_in(name, shape, dtype=F32):
        inputs[name] = (tuple(shape), dtype)
        return nc.dram_tensor(name, list(shape), dtype, kind="ExternalInput").ap()

    def dram_dbg(name, shape, dtype=F32):
        dbg_names.append(name)
        return nc.dram_tensor(name, list(shape), dtype, kind="ExternalOutput").ap()

    x_own = dram_in("x_own", [TOK, D])
    x_seq = dram_in("x_seq", [S, D])
    l0_wq = dram_in("l0_wq", [D, D]); l0_bq = dram_in("l0_bq", [D])
    l0_wk = dram_in("l0_wk", [D, D]); l0_bk = dram_in("l0_bk", [D])
    l0_wv = dram_in("l0_wv", [D, D]); l0_bv = dram_in("l0_bv", [D])
    l0_wo = dram_in("l0_wo", [D, D]); l0_bo = dram_in("l0_bo", [D])
    l0_winmask = dram_in("l0_winmask", [8, 128, 512], U16)
    moe_gw = dram_in("moe_gw", [D, MOE_E])
    moe_wg = dram_in("moe_wg", [MOE_E, D, MOE_D])
    moe_wu = dram_in("moe_wu", [MOE_E, D, MOE_D])
    moe_wd = dram_in("moe_wd", [MOE_E, MOE_D, D])
    moe_wsg = dram_in("moe_wsg", [D, MOE_D])
    moe_wsu = dram_in("moe_wsu", [D, MOE_D])
    moe_wsd = dram_in("moe_wsd", [MOE_D, D])
    l1_wqkvT = dram_in("l1_wqkvT", [D, 3 * D]); l1_bqkv = dram_in("l1_bqkv", [3 * D])
    l1_wgateT = dram_in("l1_wgateT", [D, D]); l1_bgate = dram_in("l1_bgate", [D])
    l1_woutT = dram_in("l1_woutT", [D, D]); l1_bout = dram_in("l1_bout", [D])
    l1_btile = dram_in("l1_btile", [L1_NJ, H, 128, 512], U16)
    l1_sb = dram_in("l1_sb", [128, QC, L1_NJ, 8])
    ff_wexp = dram_in("ff_wexp", [D, FF_H]); ff_bexp = dram_in("ff_bexp", [FF_H])
    ff_upT = dram_in("ff_upT", [7, FCH, 128, 128])
    ff_dnT = dram_in("ff_dnT", [7, FCH, 128, 128])
    ff_upw = dram_in("ff_upw", [12, FF_H // 2, 2, 2])
    ff_dnw = dram_in("ff_dnw", [12, FF_H // 2, 2, 2])
    ff_upb = dram_in("ff_upb", [FF_H])
    ff_wcon = dram_in("ff_wcon", [FF_H, D])
    ff_bsum = dram_in("ff_bsum", [2, D])
    out_own = nc.dram_tensor("out_own", [TOK, D], F32, kind="ExternalOutput").ap()

    l0k = nc.dram_tensor("l0k", [NG, DCH, 128, 512], BF16, kind="Internal").ap()
    l0v = nc.dram_tensor("l0v", [NG, 4, 128, VCOL * H], BF16, kind="Internal").ap()
    l1g = nc.dram_tensor("l1g", [DCH, 128, TOK], BF16, kind="Internal").ap()
    CHK = DCH * 128 * 512
    CHV = 4 * 128 * (VCOL * H)
    cc_in = nc.dram_tensor("cc_in", [QC, CHK + CHV], BF16, kind="Internal").ap()
    cc_out = nc.dram_tensor("cc_out", [2, QC, CHK + CHV], BF16, kind="Internal").ap()
    l1pad = nc.dram_tensor("l1pad", [4, CHK + CHV], BF16, kind="Internal").ap()

    def l1_kv(g):
        base = l1pad[(g % 4), :] if g < 0 else cc_out[g % 2, g // 2, :]
        k = base[0:CHK].rearrange("(c p t) -> c p t", c=DCH, p=128)
        v = base[CHK:CHK + CHV].rearrange("(f p t) -> f p t", f=4, p=128)
        return k, v

    ident_np = nc.inline_tensor(np.eye(128, dtype=bfd), "ident_bf")
    ones_np = nc.inline_tensor(np.ones((1, 128), dtype=bfd), "ones_row2")
    onesf_np = nc.inline_tensor(np.ones((1, 64), dtype=np.float32), "ones_f32")

    from contextlib import ExitStack
    with ExitStack() as _es:
        x_res = _es.enter_context(nc.sbuf_tensor("x_res", [128, NT, D], F32))
        identt = _es.enter_context(nc.sbuf_tensor("identt", [128, 128], BF16))
        onesr = _es.enter_context(nc.sbuf_tensor("onesr", [1, 128], BF16))
        onesf = _es.enter_context(nc.sbuf_tensor("onesf", [1, 64], F32))
        stats = _es.enter_context(nc.sbuf_tensor("stats", [128, 16, 8], F32))
        epscol = _es.enter_context(nc.sbuf_tensor("epscol", [128, 8], F32))
        scr = _es.enter_context(nc.sbuf_tensor("scr", [128, 1024], F32))
        scr2 = _es.enter_context(nc.sbuf_tensor("scr2", [128, 1024], F32))
        bias1 = _es.enter_context(nc.sbuf_tensor("bias1", [128, 64, 8], F32))
        bcast = _es.enter_context(nc.sbuf_tensor("bcast", [128, 3, 1024], BF16))
        lnbf = _es.enter_context(nc.sbuf_tensor("lnbf", [128, 1, 1024], BF16))
        xfm = _es.enter_context(nc.sbuf_tensor("xfm", [128, DCH, TOK], BF16))
        psA = _es.enter_context(nc.psum_tensor("psA", [128, 1024], F32))
        psB = _es.enter_context(nc.psum_tensor("psB", [128, 1024], F32))
        psC = _es.enter_context(nc.psum_tensor("psC", [128, 1024], F32))
        psD = _es.enter_context(nc.psum_tensor("psD", [128, 512], F32))
        psT = _es.enter_context(nc.psum_tensor("psT", [128, 1024], BF16))
        rrow = scr
        blk = _es.enter_context(nc.Block())
        S_ = {k: Sem(nc, k) for k in
              ["ldw", "ldx", "ldk", "mm", "tp", "ev", "dv", "ac", "kv", "cc", "gp", "st",
               "gw", "gx", "gz", "gst"]}
        BQ, BK, BV, BG1, BQK = 0, 8, 16, 24, 32

        def ln_sweep(tiles, dst_fm, dst_off, first_evac_wait=None, xsrc=None):
            n = len(tiles)
            ldx0, dv0, ac0, tp0, ev0 = (S_[k].val for k in ("ldx", "dv", "ac", "tp", "ev"))

            def f_sync(e):
                hseen = 0
                for i, (kind, ap_) in enumerate(tiles):
                    if kind == "hbm":
                        if hseen >= 2:
                            e.wait_ge(S_["dv"].h, dv0 + 2 * (i - 2) + 2)
                        S_["ldx"].inc(e.dma_start(out=xsrc[:, i % 2, :], in_=ap_), 16)
                        hseen += 1
            blk.sync(f_sync)

            def f_v(e):
                hseen = 0
                for i, (kind, ap_) in enumerate(tiles):
                    if kind == "hbm":
                        hseen += 1
                        e.wait_ge(S_["ldx"].h, ldx0 + 16 * hseen)
                        src = xsrc[:, i % 2, :]
                    else:
                        src = ap_
                    a = stats[:, (2 * i) % 8, :]
                    b = stats[:, (2 * i) % 8 + 1, :]
                    dmy = stats[:, 15, :]
                    e.tensor_scalar(scr[:], src, 1.0, 0.0, ALU.mult, ALU.add, accum_out=a[:, 2:3])
                    e.scalar_tensor_tensor(scr2[:], src, 1.0, src, ALU.mult, ALU.mult, accum_out=a[:, 3:4])
                    e.tensor_scalar_mul(a[:, 1:2], a[:, 2:3], 1.0 / D)
                    e.tensor_scalar_mul(a[:, 4:5], a[:, 3:4], 1.0 / D)
                    e.tensor_mul(a[:, 5:6], a[:, 1:2], a[:, 1:2])
                    e.tensor_copy(dmy[:, 0:1], a[:, 2:3])
                    e.tensor_copy(dmy[:, 1:2], a[:, 3:4])
                    S_["dv"].inc(e.tensor_sub(a[:, 0:1], a[:, 4:5], a[:, 5:6]))
                    e.wait_ge(S_["ac"].h, ac0 + i + 1)
                    if i >= 1:
                        e.wait_ge(S_["tp"].h, tp0 + i)
                    e.reciprocal(b[:, 1:2], b[:, 0:1])
                    e.tensor_copy(dmy[:, 0:1], a[:, 2:3])
                    e.tensor_copy(dmy[:, 1:2], a[:, 3:4])
                    S_["dv"].inc(e.tensor_scalar(lnbf[:, 0, :], src, a[:, 1:2], b[:, 1:2],
                                                 ALU.subtract, ALU.mult))

            def f_s(e):
                def evac(i):
                    e.wait_ge(S_["tp"].h, tp0 + i + 1)
                    if i == 0 and first_evac_wait is not None:
                        e.wait_ge(first_evac_wait[0].h, first_evac_wait[1])
                    o = dst_off + i * 128
                    S_["ev"].inc(e.copy(out=dst_fm[:, :, o:o + 128],
                                        in_=psT[:].rearrange("p (c x) -> p c x", c=DCH)))
                for i in range(n):
                    a = stats[:, (2 * i) % 8, :]
                    b = stats[:, (2 * i) % 8 + 1, :]
                    e.wait_ge(S_["dv"].h, dv0 + 2 * i + 1)
                    S_["ac"].inc(e.activation(out=b[:, 0:1], in_=a[:, 0:1], func=AF.Sqrt, bias=epscol[:, 0:1]))
                    if i >= 1:
                        evac(i - 1)
                evac(n - 1)

            def f_t(e):
                for i in range(n):
                    e.wait_ge(S_["dv"].h, dv0 + 2 * i + 2)
                    e.wait_ge(S_["ev"].h, ev0 + i)
                    for c in range(DCH):
                        ins = e.transpose(psT[:, c * 128:(c + 1) * 128],
                                          lnbf[:, 0, c * 128:(c + 1) * 128], identt[:])
                    S_["tp"].inc(ins)
            blk.vector(f_v)
            blk.scalar(f_s)
            blk.tensor(f_t)

        def std_proj(w_sb, x_fm_sl, dst_slices, oc_list, ts_list, func, bias_base,
                     wait_sems=(), scale=1.0, slot_free=None):
            mm0, ac0 = S_["mm"].val, S_["ac"].val
            slots = [psA[:, 0:512], psA[:, 512:1024], psB[:, 0:512], psB[:, 512:1024]]
            combos = [(oi, oc, ts) for oi, oc in enumerate(oc_list) for ts in ts_list]

            def f_t(e):
                for sem, val in wait_sems:
                    e.wait_ge(sem.h, val)
                for idx, (oi, oc, ts) in enumerate(combos):
                    if idx >= 4:
                        e.wait_ge(S_["ac"].h, ac0 + idx - 3)
                    x = x_fm_sl(ts)
                    for c in range(DCH):
                        ins = e.matmul(slots[idx % 4], w_sb[:, c, oc * 128:(oc + 1) * 128],
                                       x[:, c, :], start=(c == 0), stop=(c == DCH - 1))
                    S_["mm"].inc(ins)

            def f_s(e):
                for idx, (oi, oc, ts) in enumerate(combos):
                    e.wait_ge(S_["mm"].h, mm0 + idx + 1)
                    if slot_free is not None and idx >= 2:
                        semv = slot_free(idx)
                        e.wait_ge(semv[0].h, semv[1])
                    bias = bias1[:, bias_base + oi, 0:1] if bias_base is not None else 0.0
                    S_["ac"].inc(e.activation(out=dst_slices(oi, ts), in_=slots[idx % 4],
                                              func=func, bias=bias, scale=scale))
            blk.tensor(f_t)
            blk.scalar(f_s)

        def swap_proj_residual(src_fm, w_sb, nch, bcast_idx, act_func, extra_mul_col=None,
                               wait_sems=(), gate_fm=None):
            mm0, dv0, ac0 = S_["mm"].val, S_["dv"].val, S_["ac"].val

            def f_t(e):
                for sem, val in wait_sems:
                    e.wait_ge(sem.h, val)
                for tt in range(NT):
                    if tt >= 2:
                        e.wait_ge(S_["dv"].h, dv0 + 2 * tt - 3)
                    ps = (psC, psB)[tt % 2]
                    for c in range(nch):
                        e.matmul(ps[:, 0:512], src_fm[:, c, tt * 128:(tt + 1) * 128],
                                 w_sb[:, c, 0:512], start=(c == 0), stop=(c == nch - 1))
                        ins = e.matmul(ps[:, 512:1024], src_fm[:, c, tt * 128:(tt + 1) * 128],
                                       w_sb[:, c, 512:1024], start=(c == 0), stop=(c == nch - 1))
                    S_["mm"].inc(ins)

            def f_v(e):
                for tt in range(NT):
                    ps = (psC, psB)[tt % 2]
                    e.wait_ge(S_["mm"].h, mm0 + tt + 1)
                    if bcast_idx is not None:
                        S_["dv"].inc(e.tensor_add(scr2[:], ps[:], bcast[:, bcast_idx, :]))
                    else:
                        S_["dv"].inc(e.tensor_copy(out=scr2[:], in_=ps[:]))
                    e.wait_ge(S_["ac"].h, ac0 + tt + 1)
                    if extra_mul_col is not None:
                        S_["dv"].inc(e.scalar_tensor_tensor(x_res[:, tt, :], scr[:], extra_mul_col(tt),
                                                            x_res[:, tt, :], ALU.mult, ALU.add))
                    else:
                        S_["dv"].inc(e.tensor_add(x_res[:, tt, :], x_res[:, tt, :], scr[:]))

            def f_s(e):
                for tt in range(NT):
                    e.wait_ge(S_["dv"].h, dv0 + 2 * tt + 1)
                    S_["ac"].inc(e.activation(out=scr[:], in_=scr2[:], func=act_func))
            blk.tensor(f_t)
            blk.vector(f_v)
            blk.scalar(f_s)

        def f_p0_sync(e):
            S_["ldw"].inc(e.dma_start(out=identt[:], in_=ident_np.ap()), 16)
            S_["ldw"].inc(e.dma_start(out=onesr[:], in_=ones_np.ap()), 16)
            S_["ldw"].inc(e.dma_start(out=onesf[:], in_=onesf_np.ap()), 16)
            for ap_, base in ((l0_bq, BQ), (l0_bk, BK), (l0_bv, BV)):
                S_["ldw"].inc(e.dma_start(out=bias1[:, base:base + 8, 0:1],
                                          in_=ap_.rearrange("(c p) -> p c", p=128).unsqueeze(2), allow_slow_non_contiguous=True), 16)
            S_["ldw"].inc(e.dma_start(out=scr2[0:1, 0:1024], in_=l0_bo[None, :]), 16)
            S_["ldw"].inc(e.dma_start(out=scr2[32:33, 0:1024], in_=l1_bout[None, :]), 16)
            S_["ldw"].inc(e.dma_start(out=scr2[64:65, 0:1024], in_=ff_bsum[0:1, :]), 16)
            S_["ldw"].inc(e.dma_start(out=scr2[96:97, 0:1024], in_=ff_bsum[1:2, :]), 16)
        blk.sync(f_p0_sync)
        LDW_P0 = S_["ldw"].val

        def f_p0_g(e):
            S_["gx"].inc(e.dma_start(out=x_res[:], in_=x_own.rearrange("(t p) d -> p t d", p=128)), 16)
        blk.gpsimd(f_p0_g)
        LDX_RES = S_["gx"].val

        def f_p0_v(e):
            e.wait_ge(S_["ldw"].h, LDW_P0)
            e.memset(epscol[:, 0:1], 1e-5)
            S_["dv"].inc(e.memset(scr[:], 0.0))
        blk.vector(f_p0_v)
        DV_P0 = S_["dv"].val

        def f_p0_g2(e):
            e.wait_ge(S_["dv"].h, DV_P0)
            zb = scr[:].bitcast(BF16)
            total = 4 * (CHK + CHV)
            flat = l1pad.rearrange("a b -> (a b)")
            per = 128 * 2048
            off = 0
            while off < total:
                n = min(per, total - off)
                S_["gz"].inc(e.dma_start(out=flat[off:off + n].rearrange("(p x) -> p x", p=128),
                                         in_=zb[:, 0:n // 128]), 16)
                off += n
        blk.gpsimd(f_p0_g2)
        KV_PAD = S_["gz"].val
        TP_P00 = S_["tp"].val
        DV_P00 = S_["dv"].val

        def f_p0_v2(e):
            e.wait_ge(S_["gz"].h, KV_PAD)
            for i in range(4):
                e.wait_ge(S_["tp"].h, TP_P00 + i)
                S_["dv"].inc(e.tensor_copy(out=scr[0:1, 0:512].bitcast(BF16),
                                           in_=scr2[32 * i:32 * i + 1, 0:1024]))
        blk.vector(f_p0_v2)
        EV_BC0 = S_["ev"].val

        def f_p0_t(e):
            for i in range(4):
                e.wait_ge(S_["dv"].h, DV_P00 + i + 1)
                if i > 0:
                    e.wait_ge(S_["ev"].h, EV_BC0 + i)
                for half in range(2):
                    ins = e.matmul(psA[:, half * 512:(half + 1) * 512], onesr[0:1, :],
                                   scr[0:1, 0:512].bitcast(BF16)[:, half * 512:(half + 1) * 512],
                                   start=True, stop=True)
                S_["tp"].inc(ins)
        blk.tensor(f_p0_t)

        def f_p0_s(e):
            for i in range(4):
                e.wait_ge(S_["tp"].h, TP_P00 + i + 1)
                if i < 3:
                    S_["ev"].inc(e.copy(out=bcast[:, i, :], in_=psA[:]))
                else:
                    S_["ev"].inc(e.copy(out=scr2[:], in_=psA[:]))
        blk.scalar(f_p0_s)
        EV_BC4 = S_["ev"].val

        def f_p0_v3(e):
            e.wait_ge(S_["ev"].h, EV_BC4)
            S_["dv"].inc(e.tensor_add(bcast[:, 2, :], bcast[:, 2, :], scr2[:]))
        blk.vector(f_p0_v3)
        EV_BC = S_["ev"].val

        if "l0qkv" in phases or "l0attn" in phases:
            _esL0 = _es.enter_context(ExitStack())
            qfm = _esL0.enter_context(nc.sbuf_tensor("qfm", [128, DCH, TOK], BF16))
            with ExitStack() as _es2:
                xsrcL = _es2.enter_context(nc.sbuf_tensor("xsrcL", [128, 2, 1024], F32))
                wk_sb = _es2.enter_context(nc.sbuf_tensor("wk_sb", [128, DCH, D], BF16))
                wv_sb = _es2.enter_context(nc.sbuf_tensor("wv_sb", [128, DCH, D], BF16))
                kvo = _es2.enter_context(nc.sbuf_tensor("kvo", [128, 2, 512], BF16))
                vfm = _es2.enter_context(nc.sbuf_tensor("vfm", [128, DCH, 512], BF16))
                vto = _es2.enter_context(nc.sbuf_tensor("vto", [128, 1, VCOL * H], BF16))
                wq_sb = wk_sb
                def f_wl(e):
                    S_["gw"].inc(e.dma_start(out=wk_sb[:], in_=l0_wq.rearrange("(c p) o -> p c o", p=128)), 16)
                    S_["gw"].inc(e.dma_start(out=wv_sb[:], in_=l0_wv.rearrange("(c p) o -> p c o", p=128)), 16)
                blk.gpsimd(f_wl)
                LDW_QKV = S_["gw"].val

                def f_wait_res(e):
                    e.wait_ge(S_["gx"].h, LDX_RES)
                blk.vector(f_wait_res)
                ln_sweep([("sbuf", x_res[:, t, :]) for t in range(NT)], xfm, 0)
                EV_LN1 = S_["ev"].val
                if "dbg_xfm" in dbg:
                    dx = dram_dbg("dbg_xfm", [128, DCH, TOK], U16)
                    blk.gpsimd(lambda e, v=EV_LN1: (e.wait_ge(S_["ev"].h, v),
                                S_["gst"].inc(e.dma_start(out=dx.bitcast(BF16), in_=xfm[:]), 16)))

                std_proj(wq_sb, lambda ts: xfm[:, :, ts * 512:(ts + 1) * 512],
                         lambda oi, ts: qfm[:, oi, ts * 512:(ts + 1) * 512],
                         list(range(DCH)), list(range(4)), AF.Tanh, BQ,
                         wait_sems=[(S_["ev"], EV_LN1), (S_["gw"], LDW_QKV)])
                MM_Q = S_["mm"].val

                def f_wl2(e):
                    e.wait_ge(S_["mm"].h, MM_Q)
                    S_["gw"].inc(e.dma_start(out=wk_sb[:], in_=l0_wk.rearrange("(c p) o -> p c o", p=128)), 16)
                blk.gpsimd(f_wl2)
                LDW_K = S_["gw"].val

                if "dbg_q" in dbg:
                    dq = dram_dbg("dbg_q", [128, DCH, TOK])
                    ac0 = S_["ac"].val
                    blk.gpsimd(lambda e: S_["gst"].inc(
                        (e.wait_ge(S_["ac"].h, ac0), e.dma_start(out=dq, in_=qfm[:]))[1], 16))

                for g in range(NG):
                    fw = (S_["mm"], MM_Q) if g == 0 else None
                    ln_sweep([("hbm", x_seq[g * 512 + t * 128:g * 512 + (t + 1) * 128, :])
                              for t in range(4)], xfm, 0, first_evac_wait=fw, xsrc=xsrcL)
                    EV_LNG = S_["ev"].val
                    KV_K0 = S_["kv"].val
                    AC_K0 = S_["ac"].val
                    std_proj(wk_sb, lambda ts: xfm[:, :, 0:512],
                             lambda oi, ts: kvo[:, oi % 2, :], list(range(DCH)), [0], AF.Tanh, BK,
                             wait_sems=[(S_["ev"], EV_LNG), (S_["gw"], LDW_K)],
                             slot_free=lambda idx: (S_["kv"], KV_K0 + 16 * (idx - 1)))
                    def f_spk(e, g=g, AC_K0=AC_K0):
                        for oc in range(DCH):
                            e.wait_ge(S_["ac"].h, AC_K0 + oc + 1)
                            S_["kv"].inc(e.dma_start(out=l0k[g, oc], in_=kvo[:, oc % 2, :]), 16)
                    blk.sync(f_spk)
                    std_proj(wv_sb, lambda ts: xfm[:, :, 0:512],
                             lambda oi, ts: vfm[:, oi, :], list(range(DCH)), [0], AF.Tanh, BV,
                             wait_sems=[(S_["kv"], S_["kv"].val)])
                    AC_V = S_["ac"].val
                    tp0, ev0, dv0 = S_["tp"].val, S_["ev"].val, S_["dv"].val

                    def f_tv(e, AC_V=AC_V, tp0=tp0, ev0=ev0):
                        e.wait_ge(S_["ac"].h, AC_V)
                        for tt in range(4):
                            e.wait_ge(S_["ev"].h, ev0 + tt)
                            for c in range(DCH):
                                ins = e.transpose(psT[:, c * 128:(c + 1) * 128],
                                                  vfm[:, c, tt * 128:(tt + 1) * 128], identt[:])
                            S_["tp"].inc(ins)
                    blk.tensor(f_tv)

                    if g == 0:
                        def f_ones(e):
                            S_["dv"].inc(e.memset(
                                vto[:, 0, :].rearrange("p (h x) -> p h x", x=VCOL)[:, :, 64:65], 1.0))
                        blk.vector(f_ones)
                    DV_ONES = S_["dv"].val
                    kv0v = S_["kv"].val
                    EV_V0 = S_["ev"].val

                    def f_sv(e, tp0=tp0, kv0v=kv0v):
                        for tt in range(4):
                            e.wait_ge(S_["tp"].h, tp0 + tt + 1)
                            e.wait_ge(S_["kv"].h, kv0v + 16 * tt)
                            dst = vto[:, 0, :].rearrange("p (h x) -> p h x", x=VCOL)[:, :, 0:64]
                            S_["ev"].inc(e.copy(out=dst, in_=psT[:].rearrange("p (h x) -> p h x", x=64)))
                    blk.scalar(f_sv)

                    def f_spv(e, g=g, EV_V0=EV_V0, DV_ONES=DV_ONES):
                        e.wait_ge(S_["dv"].h, DV_ONES)
                        for tt in range(4):
                            e.wait_ge(S_["ev"].h, EV_V0 + tt + 1)
                            S_["kv"].inc(e.dma_start(out=l0v[g, tt], in_=vto[:, 0, :]), 16)
                    blk.sync(f_spv)

                if "dbg_kv" in dbg:
                    dk = dram_dbg("dbg_k", [NG, DCH, 128, 512], U16)
                    dvv = dram_dbg("dbg_v", [NG, 4, 128, VCOL * H], U16)
                    kv0 = S_["kv"].val
                    blk.gpsimd(lambda e: (e.wait_ge(S_["kv"].h, kv0),
                                          S_["gst"].inc(e.dma_start(out=dk.bitcast(BF16), in_=l0k[:]), 16),
                                          S_["gst"].inc(e.dma_start(out=dvv.bitcast(BF16), in_=l0v[:]), 16)))

        if "l0attn" in phases:
            with ExitStack() as _es2:
                wo_sb = _es2.enter_context(nc.sbuf_tensor("wo_sb", [128, DCH, D], BF16))
                winm = _es2.enter_context(nc.sbuf_tensor("winm", [128, 8, 512], BF16))
                kst = _es2.enter_context(nc.sbuf_tensor("kst", [128, 2, 128], BF16))
                vst = _es2.enter_context(nc.sbuf_tensor("vst", [128, 2, 130], BF16))
                pT = _es2.enter_context(nc.sbuf_tensor("pT", [128, 2, 1024], BF16))
                ctxfm = xfm
                KV_ALL = S_["kv"].val
                MM_QKVDONE = S_["mm"].val
                def f_wm(e):
                    e.wait_ge(S_["kv"].h, KV_ALL)
                    e.wait_ge(S_["mm"].h, MM_QKVDONE)
                    S_["ldw"].inc(e.dma_start(out=winm[:], in_=l0_winmask.rearrange("w p t -> p w t").bitcast(BF16)), 16)
                blk.sync(f_wm)
                LDW_WM = S_["ldw"].val

                for hs in range(8):
                    for qc in range(QC):
                        nj = 8 * qc + 8
                        ldk0, mm0, ac0, dv0 = (S_[k].val for k in ("ldk", "mm", "ac", "dv"))
                        QK = lambda j: mm0 + 1 if j == 0 else mm0 + 2 * j
                        AV = lambda j, nj=nj: mm0 + 2 * j + (3 if j < nj - 1 else 2)

                        def f_dma(e, qc=qc, nj=nj, ldk0=ldk0, mm0=mm0, hs=hs, AV=AV):
                            for j in range(nj):
                                if j >= 2:
                                    e.wait_ge(S_["mm"].h, AV(j - 2))
                                else:
                                    e.wait_ge(S_["mm"].h, mm0)
                                if hs == 0 and qc == 0 and j == 0:
                                    e.wait_ge(S_["kv"].h, KV_ALL)
                                g, tt = j // 4, j % 4
                                S_["ldk"].inc(e.dma_start(out=kst[:, j % 2, :],
                                                          in_=l0k[g, hs, :, tt * 128:(tt + 1) * 128]), 16)
                                S_["ldk"].inc(e.dma_start(out=vst[:, j % 2, :],
                                                          in_=l0v[g, tt, :, 130 * hs:130 * hs + 130]), 16)
                        blk.sync(f_dma)

                        def f_pe(e, qc=qc, nj=nj, ldk0=ldk0, mm0=mm0, ac0=ac0, dv0=dv0, hs=hs):
                            if hs == 0 and qc == 0:
                                e.wait_ge(S_["ldw"].h, LDW_WM)

                            def emit_qk(j):
                                strip = (psA, psB)[j % 2]
                                e.wait_ge(S_["ldk"].h, ldk0 + 32 * j + 16)
                                if j >= 2:
                                    e.wait_ge(S_["ac"].h, ac0 + j - 1)
                                elif not (hs == 0 and qc == 0):
                                    e.wait_ge(S_["ac"].h, ac0 - 2)
                                for hp in range(2):
                                    ins = e.matmul(strip[:, hp * 512:(hp + 1) * 512],
                                                   kst[hp * 64:(hp + 1) * 64, j % 2, :],
                                                   qfm[hp * 64:(hp + 1) * 64, hs, qc * 512:(qc + 1) * 512],
                                                   start=True, stop=True)
                                S_["mm"].inc(ins)

                            def emit_av(j):
                                e.wait_ge(S_["ldk"].h, ldk0 + 32 * j + 32)
                                e.wait_ge(S_["ac"].h, ac0 + j + 1)
                                if j == 0:
                                    e.wait_ge(S_["dv"].h, dv0)
                                if j >= 8 * qc:
                                    e.wait_ge(S_["dv"].h, dv0 + 2 * (j - 8 * qc) + 2)
                                for hp in range(2):
                                    avp = psC[0:65, hp * 512:(hp + 1) * 512]
                                    ins = e.matmul(avp, vst[:, j % 2, hp * 65:(hp + 1) * 65],
                                                   pT[:, j % 2, hp * 512:(hp + 1) * 512],
                                                   start=(j == 0), stop=(j == nj - 1))
                                S_["mm"].inc(ins)

                            emit_qk(0)
                            for j in range(1, nj):
                                emit_qk(j)
                                emit_av(j - 1)
                            emit_av(nj - 1)
                        blk.tensor(f_pe)

                        def f_exp(e, qc=qc, nj=nj, mm0=mm0, dv0=dv0, ac0=ac0):
                            for j in range(nj):
                                strip = (psA, psB)[j % 2]
                                e.wait_ge(S_["mm"].h, QK(j))
                                S_["ac"].inc(e.activation(out=pT[:, j % 2, :], in_=strip[:],
                                                          func=AF.Exp, scale=0.125))
                        blk.scalar(f_exp)

                        def f_mask(e, qc=qc, nj=nj, ac0=ac0, dv0=dv0):
                            for j in range(8 * qc, nj):
                                w = j - 8 * qc
                                e.wait_ge(S_["ac"].h, ac0 + j + 1)
                                for hp in range(2):
                                    S_["dv"].inc(e.tensor_mul(pT[:, j % 2, hp * 512:(hp + 1) * 512],
                                                              pT[:, j % 2, hp * 512:(hp + 1) * 512],
                                                              winm[:, w, :]))
                        blk.vector(f_mask)

                        MM_AVE = S_["mm"].val
                        dv1 = S_["dv"].val

                        def f_nv(e, qc=qc, hs=hs, MM_AVE=MM_AVE, dv1=dv1):
                            e.wait_ge(S_["mm"].h, MM_AVE)
                            S_["dv"].inc(e.reciprocal(rrow[0:1, 0:512], psC[64:65, 0:512]))
                            S_["dv"].inc(e.reciprocal(rrow[0:1, 512:1024], psC[64:65, 512:1024]))
                        blk.vector(f_nv)
                        DV_R = S_["dv"].val
                        mm1 = S_["mm"].val

                        def f_nt(e, DV_R=DV_R):
                            e.wait_ge(S_["dv"].h, DV_R)
                            rbdst = (psD[0:64, :], psD[64:128, :])
                            for hp in range(2):
                                S_["mm"].inc(e.matmul(rbdst[hp], onesf[0:1, :],
                                                      rrow[0:1, 512 * hp:512 * hp + 512], start=True, stop=True))
                        blk.tensor(f_nt)
                        AC_RB = S_["ac"].val

                        def f_nrb(e, mm1=mm1):
                            rbdst = (psD[0:64, :], psD[64:128, :])
                            for hp in range(2):
                                e.wait_ge(S_["mm"].h, mm1 + hp + 1)
                                S_["ac"].inc(e.copy(out=scr2[0:64, 512 * hp:512 * hp + 512], in_=rbdst[hp]))
                        blk.scalar(f_nrb)

                        def f_nv2(e, qc=qc, hs=hs, AC_RB=AC_RB):
                            for hp in range(2):
                                e.wait_ge(S_["ac"].h, AC_RB + hp + 1)
                                S_["dv"].inc(e.tensor_mul(
                                    ctxfm[hp * 64:(hp + 1) * 64, hs, qc * 512:(qc + 1) * 512],
                                    psC[0:64, hp * 512:(hp + 1) * 512],
                                    scr2[0:64, 512 * hp:512 * hp + 512]))
                        blk.vector(f_nv2)
                if "dbg_ctx" in dbg:
                    dc = dram_dbg("dbg_ctx", [128, DCH, TOK], U16)
                    dv0 = S_["dv"].val
                    blk.gpsimd(lambda e: (e.wait_ge(S_["dv"].h, dv0),
                                          S_["gst"].inc(e.dma_start(out=dc.bitcast(BF16), in_=ctxfm[:]), 16)))

                if "dbg_winm" in dbg:
                    dwm = dram_dbg("dbg_winm", [128, 8, 512], U16)
                    _ldwv = S_["ldw"].val
                    blk.gpsimd(lambda e: (e.wait_ge(S_["ldw"].h, _ldwv),
                                          S_["gst"].inc(e.dma_start(out=dwm.bitcast(BF16), in_=winm[:]), 16)))
                DV_CTX = S_["dv"].val
                MM_WO = S_["mm"].val
                def f_wo(e):
                    e.wait_ge(S_["mm"].h, MM_WO)
                    S_["gw"].inc(e.dma_start(out=wo_sb[:], in_=l0_wo.rearrange("(c p) o -> p c o", p=128)), 16)
                blk.gpsimd(f_wo)
                swap_proj_residual(ctxfm, wo_sb, DCH, 0, AF.Tanh,
                                   wait_sems=[(S_["dv"], DV_CTX), (S_["gw"], S_["gw"].val),
                                              (S_["ev"], EV_BC)])

        if "l0qkv" in phases or "l0attn" in phases:
            _esL0.close()

        if "moe" in phases:
            with ExitStack() as _es2:
                wgh = _es2.enter_context(nc.sbuf_tensor("wgh", [128, 2, DCH, 128], BF16))
                wuh = _es2.enter_context(nc.sbuf_tensor("wuh", [128, 2, DCH, 128], BF16))
                wde = _es2.enter_context(nc.sbuf_tensor("wde", [128, 16, D], BF16))
                hidb = _es2.enter_context(nc.sbuf_tensor("hidb", [128, 16, 512], BF16))
                sc8 = _es2.enter_context(nc.sbuf_tensor("sc8", [128, NT, 8], F32))
                scrE = _es2.enter_context(nc.sbuf_tensor("scrE", [128, NT, 8], F32))
                scrF = _es2.enter_context(nc.sbuf_tensor("scrF", [128, NT, 8], F32))
                stats2 = _es2.enter_context(nc.sbuf_tensor("stats2", [128, NT, 8], F32))
                wpe = _es2.enter_context(nc.sbuf_tensor("wpe", [128, NT, 9, 8], F32))
                gwsb = _es2.enter_context(nc.sbuf_tensor("gwsb", [128, DCH, 8], BF16))
                gscb = scr2[:].bitcast(BF16)
                gsil = scr[:].bitcast(BF16).rearrange("p (a t) -> p a t", a=4)

                MM_PRE = S_["mm"].val
                ln_sweep([("sbuf", x_res[:, t, :]) for t in range(NT)], xfm, 0,
                         first_evac_wait=(S_["mm"], MM_PRE))
                EV_LN2 = S_["ev"].val

                def f_gwl(e):
                    e.wait_ge(S_["mm"].h, MM_PRE)
                    S_["gw"].inc(e.dma_start(out=gwsb[:], in_=moe_gw.rearrange("(c p) o -> p c o", p=128)), 16)
                blk.gpsimd(f_gwl)
                GW_GATE = S_["gw"].val

                DV_Z0 = S_["dv"].val
                blk.vector(lambda e: S_["dv"].inc(e.memset(gscb[:], 0.0)))
                DV_Z = S_["dv"].val
                mmg0 = S_["mm"].val

                def f_gt(e):
                    e.wait_ge(S_["ev"].h, EV_LN2)
                    e.wait_ge(S_["gw"].h, GW_GATE)
                    for ts in range(4):
                        if ts >= 1:
                            e.wait_ge(S_["ac"].h, acg0 + ts)
                        for c in range(DCH):
                            ins = e.matmul(psD[0:8, :], gwsb[:, c, :], xfm[:, c, ts * 512:(ts + 1) * 512],
                                           start=(c == 0), stop=(c == DCH - 1))
                        S_["mm"].inc(ins)
                acg0 = S_["ac"].val
                blk.tensor(f_gt)

                def f_gs(e):
                    for ts in range(4):
                        e.wait_ge(S_["mm"].h, mmg0 + ts + 1)
                        e.wait_ge(S_["dv"].h, DV_Z)
                        S_["ac"].inc(e.copy(out=gscb[0:8, ts * 512:(ts + 1) * 512], in_=psD[0:8, :]))
                blk.scalar(f_gs)
                ACG_E = S_["ac"].val

                tpg0, evg0 = S_["tp"].val, S_["ev"].val

                def f_gt2(e):
                    e.wait_ge(S_["ac"].h, ACG_E)
                    for rnd in range(2):
                        if rnd == 1:
                            e.wait_ge(S_["ev"].h, evg0 + 1)
                        for c in range(8):
                            ins = e.transpose(psT[:, c * 128:(c + 1) * 128],
                                              gscb[:, rnd * 1024 + c * 128: rnd * 1024 + (c + 1) * 128],
                                              identt[:])
                        S_["tp"].inc(ins)
                blk.tensor(f_gt2)

                def f_gs2(e):
                    for rnd in range(2):
                        e.wait_ge(S_["tp"].h, tpg0 + rnd + 1)
                        S_["ev"].inc(e.copy(out=sc8[:, rnd * 8:(rnd + 1) * 8, :],
                                            in_=psT[:].rearrange("p (c x) -> p c x", c=8)[:, :, 0:8]))
                blk.scalar(f_gs2)
                EV_SC8 = S_["ev"].val

                dvt0, act0 = S_["dv"].val, S_["ac"].val

                def f_top_v(e):
                    e.wait_ge(S_["ev"].h, EV_SC8)
                    SS = lambda tt, k: stats[:, tt, k:k + 1]
                    for tt in range(NT):
                        e.reduce_max(SS(tt, 1), sc8[:, tt, :], axis=AX.X)
                    for tt in range(NT):
                        e.tensor_single_scalar(scrE[:, tt, :], sc8[:, tt, :], SS(tt, 1), ALU.is_equal)
                    for tt in range(NT):
                        e.scalar_tensor_tensor(scrF[:, tt, :], scrE[:, tt, :], -1e9, sc8[:, tt, :],
                                               ALU.mult, ALU.add)
                    for tt in range(NT):
                        e.reduce_max(SS(tt, 2), scrF[:, tt, :], axis=AX.X)
                    for tt in range(NT):
                        S_["dv"].inc(e.tensor_sub(SS(tt, 0), SS(tt, 2), SS(tt, 1)))
                    e.wait_ge(S_["ac"].h, act0 + NT)
                    for tt in range(NT):
                        e.tensor_scalar_add(SS(tt, 3), stats2[:, tt, 0:1], 1.0)
                    for tt in range(NT):
                        e.reciprocal(SS(tt, 4), SS(tt, 3))
                    for tt in range(NT):
                        e.tensor_mul(SS(tt, 5), stats2[:, tt, 0:1], SS(tt, 4))
                    for tt in range(NT):
                        e.tensor_sub(SS(tt, 6), SS(tt, 4), SS(tt, 5))
                    for tt in range(NT):
                        e.tensor_single_scalar(scrF[:, tt, :], sc8[:, tt, :], SS(tt, 2), ALU.is_ge)
                    for tt in range(NT):
                        e.tensor_single_scalar(scrF[:, tt, :], scrF[:, tt, :], SS(tt, 5), ALU.mult)
                    for tt in range(NT):
                        ins = e.scalar_tensor_tensor(
                            wpe[:, tt, 0:8, 0:1].squeeze(), scrE[:, tt, :], SS(tt, 6), scrF[:, tt, :],
                            ALU.mult, ALU.add)
                    S_["dv"].inc(ins, 1)
                blk.vector(f_top_v)

                def f_top_s(e):
                    e.wait_ge(S_["dv"].h, dvt0 + NT)
                    for tt in range(NT):
                        S_["ac"].inc(e.activation(out=stats2[:, tt, 0:1], in_=stats[:, tt, 0:1], func=AF.Exp))
                blk.scalar(f_top_s)
                DV_TOP = S_["dv"].val

                def f_ones2(e):
                    e.wait_ge(S_["dv"].h, DV_TOP)
                    S_["dv"].inc(e.memset(wpe[:, :, 8, 0:1], 1.0))
                blk.vector(f_ones2)
                DV_TOP2 = S_["dv"].val

                MM_LD = [S_["mm"].val]
                for e_i in range(MOE_E + 1):
                    shared = e_i == MOE_E
                    wsrc_g = moe_wsg if shared else moe_wg[e_i]
                    wsrc_u = moe_wsu if shared else moe_wu[e_i]
                    wsrc_d = moe_wsd if shared else moe_wd[e_i]
                    mm0 = S_["mm"].val
                    ac0 = S_["ac"].val
                    dv0 = S_["dv"].val
                    gw0 = S_["gw"].val
                    gslot = [psA[:, 0:512], psA[:, 512:1024]]
                    uslot = [psB[:, 0:512], psB[:, 512:1024]]
                    mm_g = lambda ts, hc, mm0=mm0: mm0 + 36 * ts + 2 * hc + 1
                    mm_u = lambda ts, hc, mm0=mm0: mm0 + 36 * ts + 2 * hc + 2
                    mm_d = lambda ts, tb, mm0=mm0: mm0 + 36 * ts + 33 + tb
                    ac_s = lambda ts, hc, ac0=ac0: ac0 + 16 * ts + hc + 1
                    dv_m = lambda ts, hc, dv0=dv0: dv0 + 20 * ts + hc + 1
                    dv_r = lambda ts, tb, dv0=dv0: dv0 + 20 * ts + 17 + tb
                    def gw_g(ts, hc, gw0=gw0):
                        return gw0 + 512 * ts + 32 * hc + 16 + (16 if ts >= 1 else 0)
                    def gw_u(ts, hc, gw0=gw0):
                        return gw_g(ts, hc) + 16
                    GW_WDE = gw0 + 32 * 16 + 16

                    def f_wl(e, wsrc_g=wsrc_g, wsrc_u=wsrc_u, wsrc_d=wsrc_d,
                             mm_g=mm_g, mm_u=mm_u, e_i=e_i, MM_LDP=MM_LD[0]):
                        if e_i == 0:
                            e.wait_ge(S_["mm"].h, MM_PRE)
                        for ts in range(4):
                            for hc in range(16):
                                if ts > 0 or hc >= 2:
                                    pts, phc = (ts, hc - 2) if hc >= 2 else (ts - 1, 14 + hc)
                                    e.wait_ge(S_["mm"].h, mm_u(pts, phc))
                                S_["gw"].inc(e.dma_start(
                                    out=wgh[:, hc % 2, :, :],
                                    in_=wsrc_g[:, hc * 128:(hc + 1) * 128].rearrange("(c p) o -> p c o", p=128)), 16)
                                S_["gw"].inc(e.dma_start(
                                    out=wuh[:, hc % 2, :, :],
                                    in_=wsrc_u[:, hc * 128:(hc + 1) * 128].rearrange("(c p) o -> p c o", p=128)), 16)
                            if ts == 0:
                                e.wait_ge(S_["mm"].h, MM_LDP)
                                S_["gw"].inc(e.dma_start(
                                    out=wde[:], in_=wsrc_d.rearrange("(hc p) o -> p hc o", p=128)), 16)
                    blk.gpsimd(f_wl)

                    def f_et(e, mm0=mm0, ac0=ac0, dv0=dv0, e_i=e_i,
                             mm_g=mm_g, mm_u=mm_u, mm_d=mm_d, ac_s=ac_s, dv_m=dv_m, dv_r=dv_r,
                             gw_g=gw_g, gw_u=gw_u, GW_WDE=GW_WDE):
                        e.wait_ge(S_["dv"].h, dv0 if e_i > 0 else DV_TOP2)
                        for ts in range(4):
                            for hc in range(16):
                                e.wait_ge(S_["gw"].h, gw_g(ts, hc))
                                if ts > 0 or hc >= 2:
                                    pts, phc = (ts, hc - 2) if hc >= 2 else (ts - 1, 14 + hc)
                                    e.wait_ge(S_["ac"].h, ac_s(pts, phc))
                                for c in range(DCH):
                                    ins = e.matmul(gslot[hc % 2], wgh[:, hc % 2, c, :],
                                                   xfm[:, c, ts * 512:(ts + 1) * 512],
                                                   start=(c == 0), stop=(c == DCH - 1))
                                S_["mm"].inc(ins)
                                e.wait_ge(S_["gw"].h, gw_u(ts, hc))
                                if ts > 0 or hc >= 2:
                                    pts, phc = (ts, hc - 2) if hc >= 2 else (ts - 1, 14 + hc)
                                    e.wait_ge(S_["dv"].h, dv_m(pts, phc))
                                for c in range(DCH):
                                    ins = e.matmul(uslot[hc % 2], wuh[:, hc % 2, c, :],
                                                   xfm[:, c, ts * 512:(ts + 1) * 512],
                                                   start=(c == 0), stop=(c == DCH - 1))
                                S_["mm"].inc(ins)
                            if ts == 0:
                                e.wait_ge(S_["gw"].h, GW_WDE)
                            e.wait_ge(S_["dv"].h, dv_m(ts, 15))
                            for tb in range(4):
                                if ts > 0 or tb > 0:
                                    pts, ptb = (ts, tb - 1) if tb > 0 else (ts - 1, 3)
                                    e.wait_ge(S_["dv"].h, dv_r(pts, ptb))
                                for hc in range(16):
                                    e.matmul(psC[:, 0:512], hidb[:, hc, tb * 128:(tb + 1) * 128],
                                             wde[:, hc, 0:512], start=(hc == 0), stop=(hc == 15))
                                    ins = e.matmul(psC[:, 512:1024], hidb[:, hc, tb * 128:(tb + 1) * 128],
                                                   wde[:, hc, 512:1024], start=(hc == 0), stop=(hc == 15))
                                S_["mm"].inc(ins)
                    blk.tensor(f_et)

                    def f_es(e, mm_g=mm_g, dv_m=dv_m):
                        for ts in range(4):
                            for hc in range(16):
                                e.wait_ge(S_["mm"].h, mm_g(ts, hc))
                                if ts > 0 or hc >= 2:
                                    pts, phc = (ts, hc - 2) if hc >= 2 else (ts - 1, 14 + hc)
                                    e.wait_ge(S_["dv"].h, dv_m(pts, phc))
                                S_["ac"].inc(e.activation(out=gsil[:, hc % 2, :], in_=gslot[hc % 2], func=AF.Silu))
                    blk.scalar(f_es)

                    def f_ev(e, mm_u=mm_u, mm_d=mm_d, ac_s=ac_s, e_i=e_i, shared=shared):
                        for ts in range(4):
                            for hc in range(16):
                                e.wait_ge(S_["mm"].h, mm_u(ts, hc))
                                e.wait_ge(S_["ac"].h, ac_s(ts, hc))
                                S_["dv"].inc(e.tensor_mul(hidb[:, hc, :], uslot[hc % 2], gsil[:, hc % 2, :]))
                            for tb in range(4):
                                tile = ts * 4 + tb
                                e.wait_ge(S_["mm"].h, mm_d(ts, tb))
                                wcol = wpe[:, tile, 8 if shared else e_i, 0:1]
                                S_["dv"].inc(e.scalar_tensor_tensor(
                                    x_res[:, tile, :], psC[:], wcol, x_res[:, tile, :],
                                    ALU.mult, ALU.add))
                    blk.vector(f_ev)
                    MM_LD[0] = S_["mm"].val

                if "dbg_wpe" in dbg:
                    dw = dram_dbg("dbg_wpe", [128, NT, 9, 8])
                    dv0 = S_["dv"].val
                    blk.gpsimd(lambda e: (e.wait_ge(S_["dv"].h, dv0),
                                          S_["gst"].inc(e.dma_start(out=dw, in_=wpe[:]), 16)))

        if "l1qkv" in phases:
            with ExitStack() as _es2:
                qfm1g = _es.enter_context(ExitStack())
                qfm1 = qfm1g.enter_context(nc.sbuf_tensor("qfm1", [128, DCH, TOK], BF16))
                w3 = _es2.enter_context(nc.sbuf_tensor("w3", [128, DCH, D], BF16))
                kvo1 = _es2.enter_context(nc.sbuf_tensor("kvo1", [128, 2, 512], BF16))
                vfm1 = _es2.enter_context(nc.sbuf_tensor("vfm1", [128, DCH, 512], BF16))
                vto1 = _es2.enter_context(nc.sbuf_tensor("vto1", [128, 1, VCOL * H], BF16))

                MM_PRE1 = S_["mm"].val
                DV_PRE1 = S_["dv"].val
                ln_sweep([("sbuf", x_res[:, t, :]) for t in range(NT)], xfm, 0,
                         first_evac_wait=(S_["mm"], MM_PRE1))
                EV_LN3 = S_["ev"].val

                def f_b1(e):
                    e.wait_ge(S_["mm"].h, MM_PRE1)
                    e.wait_ge(S_["dv"].h, DV_PRE1)
                    for third in range(3):
                        S_["ldw"].inc(e.dma_start(
                            out=bias1[:, BQK + 8 * third:BQK + 8 * third + 8, 0:1],
                            in_=l1_bqkv[third * D:(third + 1) * D].rearrange("(c p) -> p c", p=128).unsqueeze(2),
                            allow_slow_non_contiguous=True), 16)
                    S_["ldw"].inc(e.dma_start(
                        out=bias1[:, BG1:BG1 + 8, 0:1],
                        in_=l1_bgate.rearrange("(c p) -> p c", p=128).unsqueeze(2),
                        allow_slow_non_contiguous=True), 16)
                blk.sync(f_b1)
                LDW_B1 = S_["ldw"].val
                DV_QB = S_["dv"].val

                def f_qb(e):
                    e.wait_ge(S_["ldw"].h, LDW_B1)
                    S_["dv"].inc(e.tensor_scalar_mul(bias1[:, BQK:BQK + 8, 0:1], bias1[:, BQK:BQK + 8, 0:1], 0.125))
                blk.vector(f_qb)
                DV_QB2 = S_["dv"].val

                def f_w3q(e):
                    e.wait_ge(S_["mm"].h, MM_PRE1)
                    e.wait_ge(S_["dv"].h, DV_PRE1)
                    S_["gw"].inc(e.dma_start(out=w3[:], in_=l1_wqkvT.rearrange("(c p) o -> p c o", p=128)[:, :, 0:D]), 16)
                blk.gpsimd(f_w3q)
                GW_Q1 = S_["gw"].val

                def f_qwait(e):
                    e.wait_ge(S_["dv"].h, DV_QB2)
                blk.scalar(f_qwait)
                std_proj(w3, lambda ts: xfm[:, :, ts * 512:(ts + 1) * 512],
                         lambda oi, ts: qfm1[:, oi, ts * 512:(ts + 1) * 512],
                         list(range(DCH)), list(range(4)), AF.Identity, BQK, scale=0.125,
                         wait_sems=[(S_["ev"], EV_LN3), (S_["gw"], GW_Q1)])
                MM_Q1 = S_["mm"].val

                def f_w3k(e):
                    e.wait_ge(S_["mm"].h, MM_Q1)
                    S_["gw"].inc(e.dma_start(out=w3[:], in_=l1_wqkvT.rearrange("(c p) o -> p c o", p=128)[:, :, D:2 * D]), 16)
                blk.gpsimd(f_w3k)
                GW_K1 = S_["gw"].val

                MM_KDONE = S_["mm"].val
                for qc in range(QC):
                    KV_K0 = S_["kv"].val
                    AC_K0 = S_["ac"].val
                    std_proj(w3, lambda ts, qc=qc: xfm[:, :, qc * 512:(qc + 1) * 512],
                             lambda oi, ts: kvo1[:, oi % 2, :], list(range(DCH)), [0], AF.Identity, BQK + 8,
                             wait_sems=[(S_["gw"], GW_K1)],
                             slot_free=lambda idx, KV_K0=KV_K0: (S_["kv"], KV_K0 + 16 * (idx - 1)))

                    def f_spk1(e, qc=qc, AC_K0=AC_K0):
                        for oc in range(DCH):
                            e.wait_ge(S_["ac"].h, AC_K0 + oc + 1)
                            dst = cc_in[qc, 0:CHK].rearrange("(c p t) -> c p t", c=DCH, p=128)[oc]
                            S_["kv"].inc(e.dma_start(out=dst, in_=kvo1[:, oc % 2, :]), 16)
                    blk.sync(f_spk1)
                MM_KALL = S_["mm"].val

                def f_w3v(e):
                    e.wait_ge(S_["mm"].h, MM_KALL)
                    S_["gw"].inc(e.dma_start(out=w3[:], in_=l1_wqkvT.rearrange("(c p) o -> p c o", p=128)[:, :, 2 * D:3 * D]), 16)
                blk.gpsimd(f_w3v)
                GW_V1 = S_["gw"].val

                first_v = [True]
                for qc in range(QC):
                    std_proj(w3, lambda ts, qc=qc: xfm[:, :, qc * 512:(qc + 1) * 512],
                             lambda oi, ts: vfm1[:, oi, :], list(range(DCH)), [0], AF.Identity, BQK + 16,
                             wait_sems=[(S_["gw"], GW_V1), (S_["kv"], S_["kv"].val)])
                    AC_V1 = S_["ac"].val
                    tp0, ev0 = S_["tp"].val, S_["ev"].val

                    def f_tv1(e, AC_V1=AC_V1, tp0=tp0, ev0=ev0):
                        e.wait_ge(S_["ac"].h, AC_V1)
                        for tt in range(4):
                            e.wait_ge(S_["ev"].h, ev0 + tt)
                            for c in range(DCH):
                                ins = e.transpose(psT[:, c * 128:(c + 1) * 128],
                                                  vfm1[:, c, tt * 128:(tt + 1) * 128], identt[:])
                            S_["tp"].inc(ins)
                    blk.tensor(f_tv1)
                    if first_v[0]:
                        def f_ones1(e):
                            S_["dv"].inc(e.memset(
                                vto1[:, 0, :].rearrange("p (h x) -> p h x", x=VCOL)[:, :, 64:65], 1.0))
                        blk.vector(f_ones1)
                        first_v[0] = False
                    DV_ONES1 = S_["dv"].val
                    kv0v = S_["kv"].val
                    EV_V0 = S_["ev"].val

                    def f_sv1(e, tp0=tp0, kv0v=kv0v):
                        for tt in range(4):
                            e.wait_ge(S_["tp"].h, tp0 + tt + 1)
                            e.wait_ge(S_["kv"].h, kv0v + 16 * tt)
                            dst = vto1[:, 0, :].rearrange("p (h x) -> p h x", x=VCOL)[:, :, 0:64]
                            S_["ev"].inc(e.copy(out=dst, in_=psT[:].rearrange("p (h x) -> p h x", x=64)))
                    blk.scalar(f_sv1)

                    def f_spv1(e, qc=qc, EV_V0=EV_V0, DV_ONES1=DV_ONES1):
                        e.wait_ge(S_["dv"].h, DV_ONES1)
                        for tt in range(4):
                            e.wait_ge(S_["ev"].h, EV_V0 + tt + 1)
                            dst = cc_in[qc, CHK:CHK + CHV].rearrange("(f p t) -> f p t", f=4, p=128)[tt]
                            S_["kv"].inc(e.dma_start(out=dst, in_=vto1[:, 0, :]), 16)
                    blk.sync(f_spv1)

                if "dbg_q1" in dbg:
                    dq1 = dram_dbg("dbg_q1", [128, DCH, TOK], U16)
                    _mmq = S_["mm"].val
                    _acq = S_["ac"].val
                    blk.gpsimd(lambda e: (e.wait_ge(S_["ac"].h, _acq),
                                          S_["gst"].inc(e.dma_start(out=dq1.bitcast(BF16), in_=qfm1[:]), 16)))
                MM_G1 = S_["mm"].val

                def f_wg1(e):
                    e.wait_ge(S_["mm"].h, MM_G1)
                    S_["gw"].inc(e.dma_start(out=w3[:], in_=l1_wgateT.rearrange("(c p) o -> p c o", p=128)), 16)
                blk.gpsimd(f_wg1)
                GW_G1 = S_["gw"].val
                for ts in range(4):
                    KV_G0 = S_["kv"].val
                    AC_G0 = S_["ac"].val
                    std_proj(w3, lambda _ts, ts=ts: xfm[:, :, ts * 512:(ts + 1) * 512],
                             lambda oi, _ts: kvo1[:, oi % 2, :], list(range(DCH)), [0], AF.Sigmoid, BG1,
                             wait_sems=[(S_["gw"], GW_G1)],
                             slot_free=lambda idx, KV_G0=KV_G0: (S_["kv"], KV_G0 + 16 * (idx - 1)))

                    def f_spg(e, ts=ts, AC_G0=AC_G0):
                        for oc in range(DCH):
                            e.wait_ge(S_["ac"].h, AC_G0 + oc + 1)
                            S_["kv"].inc(e.dma_start(out=l1g[oc, :, ts * 512:(ts + 1) * 512],
                                                     in_=kvo1[:, oc % 2, :]), 16)
                    blk.sync(f_spg)

                if "dbg_cc" in dbg:
                    dcc = dram_dbg("dbg_cc", [QC, 2048], U16)
                    _kvc = S_["kv"].val
                    blk.gpsimd(lambda e: (e.wait_ge(S_["kv"].h, _kvc),
                                          S_["gst"].inc(e.dma_start(out=dcc.bitcast(BF16),
                                                                    in_=cc_in[:, 0:2048]), 16)))
                KV_CC = S_["kv"].val

                def f_ag(e):
                    e.wait_ge(S_["kv"].h, KV_CC)
                    e.wait_ge(S_["gz"].h, KV_PAD)
                    S_["cc"].inc(e.collective_compute(
                        "AllGather", ALU.bypass,
                        ins=[cc_in[:]], outs=[cc_out.rearrange("a b c -> (a b) c")],
                        replica_groups=[[0, 1], [2, 3], [4, 5], [6, 7]]), 1)
                blk.gpsimd(f_ag)
                CC_DONE = S_["cc"].val

        if "l1attn" in phases:
            with ExitStack() as _es2:
                kst1 = _es2.enter_context(nc.sbuf_tensor("kst1", [128, 2, 128], BF16))
                vst1 = _es2.enter_context(nc.sbuf_tensor("vst1", [128, 2, 130], BF16))
                btst = _es2.enter_context(nc.sbuf_tensor("btst", [128, 2, 2, 512], BF16))
                pT1 = _es2.enter_context(nc.sbuf_tensor("pT1", [128, 2, 1024], BF16))
                sbcol = _es2.enter_context(nc.sbuf_tensor("sbcol", [128, QC, L1_NJ, 8], F32))
                wog = _es2.enter_context(nc.sbuf_tensor("wog", [128, DCH, D], BF16))
                ctx1 = xfm

                MM_A1 = S_["mm"].val

                def f_sb(e):
                    e.wait_ge(S_["mm"].h, MM_A1)
                    e.wait_ge(S_["cc"].h, CC_DONE)
                    S_["ldw"].inc(e.dma_start(out=sbcol[:], in_=l1_sb[:]), 16)
                blk.sync(f_sb)
                LDW_SB = S_["ldw"].val

                for hs in range(8):
                    for qc in range(QC):
                        nj = L1_NJ
                        ldk0, mm0, ac0, dv0 = (S_[k].val for k in ("ldk", "mm", "ac", "dv"))
                        QK1 = lambda j, mm0=mm0: mm0 + 1 if j == 0 else mm0 + 2 * j
                        AV1 = lambda j, mm0=mm0, nj=nj: mm0 + 2 * j + (3 if j < nj - 1 else 2)

                        def f_dma1(e, qc=qc, nj=nj, mm0=mm0, hs=hs, AV1=AV1):
                            for j in range(nj):
                                if j >= 2:
                                    e.wait_ge(S_["mm"].h, AV1(j - 2))
                                else:
                                    e.wait_ge(S_["mm"].h, mm0)
                                b13 = 8 * qc + j - 13
                                g, tt = b13 // 4, b13 % 4
                                if g < 0:
                                    ksrc = l1pad[(g % 4), 0:CHK].rearrange("(c p t) -> c p t", c=DCH, p=128)
                                    vsrc = l1pad[(g % 4), CHK:CHK + CHV].rearrange("(f p t) -> f p t", f=4, p=128)
                                else:
                                    ksrc = cc_out[g % 2, g // 2, 0:CHK].rearrange("(c p t) -> c p t", c=DCH, p=128)
                                    vsrc = cc_out[g % 2, g // 2, CHK:CHK + CHV].rearrange("(f p t) -> f p t", f=4, p=128)
                                S_["ldk"].inc(e.dma_start(out=kst1[:, j % 2, :],
                                                          in_=ksrc[hs, :, tt * 128:(tt + 1) * 128]), 16)
                                S_["ldk"].inc(e.dma_start(out=vst1[:, j % 2, :],
                                                          in_=vsrc[tt, :, 130 * hs:130 * hs + 130]), 16)
                                S_["ldk"].inc(e.dma_start(out=btst[:, j % 2, :, :],
                                                          in_=l1_btile[j, 2 * hs:2 * hs + 2].rearrange("h p t -> p h t").bitcast(BF16)), 16)
                        blk.sync(f_dma1)

                        def f_pe1(e, qc=qc, nj=nj, ldk0=ldk0, mm0=mm0, ac0=ac0, dv0=dv0, hs=hs):
                            if hs == 0 and qc == 0:
                                e.wait_ge(S_["ldw"].h, LDW_SB)

                            def emit_qk(j):
                                strip = (psA, psB)[j % 2]
                                e.wait_ge(S_["ldk"].h, ldk0 + 48 * j + 48)
                                if j >= 2:
                                    e.wait_ge(S_["ac"].h, ac0 + j - 1)
                                elif not (hs == 0 and qc == 0):
                                    e.wait_ge(S_["ac"].h, ac0 - 2)
                                for hp in range(2):
                                    e.matmul(strip[:, hp * 512:(hp + 1) * 512],
                                             kst1[hp * 64:(hp + 1) * 64, j % 2, :],
                                             qfm1[hp * 64:(hp + 1) * 64, hs, qc * 512:(qc + 1) * 512],
                                             start=True, stop=False)
                                    ins = e.matmul(strip[:, hp * 512:(hp + 1) * 512],
                                                   identt[:], btst[:, j % 2, hp, :],
                                                   start=False, stop=True)
                                S_["mm"].inc(ins)

                            def emit_av(j):
                                e.wait_ge(S_["ac"].h, ac0 + j + 1)
                                if j == 0:
                                    e.wait_ge(S_["dv"].h, dv0)
                                for hp in range(2):
                                    avp = psC[0:65, hp * 512:(hp + 1) * 512]
                                    ins = e.matmul(avp, vst1[:, j % 2, hp * 65:(hp + 1) * 65],
                                                   pT1[:, j % 2, hp * 512:(hp + 1) * 512],
                                                   start=(j == 0), stop=(j == nj - 1))
                                S_["mm"].inc(ins)

                            emit_qk(0)
                            for j in range(1, nj):
                                emit_qk(j)
                                emit_av(j - 1)
                            emit_av(nj - 1)
                        blk.tensor(f_pe1)

                        def f_exp1(e, qc=qc, nj=nj, mm0=mm0, QK1=QK1, AV1=AV1):
                            for j in range(nj):
                                strip = (psA, psB)[j % 2]
                                e.wait_ge(S_["mm"].h, QK1(j))
                                S_["ac"].inc(e.activation(out=pT1[:, j % 2, :], in_=strip[:],
                                                          func=AF.Exp, bias=sbcol[:, qc, j, 0:1]))
                        blk.scalar(f_exp1)

                        MM_AVE1 = S_["mm"].val

                        def f_nv1(e, MM_AVE1=MM_AVE1):
                            e.wait_ge(S_["mm"].h, MM_AVE1)
                            S_["dv"].inc(e.reciprocal(rrow[0:1, 0:512], psC[64:65, 0:512]))
                            S_["dv"].inc(e.reciprocal(rrow[0:1, 512:1024], psC[64:65, 512:1024]))
                        blk.vector(f_nv1)
                        DV_R1 = S_["dv"].val
                        mm1 = S_["mm"].val

                        def f_nt1(e, DV_R1=DV_R1):
                            e.wait_ge(S_["dv"].h, DV_R1)
                            rbdst = (psD[0:64, :], psD[64:128, :])
                            for hp in range(2):
                                S_["mm"].inc(e.matmul(rbdst[hp], onesf[0:1, :],
                                                      rrow[0:1, 512 * hp:512 * hp + 512], start=True, stop=True))
                        blk.tensor(f_nt1)
                        AC_RB1 = S_["ac"].val

                        def f_nrb1(e, mm1=mm1):
                            rbdst = (psD[0:64, :], psD[64:128, :])
                            for hp in range(2):
                                e.wait_ge(S_["mm"].h, mm1 + hp + 1)
                                S_["ac"].inc(e.copy(out=scr2[0:64, 512 * hp:512 * hp + 512], in_=rbdst[hp]))
                        blk.scalar(f_nrb1)

                        def f_nv21(e, qc=qc, hs=hs, AC_RB1=AC_RB1):
                            for hp in range(2):
                                e.wait_ge(S_["ac"].h, AC_RB1 + hp + 1)
                                S_["dv"].inc(e.tensor_mul(
                                    ctx1[hp * 64:(hp + 1) * 64, hs, qc * 512:(qc + 1) * 512],
                                    psC[0:64, hp * 512:(hp + 1) * 512],
                                    scr2[0:64, 512 * hp:512 * hp + 512]))
                        blk.vector(f_nv21)

                DV_CTX1 = S_["dv"].val
                if "dbg_ctx1" in dbg:
                    dc1 = dram_dbg("dbg_ctx1", [128, DCH, TOK], U16)
                    blk.gpsimd(lambda e: (e.wait_ge(S_["dv"].h, DV_CTX1),
                                          S_["gst"].inc(e.dma_start(out=dc1.bitcast(BF16), in_=ctx1[:]), 16)))

                def f_wgl(e):
                    e.wait_ge(S_["mm"].h, MM_A1)
                    S_["gw"].inc(e.dma_start(out=wog[:], in_=l1_woutT.rearrange("(c p) o -> p c o", p=128)), 16)
                blk.gpsimd(f_wgl)
                GW_WOUT = S_["gw"].val
                ldk0g = S_["ldk"].val
                dv0g = S_["dv"].val

                def f_gld(e, ldk0g=ldk0g, dv0g=dv0g):
                    for t in range(NT):
                        if t == 0:
                            e.wait_ge(S_["dv"].h, DV_CTX1)
                        if t >= 2:
                            e.wait_ge(S_["dv"].h, dv0g + t - 1)
                        S_["ldk"].inc(e.dma_start(
                            out=btst[:, t % 2, :, :].rearrange("p a b -> p (a b)").rearrange("p (c x) -> p c x", c=8),
                            in_=l1g[:, :, t * 128:(t + 1) * 128].rearrange("c p t -> p c t")), 16)
                blk.sync(f_gld)

                def f_gmul(e, ldk0g=ldk0g, dv0g=dv0g):
                    for t in range(NT):
                        e.wait_ge(S_["ldk"].h, ldk0g + 16 * (t + 1))
                        S_["dv"].inc(e.tensor_mul(
                            ctx1[:, :, t * 128:(t + 1) * 128],
                            ctx1[:, :, t * 128:(t + 1) * 128],
                            btst[:, t % 2, :, :].rearrange("p a b -> p (a b)").rearrange("p (c x) -> p c x", c=8)))
                blk.vector(f_gmul)
                DV_GM = S_["dv"].val

                swap_proj_residual(ctx1, wog, DCH, 1, AF.Identity,
                                   wait_sems=[(S_["dv"], DV_GM), (S_["gw"], GW_WOUT)])
            qfm1g.close()

        if "l1ffn" in phases:
            with ExitStack() as _es2:
                h1 = _es2.enter_context(nc.sbuf_tensor("h1", [128, FCH, 512], BF16))
                cup = _es2.enter_context(nc.sbuf_tensor("cup", [128, FCH, 128], BF16))
                cdn = _es2.enter_context(nc.sbuf_tensor("cdn", [128, FCH, 128], BF16))
                mstg = _es2.enter_context(nc.sbuf_tensor("mstg", [128, 7, 128], BF16))
                xsc = _es2.enter_context(nc.sbuf_tensor("xsc", [128, 128], BF16))
                hsc = _es2.enter_context(nc.sbuf_tensor("hsc", [128, 2, 2, 512], BF16))
                wexh = _es2.enter_context(nc.sbuf_tensor("wexh", [128, 2, DCH, 128], BF16))
                wconh = _es2.enter_context(nc.sbuf_tensor("wconh", [128, 2, D], BF16))
                bwal = _es2.enter_context(nc.sbuf_tensor("bwal", [128, 2, 5, 16, 2, 2], F32))

                MM_F0 = S_["mm"].val
                def f_fb(e):
                    e.wait_ge(S_["mm"].h, MM_F0)
                    S_["ldw"].inc(e.dma_start(out=bias1[:, 0:32, 0:1],
                                              in_=ff_bexp.rearrange("(c p) -> p c", p=128).unsqueeze(2),
                                              allow_slow_non_contiguous=True), 16)
                    S_["ldw"].inc(e.dma_start(out=bias1[:, 32:64, 0:1],
                                              in_=ff_upb.rearrange("(c p) -> p c", p=128).unsqueeze(2),
                                              allow_slow_non_contiguous=True), 16)
                    for ud, wsrc in ((0, ff_upw), (1, ff_dnw)):
                        for s5 in range(5):
                            sa = s5 + 7
                            for ai, ch in enumerate([c for c in range(FCH) if not (c >> (sa - 7)) & 1]):
                                kbase = (ch >> (sa - 6)) * (2 ** sa) + (ch % (2 ** (sa - 7))) * 128
                                S_["ldw"].inc(e.dma_start(
                                    out=bwal[:, ud, s5, ai, :, :],
                                    in_=wsrc[sa, kbase:kbase + 128, :, :],
                                    allow_slow_non_contiguous=True), 16)
                blk.sync(f_fb)
                LDW_FB = S_["ldw"].val

                def compose(c_dst, stsrc, tag):
                    ldk0, tp0, ev0, mm0, ac0 = (S_[k].val for k in ("ldk", "tp", "ev", "mm", "ac"))

                    def f_cd(e, ldk0=ldk0, ac0=ac0):
                        e.wait_ge(S_["mm"].h, MM_F0)
                        for ch in range(FCH):
                            if ch >= 1:
                                e.wait_ge(S_["ac"].h, ac0 + 7 * ch)
                            S_["gz"].inc(e.dma_start(out=mstg[:],
                                                      in_=stsrc[:, ch, :, :].rearrange("s p t -> p s t")), 16)
                    blk.gpsimd(f_cd)

                    gz0 = S_["gz"].val - 16 * FCH

                    def f_ct(e, gz0=gz0, tp0=tp0, ev0=ev0, mm0=mm0, ac0=ac0):
                        for ch in range(FCH):
                            e.wait_ge(S_["gz"].h, gz0 + 16 * (ch + 1))
                            if ch >= 1:
                                e.wait_ge(S_["ac"].h, ac0 + 7 * ch)
                            ins = e.transpose(psT[:, 0:128], mstg[:, 6, :], identt[:])
                            S_["tp"].inc(ins)
                            for s in range(5, -1, -1):
                                e.wait_ge(S_["ac"].h, ac0 + 7 * ch + (6 - s))
                                ins = e.matmul(psA[:, 0:128], mstg[:, s, :], xsc[:], start=True, stop=True)
                                S_["mm"].inc(ins)
                    blk.tensor(f_ct)

                    def f_cs(e, tp0=tp0, mm0=mm0, ac0=ac0, c_dst=c_dst):
                        for ch in range(FCH):
                            e.wait_ge(S_["tp"].h, tp0 + ch + 1)
                            S_["ac"].inc(e.copy(out=xsc[:], in_=psT[:, 0:128]))
                            for s in range(5, -1, -1):
                                e.wait_ge(S_["mm"].h, mm0 + 6 * ch + (6 - s))
                                dst = xsc[:] if s > 0 else c_dst[:, ch, :]
                                S_["ac"].inc(e.copy(out=dst, in_=psA[:, 0:128]))
                    blk.scalar(f_cs)
                compose(cup, ff_upT, "up")
                compose(cdn, ff_dnT, "dn")
                LDW_C = S_["ldw"].val

                MM_PRE4 = S_["mm"].val
                ln_sweep([("sbuf", x_res[:, t, :]) for t in range(NT)], xfm, 0,
                         first_evac_wait=(S_["mm"], MM_PRE4))
                EV_LN4 = S_["ev"].val

                slots = [psA[:, 0:512], psA[:, 512:1024], psB[:, 0:512], psB[:, 512:1024]]

                def bfly_pairs():
                    out = []
                    for s5 in range(5):
                        step = 2 ** s5
                        achs = [c for c in range(FCH) if not (c >> s5) & 1]
                        for ai, ch in enumerate(achs):
                            out.append((ch, ch + step, s5, ai))
                    return out

                def butterfly_large(ts, ud, cols):
                    pairs = bfly_pairs()
                    dvb = S_["dv"].val
                    acb = S_["ac"].val

                    def f_a(e, pairs=pairs, dvb=dvb, ud=ud):
                        e.wait_ge(S_["dv"].h, dvb)
                        for q, (a, b, s5, ai) in enumerate(pairs):
                            u = q % 2
                            if q >= 2:
                                e.wait_ge(S_["dv"].h, dvb + 6 * (q // 2))
                            S_["ac"].inc(e.activation(out=scr2[:, 512 * u:512 * u + 512],
                                                      in_=h1[:, b, :], func=AF.Identity,
                                                      scale=cols[:, ud, s5, ai, 0, 1:2]))
                            S_["ac"].inc(e.activation(out=hsc[:, u, 1, :],
                                                      in_=h1[:, b, :], func=AF.Identity,
                                                      scale=cols[:, ud, s5, ai, 1, 1:2]))
                    blk.scalar(f_a)

                    def f_v(e, pairs=pairs, acb=acb, ud=ud):
                        for g in range(len(pairs) // 2):
                            e.wait_ge(S_["ac"].h, acb + 4 * (g + 1))
                            P = (pairs[2 * g], pairs[2 * g + 1])
                            for u in range(2):
                                S_["dv"].inc(e.tensor_copy(out=hsc[:, u, 0, :], in_=h1[:, P[u][0], :]))
                            for u in range(2):
                                a, b, s5, ai = P[u]
                                S_["dv"].inc(e.scalar_tensor_tensor(
                                    h1[:, a, :], hsc[:, u, 0, :], cols[:, ud, s5, ai, 0, 0:1],
                                    scr2[:, 512 * u:512 * u + 512], ALU.mult, ALU.add))
                            for u in range(2):
                                a, b, s5, ai = P[u]
                                S_["dv"].inc(e.scalar_tensor_tensor(
                                    h1[:, b, :], hsc[:, u, 0, :], cols[:, ud, s5, ai, 1, 0:1],
                                    hsc[:, u, 1, :], ALU.mult, ALU.add))
                    blk.vector(f_v)

                for ts in range(4):
                    mm0, ac0, gw0 = S_["mm"].val, S_["ac"].val, S_["gw"].val

                    def f_xw(e, ts=ts, mm0=mm0, gw0=gw0):
                        for oc in range(FCH):
                            if oc < 2:
                                e.wait_ge(S_["mm"].h, mm0)
                            else:
                                e.wait_ge(S_["mm"].h, mm0 + (oc - 2) + 1)
                            S_["gw"].inc(e.dma_start(
                                out=wexh[:, oc % 2, :, :],
                                in_=ff_wexp[:, oc * 128:(oc + 1) * 128].rearrange("(c p) o -> p c o", p=128)), 16)
                    blk.gpsimd(f_xw)

                    DV_PREV = S_["dv"].val

                    def f_xt(e, ts=ts, mm0=mm0, ac0=ac0, gw0=gw0, DV_PREV=DV_PREV):
                        if ts == 0:
                            e.wait_ge(S_["ev"].h, EV_LN4)
                            e.wait_ge(S_["ldw"].h, LDW_C)
                        e.wait_ge(S_["dv"].h, DV_PREV)
                        for oc in range(FCH):
                            e.wait_ge(S_["gw"].h, gw0 + 16 * (oc + 1))
                            if oc >= 4:
                                e.wait_ge(S_["ac"].h, ac0 + oc - 3)
                            for c in range(DCH):
                                ins = e.matmul(slots[oc % 4], wexh[:, oc % 2, c, :],
                                               xfm[:, c, ts * 512:(ts + 1) * 512],
                                               start=(c == 0), stop=(c == DCH - 1))
                            S_["mm"].inc(ins)
                    blk.tensor(f_xt)

                    def f_xs(e, mm0=mm0, ac0=ac0):
                        for oc in range(FCH):
                            e.wait_ge(S_["mm"].h, mm0 + oc + 1)
                            S_["ac"].inc(e.activation(out=h1[:, oc, :], in_=slots[oc % 4],
                                                      func=AF.Identity, bias=bias1[:, oc, 0:1]))
                    blk.scalar(f_xs)
                    AC_EXP = S_["ac"].val

                    mm1 = S_["mm"].val
                    ac1 = S_["ac"].val

                    def f_bu_t(e, mm1=mm1, ac1=ac1, AC_EXP=AC_EXP):
                        e.wait_ge(S_["ac"].h, AC_EXP)
                        for ch in range(FCH):
                            if ch >= 4:
                                e.wait_ge(S_["ac"].h, ac1 + ch - 3)
                            ins = e.matmul(slots[ch % 4], cup[:, ch, :], h1[:, ch, :], start=True, stop=True)
                            S_["mm"].inc(ins)
                    blk.tensor(f_bu_t)

                    def f_bu_s(e, mm1=mm1):
                        for ch in range(FCH):
                            e.wait_ge(S_["mm"].h, mm1 + ch + 1)
                            S_["ac"].inc(e.copy(out=h1[:, ch, :], in_=slots[ch % 4]))
                    blk.scalar(f_bu_s)
                    if ts == 0:
                        blk.scalar(lambda e: e.wait_ge(S_["ldw"].h, LDW_FB))
                    butterfly_large(ts, 0, bwal)
                    DV_BU = S_["dv"].val

                    ac2 = S_["ac"].val

                    def f_gel(e, DV_BU=DV_BU):
                        e.wait_ge(S_["dv"].h, DV_BU)
                        for ch in range(FCH):
                            S_["ac"].inc(e.activation(out=h1[:, ch, :], in_=h1[:, ch, :],
                                                      func=AF.Gelu, bias=bias1[:, 32 + ch, 0:1]))
                    blk.scalar(f_gel)
                    AC_GEL = S_["ac"].val

                    mm2 = S_["mm"].val
                    ac3 = S_["ac"].val

                    def f_bd_t(e, mm2=mm2, ac3=ac3, AC_GEL=AC_GEL):
                        e.wait_ge(S_["ac"].h, AC_GEL)
                        for ch in range(FCH):
                            if ch >= 4:
                                e.wait_ge(S_["ac"].h, ac3 + ch - 3)
                            ins = e.matmul(slots[ch % 4], cdn[:, ch, :], h1[:, ch, :], start=True, stop=True)
                            S_["mm"].inc(ins)
                    blk.tensor(f_bd_t)

                    def f_bd_s(e, mm2=mm2):
                        for ch in range(FCH):
                            e.wait_ge(S_["mm"].h, mm2 + ch + 1)
                            S_["ac"].inc(e.copy(out=h1[:, ch, :], in_=slots[ch % 4]))
                    blk.scalar(f_bd_s)
                    butterfly_large(ts, 1, bwal)
                    DV_BD = S_["dv"].val

                    for rnd in range(2):
                        mm3, gw3, dv3 = S_["mm"].val, S_["gw"].val, S_["dv"].val
                        pss = (psC, psB)

                        def f_cw(e, ts=ts, rnd=rnd, mm3=mm3, DV_BD=DV_BD):
                            for hc in range(FCH):
                                if hc >= 2:
                                    e.wait_ge(S_["mm"].h, mm3 + (hc - 2) + 1)
                                S_["gw"].inc(e.dma_start(
                                    out=wconh[:, hc % 2, :],
                                    in_=ff_wcon[hc * 128:(hc + 1) * 128, :]), 16)
                        blk.gpsimd(f_cw)

                        def f_c_t(e, ts=ts, rnd=rnd, mm3=mm3, gw3=gw3, dv3=dv3, DV_BD=DV_BD):
                            e.wait_ge(S_["dv"].h, DV_BD)
                            if rnd == 1:
                                e.wait_ge(S_["dv"].h, dv3)
                            for hc in range(FCH):
                                e.wait_ge(S_["gw"].h, gw3 + 16 * (hc + 1))
                                for t2 in range(2):
                                    tt = ts * 4 + rnd * 2 + t2
                                    lt = tt % 4
                                    e.matmul(pss[t2][:, 0:512], h1[:, hc, lt * 128:(lt + 1) * 128],
                                             wconh[:, hc % 2, 0:512], start=(hc == 0), stop=(hc == FCH - 1))
                                    ins = e.matmul(pss[t2][:, 512:1024], h1[:, hc, lt * 128:(lt + 1) * 128],
                                                   wconh[:, hc % 2, 512:1024], start=(hc == 0), stop=(hc == FCH - 1))
                                S_["mm"].inc(ins, 1)
                        blk.tensor(f_c_t)
                        MM_C = S_["mm"].val

                        def f_c_v(e, ts=ts, rnd=rnd, MM_C=MM_C):
                            e.wait_ge(S_["mm"].h, MM_C)
                            for t2 in range(2):
                                tt = ts * 4 + rnd * 2 + t2
                                e.tensor_add(scr2[:], pss[t2][:], bcast[:, 2, :])
                                S_["dv"].inc(e.tensor_add(x_res[:, tt, :], x_res[:, tt, :], scr2[:]))
                        blk.vector(f_c_v)

        if "dbg_res2" in dbg:
            dr2 = dram_dbg("dbg_res2", [128, NT, D])
            dv0 = S_["dv"].val
            blk.gpsimd(lambda e: (e.wait_ge(S_["dv"].h, dv0),
                                  S_["gst"].inc(e.dma_start(out=dr2, in_=x_res[:]), 16)))

        if "dbg_res1" in dbg:
            dr = dram_dbg("dbg_res1", [128, NT, D])
            dv0 = S_["dv"].val
            blk.gpsimd(lambda e: (e.wait_ge(S_["dv"].h, dv0),
                                  S_["gst"].inc(e.dma_start(out=dr, in_=x_res[:]), 16)))

        DV_END = S_["dv"].val
        MM_END = S_["mm"].val
        AC_END = S_["ac"].val

        def f_out(e):
            e.wait_ge(S_["dv"].h, DV_END)
            e.wait_ge(S_["mm"].h, MM_END)
            e.wait_ge(S_["ac"].h, AC_END)
            S_["st"].inc(e.dma_start(out=out_own.rearrange("(t p) d -> p t d", p=128), in_=x_res[:]), 16)
        blk.sync(f_out)
        ST_END = S_["st"].val
        blk.gpsimd(lambda e: (e.wait_ge(S_["st"].h, ST_END), e.wait_ge(S_["gst"].h, S_["gst"].val)))

    return nc, inputs, dbg_names


OFFSETS = sorted(set(range(0, 65)) | {96, 128, 192, 256, 384, 512, 768, 1024, 1536})


def own_rows(core):
    seq, r = core // 2, core % 2
    rows = []
    for qc in range(QC):
        g = 2 * qc + r
        rows.append(np.arange(seq * S + g * 512, seq * S + g * 512 + 512))
    return np.concatenate(rows)


def make_winmask(r):
    w = np.arange(8)[:, None, None]
    p = np.arange(128)[None, :, None]
    f = np.arange(512)[None, None, :]
    return ((w * 128 + p) <= (r * 512 + f)).astype(bfd).view(np.uint16)


def make_l1_btile(r, pos_bias):
    o2idx = -np.ones(4096, dtype=np.int64)
    for i, o in enumerate(OFFSETS):
        o2idx[o] = i
    jj = np.arange(L1_NJ)[:, None, None]
    p = np.arange(128)[None, :, None]
    f = np.arange(512)[None, None, :]
    o = (r * 512 + (13 - jj) * 128) + f - p
    valid = (o >= 0) & (o <= 1536)
    idx = np.where(valid, o2idx[np.clip(o, 0, 4095)], -1)
    valid = idx >= 0
    pb = pos_bias[np.clip(idx, 0, 73), :]
    out = np.where(valid[..., None], pb, np.float32(-1e9)).astype(bfd)
    return np.ascontiguousarray(out.transpose(0, 3, 1, 2)).view(np.uint16)


def make_l1_sb(r):
    out = np.full((QC, L1_NJ, 8, 128), -1e9, dtype=np.float32)
    for qc in range(QC):
        g = 2 * qc + r
        for jj in range(L1_NJ):
            ab = 8 * qc + jj - 13
            if ab < 4 * g - 13 - 1 or ab > 4 * g + 3:
                pass
            kpos = ab * 128 + np.arange(128)
            ok = (kpos >= 0) & (ab >= 4 * g - 12) & (ab <= 4 * g + 3)
            out[qc, jj, 0, :] = np.where(ok, np.float32(-4.0), np.float32(-1e9))
    return np.ascontiguousarray(out.transpose(3, 0, 1, 2))


def scatter_stageT(wts):
    out = np.zeros((7, FF_H // 128, 128, 128), dtype=np.float32)
    for st in range(7):
        s = 2 ** st
        k = np.arange(FF_H // 2)
        i = (k // s) * 2 * s + (k % s)
        j = i + s
        w = wts[st]
        ch = i // 128
        il, jl = i % 128, j % 128
        assert np.all(j // 128 == ch)
        out[st, ch, il, il] = w[:, 0, 0]
        out[st, ch, il, jl] = w[:, 0, 1]
        out[st, ch, jl, il] = w[:, 1, 0]
        out[st, ch, jl, jl] = w[:, 1, 1]
    return out


def core_inputs(core, inp):
    seq, r = core // 2, core % 2
    x = np.asarray(inp["L0_attn_states"]).reshape(B * S, D)
    rows = own_rows(core)
    d = {
        "x_own": np.ascontiguousarray(x[rows]),
        "x_seq": np.ascontiguousarray(x[seq * S:(seq + 1) * S]),
        "l0_wq": np.asarray(inp["L0_attn_Wq"]), "l0_bq": np.asarray(inp["L0_attn_bq"]),
        "l0_wk": np.asarray(inp["L0_attn_Wk"]), "l0_bk": np.asarray(inp["L0_attn_bk"]),
        "l0_wv": np.asarray(inp["L0_attn_Wv"]), "l0_bv": np.asarray(inp["L0_attn_bv"]),
        "l0_wo": np.asarray(inp["L0_attn_Wo"]), "l0_bo": np.asarray(inp["L0_attn_bo"]),
        "l0_winmask": make_winmask(r),
        "moe_gw": np.asarray(inp["L0_ffn_gate_w"]),
        "moe_wg": np.asarray(inp["L0_ffn_w_gate"]),
        "moe_wu": np.asarray(inp["L0_ffn_w_up"]),
        "moe_wd": np.asarray(inp["L0_ffn_w_down"]),
        "moe_wsg": np.asarray(inp["L0_ffn_ws_gate"]),
        "moe_wsu": np.asarray(inp["L0_ffn_ws_up"]),
        "moe_wsd": np.asarray(inp["L0_ffn_ws_down"]),
        "l1_wqkvT": np.ascontiguousarray(np.asarray(inp["L1_attn_Wqkv"]).T), "l1_bqkv": np.asarray(inp["L1_attn_bqkv"]),
        "l1_wgateT": np.ascontiguousarray(np.asarray(inp["L1_attn_Wgate"]).T), "l1_bgate": np.asarray(inp["L1_attn_bgate"]),
        "l1_woutT": np.ascontiguousarray(np.asarray(inp["L1_attn_Wout"]).T), "l1_bout": np.asarray(inp["L1_attn_bout"]),
        "l1_btile": make_l1_btile(r, np.asarray(inp["L1_attn_pos_bias"])),
        "l1_sb": make_l1_sb(r),
        "ff_wexp": np.asarray(inp["L1_ffn_W_exp"]), "ff_bexp": np.asarray(inp["L1_ffn_b_exp"]),
        "ff_upT": scatter_stageT(np.asarray(inp["L1_ffn_up_weights"])),
        "ff_dnT": scatter_stageT(np.asarray(inp["L1_ffn_down_weights"])),
        "ff_upw": np.asarray(inp["L1_ffn_up_weights"]),
        "ff_dnw": np.asarray(inp["L1_ffn_down_weights"]),
        "ff_upb": np.asarray(inp["L1_ffn_up_bias"]),
        "ff_wcon": np.asarray(inp["L1_ffn_W_con"]),
        "ff_bsum": np.stack([np.asarray(inp["L1_ffn_b_con"]), np.asarray(inp["L1_ffn_down_bias"])]),
    }
    return d


def assemble_output(results):
    out = np.zeros((B * S, D), dtype=np.float32)
    for core in range(8):
        out[own_rows(core)] = results[core]["out_own"]
    return out.reshape(B, S, D)


from concourse.bass_utils import run_bass_kernel_spmd

_CACHE = {}


def kernel(**inputs):
    if "nc" not in _CACHE:
        nc, inspecs, _ = build()
        _CACHE["nc"] = (nc, inspecs)
    nc, inspecs = _CACHE["nc"]
    in_maps = []
    for c in range(8):
        m = core_inputs(c, inputs)
        in_maps.append({k: m[k] for k in inspecs})
    res = run_bass_kernel_spmd(nc, in_maps, core_ids=list(range(8)))
    return assemble_output(res.results)


# version 07
# speedup vs baseline: 1.4278x; 1.4278x over previous
import numpy as np
import ml_dtypes
import concourse.bass as bass
import concourse.mybir as mybir

F32 = mybir.dt.float32
BF16 = mybir.dt.bfloat16
AF = mybir.ActivationFunctionType
ALU = mybir.AluOpType
AX = mybir.AxisListType
bfd = ml_dtypes.bfloat16
U16 = mybir.dt.uint16

B, S, D = 4, 4096, 1024
H, HD = 16, 64
TOK, NT, QC, NG = 2048, 16, 4, 8
DCH = 8
MOE_D, MOE_E = 2048, 8
FF_H, FCH = 4096, 32
L1_NJ = 21
VCOL = 65
PH_ALL = ("l0qkv", "l0attn", "moe", "l1qkv", "l1attn", "l1ffn")


class Sem:
    def __init__(self, nc, name):
        self.h = nc.alloc_semaphore(name)
        self.val = 0

    def inc(self, ins, n=1):
        ins.then_inc(self.h, n)
        self.val += n
        return ins


def build(phases=PH_ALL, dbg=()):
    phases = list(phases)
    dbg = set(dbg)
    nc = bass.Bass("TRN2", target_bir_lowering=False, detect_race_conditions=False)
    inputs = {}
    dbg_names = []

    def dram_in(name, shape, dtype=F32):
        inputs[name] = (tuple(shape), dtype)
        return nc.dram_tensor(name, list(shape), dtype, kind="ExternalInput").ap()

    def dram_dbg(name, shape, dtype=F32):
        dbg_names.append(name)
        return nc.dram_tensor(name, list(shape), dtype, kind="ExternalOutput").ap()

    x_own = dram_in("x_own", [TOK, D])
    x_seq = dram_in("x_seq", [S, D])
    l0_wq = dram_in("l0_wq", [D, D]); l0_bq = dram_in("l0_bq", [D])
    l0_wk = dram_in("l0_wk", [D, D]); l0_bk = dram_in("l0_bk", [D])
    l0_wv = dram_in("l0_wv", [D, D]); l0_bv = dram_in("l0_bv", [D])
    l0_wo = dram_in("l0_wo", [D, D]); l0_bo = dram_in("l0_bo", [D])
    l0_winmask = dram_in("l0_winmask", [8, 128, 512], U16)
    moe_gw = dram_in("moe_gw", [D, MOE_E])
    moe_wg = dram_in("moe_wg", [MOE_E, D, MOE_D])
    moe_wu = dram_in("moe_wu", [MOE_E, D, MOE_D])
    moe_wd = dram_in("moe_wd", [MOE_E, MOE_D, D])
    moe_wsg = dram_in("moe_wsg", [D, MOE_D])
    moe_wsu = dram_in("moe_wsu", [D, MOE_D])
    moe_wsd = dram_in("moe_wsd", [MOE_D, D])
    l1_wqkvT = dram_in("l1_wqkvT", [D, 3 * D]); l1_bqkv = dram_in("l1_bqkv", [3 * D])
    l1_wgateT = dram_in("l1_wgateT", [D, D]); l1_bgate = dram_in("l1_bgate", [D])
    l1_woutT = dram_in("l1_woutT", [D, D]); l1_bout = dram_in("l1_bout", [D])
    l1_btile = dram_in("l1_btile", [L1_NJ, H, 128, 512], U16)
    l1_sb = dram_in("l1_sb", [128, QC, L1_NJ, 8])
    ff_wexp = dram_in("ff_wexp", [D, FF_H]); ff_bexp = dram_in("ff_bexp", [FF_H])
    ff_upT = dram_in("ff_upT", [7, FCH, 128, 128])
    ff_dnT = dram_in("ff_dnT", [7, FCH, 128, 128])
    ff_upw = dram_in("ff_upw", [12, FF_H // 2, 2, 2])
    ff_dnw = dram_in("ff_dnw", [12, FF_H // 2, 2, 2])
    ff_upb = dram_in("ff_upb", [FF_H])
    ff_wcon = dram_in("ff_wcon", [FF_H, D])
    ff_bsum = dram_in("ff_bsum", [2, D])
    out_own = nc.dram_tensor("out_own", [TOK, D], F32, kind="ExternalOutput").ap()

    l0k = nc.dram_tensor("l0k", [NG, DCH, 128, 512], BF16, kind="Internal").ap()
    l0v = nc.dram_tensor("l0v", [NG, 4, 128, VCOL * H], BF16, kind="Internal").ap()
    l1g = nc.dram_tensor("l1g", [DCH, 128, TOK], BF16, kind="Internal").ap()
    CHK = DCH * 128 * 512
    CHV = 4 * 128 * (VCOL * H)
    cc_in = nc.dram_tensor("cc_in", [QC, CHK + CHV], BF16, kind="Internal").ap()
    cc_out = nc.dram_tensor("cc_out", [2, QC, CHK + CHV], BF16, kind="Internal").ap()
    l1pad = nc.dram_tensor("l1pad", [4, CHK + CHV], BF16, kind="Internal").ap()

    def l1_kv(g):
        base = l1pad[(g % 4), :] if g < 0 else cc_out[g % 2, g // 2, :]
        k = base[0:CHK].rearrange("(c p t) -> c p t", c=DCH, p=128)
        v = base[CHK:CHK + CHV].rearrange("(f p t) -> f p t", f=4, p=128)
        return k, v

    ident_np = nc.inline_tensor(np.eye(128, dtype=bfd), "ident_bf")
    ones_np = nc.inline_tensor(np.ones((1, 128), dtype=bfd), "ones_row2")
    onesf_np = nc.inline_tensor(np.ones((1, 64), dtype=np.float32), "ones_f32")

    from contextlib import ExitStack
    with ExitStack() as _es:
        x_res = _es.enter_context(nc.sbuf_tensor("x_res", [128, NT, D], F32))
        identt = _es.enter_context(nc.sbuf_tensor("identt", [128, 128], BF16))
        onesr = _es.enter_context(nc.sbuf_tensor("onesr", [1, 128], BF16))
        onesf = _es.enter_context(nc.sbuf_tensor("onesf", [1, 64], F32))
        stats = _es.enter_context(nc.sbuf_tensor("stats", [128, 16, 8], F32))
        epscol = _es.enter_context(nc.sbuf_tensor("epscol", [128, 8], F32))
        scr = _es.enter_context(nc.sbuf_tensor("scr", [128, 1024], F32))
        scr2 = _es.enter_context(nc.sbuf_tensor("scr2", [128, 1024], F32))
        bias1 = _es.enter_context(nc.sbuf_tensor("bias1", [128, 64, 8], F32))
        bcast = _es.enter_context(nc.sbuf_tensor("bcast", [128, 3, 1024], BF16))
        lnbf = _es.enter_context(nc.sbuf_tensor("lnbf", [128, 1, 1024], BF16))
        xfm = _es.enter_context(nc.sbuf_tensor("xfm", [128, DCH, TOK], BF16))
        psA = _es.enter_context(nc.psum_tensor("psA", [128, 1024], F32))
        psB = _es.enter_context(nc.psum_tensor("psB", [128, 1024], F32))
        psC = _es.enter_context(nc.psum_tensor("psC", [128, 1024], F32))
        psD = _es.enter_context(nc.psum_tensor("psD", [128, 512], F32))
        psT = _es.enter_context(nc.psum_tensor("psT", [128, 1024], BF16))
        rrow = scr
        blk = _es.enter_context(nc.Block())
        S_ = {k: Sem(nc, k) for k in
              ["ldw", "ldx", "ldk", "mm", "tp", "ev", "dv", "ac", "kv", "cc", "gp", "st",
               "gw", "gx", "gz", "gst"]}
        BQ, BK, BV, BG1, BQK = 0, 8, 16, 24, 32

        def ln_sweep(tiles, dst_fm, dst_off, first_evac_wait=None, xsrc=None):
            n = len(tiles)
            ldx0, dv0, ac0, tp0, ev0 = (S_[k].val for k in ("ldx", "dv", "ac", "tp", "ev"))

            def f_sync(e):
                hseen = 0
                for i, (kind, ap_) in enumerate(tiles):
                    if kind == "hbm":
                        if hseen >= 2:
                            e.wait_ge(S_["dv"].h, dv0 + 2 * (i - 2) + 2)
                        S_["ldx"].inc(e.dma_start(out=xsrc[:, i % 2, :], in_=ap_), 16)
                        hseen += 1
            blk.sync(f_sync)

            def f_v(e):
                hseen = 0
                for i, (kind, ap_) in enumerate(tiles):
                    if kind == "hbm":
                        hseen += 1
                        e.wait_ge(S_["ldx"].h, ldx0 + 16 * hseen)
                        src = xsrc[:, i % 2, :]
                    else:
                        src = ap_
                    a = stats[:, (2 * i) % 8, :]
                    b = stats[:, (2 * i) % 8 + 1, :]
                    dmy = stats[:, 15, :]
                    e.tensor_scalar(scr[:], src, 1.0, 0.0, ALU.mult, ALU.add, accum_out=a[:, 2:3])
                    e.scalar_tensor_tensor(scr2[:], src, 1.0, src, ALU.mult, ALU.mult, accum_out=a[:, 3:4])
                    e.tensor_scalar_mul(a[:, 1:2], a[:, 2:3], 1.0 / D)
                    e.tensor_scalar_mul(a[:, 4:5], a[:, 3:4], 1.0 / D)
                    e.tensor_mul(a[:, 5:6], a[:, 1:2], a[:, 1:2])
                    e.tensor_copy(dmy[:, 0:1], a[:, 2:3])
                    e.tensor_copy(dmy[:, 1:2], a[:, 3:4])
                    S_["dv"].inc(e.tensor_sub(a[:, 0:1], a[:, 4:5], a[:, 5:6]))
                    e.wait_ge(S_["ac"].h, ac0 + i + 1)
                    if i >= 1:
                        e.wait_ge(S_["tp"].h, tp0 + i)
                    e.reciprocal(b[:, 1:2], b[:, 0:1])
                    e.tensor_copy(dmy[:, 0:1], a[:, 2:3])
                    e.tensor_copy(dmy[:, 1:2], a[:, 3:4])
                    S_["dv"].inc(e.tensor_scalar(lnbf[:, 0, :], src, a[:, 1:2], b[:, 1:2],
                                                 ALU.subtract, ALU.mult))

            def f_s(e):
                def evac(i):
                    e.wait_ge(S_["tp"].h, tp0 + i + 1)
                    if i == 0 and first_evac_wait is not None:
                        e.wait_ge(first_evac_wait[0].h, first_evac_wait[1])
                    o = dst_off + i * 128
                    S_["ev"].inc(e.copy(out=dst_fm[:, :, o:o + 128],
                                        in_=psT[:].rearrange("p (c x) -> p c x", c=DCH)))
                for i in range(n):
                    a = stats[:, (2 * i) % 8, :]
                    b = stats[:, (2 * i) % 8 + 1, :]
                    e.wait_ge(S_["dv"].h, dv0 + 2 * i + 1)
                    S_["ac"].inc(e.activation(out=b[:, 0:1], in_=a[:, 0:1], func=AF.Sqrt, bias=epscol[:, 0:1]))
                    if i >= 1:
                        evac(i - 1)
                evac(n - 1)

            def f_t(e):
                for i in range(n):
                    e.wait_ge(S_["dv"].h, dv0 + 2 * i + 2)
                    e.wait_ge(S_["ev"].h, ev0 + i)
                    for c in range(DCH):
                        ins = e.transpose(psT[:, c * 128:(c + 1) * 128],
                                          lnbf[:, 0, c * 128:(c + 1) * 128], identt[:])
                    S_["tp"].inc(ins)
            blk.vector(f_v)
            blk.scalar(f_s)
            blk.tensor(f_t)

        def std_proj(w_sb, x_fm_sl, dst_slices, oc_list, ts_list, func, bias_base,
                     wait_sems=(), scale=1.0, slot_free=None):
            mm0, ac0 = S_["mm"].val, S_["ac"].val
            slots = [psA[:, 0:512], psA[:, 512:1024], psB[:, 0:512], psB[:, 512:1024]]
            combos = [(oi, oc, ts) for oi, oc in enumerate(oc_list) for ts in ts_list]

            def f_t(e):
                for sem, val in wait_sems:
                    e.wait_ge(sem.h, val)
                for idx, (oi, oc, ts) in enumerate(combos):
                    if idx >= 4:
                        e.wait_ge(S_["ac"].h, ac0 + idx - 3)
                    x = x_fm_sl(ts)
                    for c in range(DCH):
                        ins = e.matmul(slots[idx % 4], w_sb[:, c, oc * 128:(oc + 1) * 128],
                                       x[:, c, :], start=(c == 0), stop=(c == DCH - 1))
                    S_["mm"].inc(ins)

            def f_s(e):
                for idx, (oi, oc, ts) in enumerate(combos):
                    e.wait_ge(S_["mm"].h, mm0 + idx + 1)
                    if slot_free is not None and idx >= 2:
                        semv = slot_free(idx)
                        e.wait_ge(semv[0].h, semv[1])
                    bias = bias1[:, bias_base + oi, 0:1] if bias_base is not None else 0.0
                    S_["ac"].inc(e.activation(out=dst_slices(oi, ts), in_=slots[idx % 4],
                                              func=func, bias=bias, scale=scale))
            blk.tensor(f_t)
            blk.scalar(f_s)

        def swap_proj_residual(src_fm, w_sb, nch, bcast_idx, act_func, extra_mul_col=None,
                               wait_sems=(), gate_fm=None):
            mm0, dv0, ac0 = S_["mm"].val, S_["dv"].val, S_["ac"].val

            def f_t(e):
                for sem, val in wait_sems:
                    e.wait_ge(sem.h, val)
                for tt in range(NT):
                    if tt >= 2:
                        e.wait_ge(S_["dv"].h, dv0 + 2 * tt - 3)
                    ps = (psC, psB)[tt % 2]
                    for c in range(nch):
                        e.matmul(ps[:, 0:512], src_fm[:, c, tt * 128:(tt + 1) * 128],
                                 w_sb[:, c, 0:512], start=(c == 0), stop=(c == nch - 1))
                        ins = e.matmul(ps[:, 512:1024], src_fm[:, c, tt * 128:(tt + 1) * 128],
                                       w_sb[:, c, 512:1024], start=(c == 0), stop=(c == nch - 1))
                    S_["mm"].inc(ins)

            def f_v(e):
                for tt in range(NT):
                    ps = (psC, psB)[tt % 2]
                    e.wait_ge(S_["mm"].h, mm0 + tt + 1)
                    if bcast_idx is not None:
                        S_["dv"].inc(e.tensor_add(scr2[:], ps[:], bcast[:, bcast_idx, :]))
                    else:
                        S_["dv"].inc(e.tensor_copy(out=scr2[:], in_=ps[:]))
                    e.wait_ge(S_["ac"].h, ac0 + tt + 1)
                    if extra_mul_col is not None:
                        S_["dv"].inc(e.scalar_tensor_tensor(x_res[:, tt, :], scr[:], extra_mul_col(tt),
                                                            x_res[:, tt, :], ALU.mult, ALU.add))
                    else:
                        S_["dv"].inc(e.tensor_add(x_res[:, tt, :], x_res[:, tt, :], scr[:]))

            def f_s(e):
                for tt in range(NT):
                    e.wait_ge(S_["dv"].h, dv0 + 2 * tt + 1)
                    S_["ac"].inc(e.activation(out=scr[:], in_=scr2[:], func=act_func))
            blk.tensor(f_t)
            blk.vector(f_v)
            blk.scalar(f_s)

        def f_p0_sync(e):
            S_["ldw"].inc(e.dma_start(out=identt[:], in_=ident_np.ap()), 16)
            S_["ldw"].inc(e.dma_start(out=onesr[:], in_=ones_np.ap()), 16)
            S_["ldw"].inc(e.dma_start(out=onesf[:], in_=onesf_np.ap()), 16)
            for ap_, base in ((l0_bq, BQ), (l0_bk, BK), (l0_bv, BV)):
                S_["ldw"].inc(e.dma_start(out=bias1[:, base:base + 8, 0:1],
                                          in_=ap_.rearrange("(c p) -> p c", p=128).unsqueeze(2), allow_slow_non_contiguous=True), 16)
            S_["ldw"].inc(e.dma_start(out=scr2[0:1, 0:1024], in_=l0_bo[None, :]), 16)
            S_["ldw"].inc(e.dma_start(out=scr2[32:33, 0:1024], in_=l1_bout[None, :]), 16)
            S_["ldw"].inc(e.dma_start(out=scr2[64:65, 0:1024], in_=ff_bsum[0:1, :]), 16)
            S_["ldw"].inc(e.dma_start(out=scr2[96:97, 0:1024], in_=ff_bsum[1:2, :]), 16)
        blk.sync(f_p0_sync)
        LDW_P0 = S_["ldw"].val

        def f_p0_g(e):
            S_["gx"].inc(e.dma_start(out=x_res[:], in_=x_own.rearrange("(t p) d -> p t d", p=128)), 16)
        blk.gpsimd(f_p0_g)
        LDX_RES = S_["gx"].val

        def f_p0_v(e):
            e.wait_ge(S_["ldw"].h, LDW_P0)
            e.memset(epscol[:, 0:1], 1e-5)
            S_["dv"].inc(e.memset(scr[:], 0.0))
        blk.vector(f_p0_v)
        DV_P0 = S_["dv"].val

        def f_p0_g2(e):
            e.wait_ge(S_["dv"].h, DV_P0)
            zb = scr[:].bitcast(BF16)
            total = 4 * (CHK + CHV)
            flat = l1pad.rearrange("a b -> (a b)")
            per = 128 * 2048
            off = 0
            while off < total:
                n = min(per, total - off)
                S_["gz"].inc(e.dma_start(out=flat[off:off + n].rearrange("(p x) -> p x", p=128),
                                         in_=zb[:, 0:n // 128]), 16)
                off += n
        blk.gpsimd(f_p0_g2)
        KV_PAD = S_["gz"].val
        TP_P00 = S_["tp"].val
        DV_P00 = S_["dv"].val

        def f_p0_v2(e):
            e.wait_ge(S_["gz"].h, KV_PAD)
            for i in range(4):
                e.wait_ge(S_["tp"].h, TP_P00 + i)
                S_["dv"].inc(e.tensor_copy(out=scr[0:1, 0:512].bitcast(BF16),
                                           in_=scr2[32 * i:32 * i + 1, 0:1024]))
        blk.vector(f_p0_v2)
        EV_BC0 = S_["ev"].val

        def f_p0_t(e):
            for i in range(4):
                e.wait_ge(S_["dv"].h, DV_P00 + i + 1)
                if i > 0:
                    e.wait_ge(S_["ev"].h, EV_BC0 + i)
                for half in range(2):
                    ins = e.matmul(psA[:, half * 512:(half + 1) * 512], onesr[0:1, :],
                                   scr[0:1, 0:512].bitcast(BF16)[:, half * 512:(half + 1) * 512],
                                   start=True, stop=True)
                S_["tp"].inc(ins)
        blk.tensor(f_p0_t)

        def f_p0_s(e):
            for i in range(4):
                e.wait_ge(S_["tp"].h, TP_P00 + i + 1)
                if i < 3:
                    S_["ev"].inc(e.copy(out=bcast[:, i, :], in_=psA[:]))
                else:
                    S_["ev"].inc(e.copy(out=scr2[:], in_=psA[:]))
        blk.scalar(f_p0_s)
        EV_BC4 = S_["ev"].val

        def f_p0_v3(e):
            e.wait_ge(S_["ev"].h, EV_BC4)
            S_["dv"].inc(e.tensor_add(bcast[:, 2, :], bcast[:, 2, :], scr2[:]))
        blk.vector(f_p0_v3)
        EV_BC = S_["ev"].val

        if "l0qkv" in phases or "l0attn" in phases:
            _esL0 = _es.enter_context(ExitStack())
            qfm = _esL0.enter_context(nc.sbuf_tensor("qfm", [128, DCH, TOK], BF16))
            with ExitStack() as _es2:
                xsrcL = _es2.enter_context(nc.sbuf_tensor("xsrcL", [128, 2, 1024], F32))
                wk_sb = _es2.enter_context(nc.sbuf_tensor("wk_sb", [128, DCH, D], BF16))
                wv_sb = _es2.enter_context(nc.sbuf_tensor("wv_sb", [128, DCH, D], BF16))
                kvo = _es2.enter_context(nc.sbuf_tensor("kvo", [128, 2, 512], BF16))
                vfm = _es2.enter_context(nc.sbuf_tensor("vfm", [128, DCH, 512], BF16))
                vto = _es2.enter_context(nc.sbuf_tensor("vto", [128, 1, VCOL * H], BF16))
                wq_sb = wk_sb
                def f_wl(e):
                    S_["gw"].inc(e.dma_start(out=wk_sb[:], in_=l0_wq.rearrange("(c p) o -> p c o", p=128)), 16)
                    S_["gw"].inc(e.dma_start(out=wv_sb[:], in_=l0_wv.rearrange("(c p) o -> p c o", p=128)), 16)
                blk.gpsimd(f_wl)
                LDW_QKV = S_["gw"].val

                def f_wait_res(e):
                    e.wait_ge(S_["gx"].h, LDX_RES)
                blk.vector(f_wait_res)
                ln_sweep([("sbuf", x_res[:, t, :]) for t in range(NT)], xfm, 0)
                EV_LN1 = S_["ev"].val
                if "dbg_xfm" in dbg:
                    dx = dram_dbg("dbg_xfm", [128, DCH, TOK], U16)
                    blk.gpsimd(lambda e, v=EV_LN1: (e.wait_ge(S_["ev"].h, v),
                                S_["gst"].inc(e.dma_start(out=dx.bitcast(BF16), in_=xfm[:]), 16)))

                std_proj(wq_sb, lambda ts: xfm[:, :, ts * 512:(ts + 1) * 512],
                         lambda oi, ts: qfm[:, oi, ts * 512:(ts + 1) * 512],
                         list(range(DCH)), list(range(4)), AF.Tanh, BQ,
                         wait_sems=[(S_["ev"], EV_LN1), (S_["gw"], LDW_QKV)])
                MM_Q = S_["mm"].val

                def f_wl2(e):
                    e.wait_ge(S_["mm"].h, MM_Q)
                    S_["gw"].inc(e.dma_start(out=wk_sb[:], in_=l0_wk.rearrange("(c p) o -> p c o", p=128)), 16)
                blk.gpsimd(f_wl2)
                LDW_K = S_["gw"].val

                if "dbg_q" in dbg:
                    dq = dram_dbg("dbg_q", [128, DCH, TOK])
                    ac0 = S_["ac"].val
                    blk.gpsimd(lambda e: S_["gst"].inc(
                        (e.wait_ge(S_["ac"].h, ac0), e.dma_start(out=dq, in_=qfm[:]))[1], 16))

                for g in range(NG):
                    fw = (S_["mm"], MM_Q) if g == 0 else None
                    ln_sweep([("hbm", x_seq[g * 512 + t * 128:g * 512 + (t + 1) * 128, :])
                              for t in range(4)], xfm, 0, first_evac_wait=fw, xsrc=xsrcL)
                    EV_LNG = S_["ev"].val
                    KV_K0 = S_["kv"].val
                    AC_K0 = S_["ac"].val
                    std_proj(wk_sb, lambda ts: xfm[:, :, 0:512],
                             lambda oi, ts: kvo[:, oi % 2, :], list(range(DCH)), [0], AF.Tanh, BK,
                             wait_sems=[(S_["ev"], EV_LNG), (S_["gw"], LDW_K)],
                             slot_free=lambda idx: (S_["kv"], KV_K0 + 16 * (idx - 1)))
                    def f_spk(e, g=g, AC_K0=AC_K0):
                        for oc in range(DCH):
                            e.wait_ge(S_["ac"].h, AC_K0 + oc + 1)
                            S_["kv"].inc(e.dma_start(out=l0k[g, oc], in_=kvo[:, oc % 2, :]), 16)
                    blk.sync(f_spk)
                    std_proj(wv_sb, lambda ts: xfm[:, :, 0:512],
                             lambda oi, ts: vfm[:, oi, :], list(range(DCH)), [0], AF.Tanh, BV,
                             wait_sems=[(S_["kv"], S_["kv"].val)])
                    AC_V = S_["ac"].val
                    tp0, ev0, dv0 = S_["tp"].val, S_["ev"].val, S_["dv"].val

                    def f_tv(e, AC_V=AC_V, tp0=tp0, ev0=ev0):
                        e.wait_ge(S_["ac"].h, AC_V)
                        for tt in range(4):
                            e.wait_ge(S_["ev"].h, ev0 + tt)
                            for c in range(DCH):
                                ins = e.transpose(psT[:, c * 128:(c + 1) * 128],
                                                  vfm[:, c, tt * 128:(tt + 1) * 128], identt[:])
                            S_["tp"].inc(ins)
                    blk.tensor(f_tv)

                    if g == 0:
                        def f_ones(e):
                            S_["dv"].inc(e.memset(
                                vto[:, 0, :].rearrange("p (h x) -> p h x", x=VCOL)[:, :, 64:65], 1.0))
                        blk.vector(f_ones)
                    DV_ONES = S_["dv"].val
                    kv0v = S_["kv"].val
                    EV_V0 = S_["ev"].val

                    def f_sv(e, tp0=tp0, kv0v=kv0v):
                        for tt in range(4):
                            e.wait_ge(S_["tp"].h, tp0 + tt + 1)
                            e.wait_ge(S_["kv"].h, kv0v + 16 * tt)
                            dst = vto[:, 0, :].rearrange("p (h x) -> p h x", x=VCOL)[:, :, 0:64]
                            S_["ev"].inc(e.copy(out=dst, in_=psT[:].rearrange("p (h x) -> p h x", x=64)))
                    blk.scalar(f_sv)

                    def f_spv(e, g=g, EV_V0=EV_V0, DV_ONES=DV_ONES):
                        e.wait_ge(S_["dv"].h, DV_ONES)
                        for tt in range(4):
                            e.wait_ge(S_["ev"].h, EV_V0 + tt + 1)
                            S_["kv"].inc(e.dma_start(out=l0v[g, tt], in_=vto[:, 0, :]), 16)
                    blk.sync(f_spv)

                if "dbg_kv" in dbg:
                    dk = dram_dbg("dbg_k", [NG, DCH, 128, 512], U16)
                    dvv = dram_dbg("dbg_v", [NG, 4, 128, VCOL * H], U16)
                    kv0 = S_["kv"].val
                    blk.gpsimd(lambda e: (e.wait_ge(S_["kv"].h, kv0),
                                          S_["gst"].inc(e.dma_start(out=dk.bitcast(BF16), in_=l0k[:]), 16),
                                          S_["gst"].inc(e.dma_start(out=dvv.bitcast(BF16), in_=l0v[:]), 16)))

        if "l0attn" in phases:
            with ExitStack() as _es2:
                wo_sb = _es2.enter_context(nc.sbuf_tensor("wo_sb", [128, DCH, D], BF16))
                winm = _es2.enter_context(nc.sbuf_tensor("winm", [128, 8, 512], BF16))
                kst = _es2.enter_context(nc.sbuf_tensor("kst", [128, 4, 128], BF16))
                vst = _es2.enter_context(nc.sbuf_tensor("vst", [128, 4, 130], BF16))
                pT = _es2.enter_context(nc.sbuf_tensor("pT", [128, 2, 1024], BF16))
                ctxfm = xfm
                KV_ALL = S_["kv"].val
                MM_QKVDONE = S_["mm"].val
                def f_wm(e):
                    e.wait_ge(S_["kv"].h, KV_ALL)
                    e.wait_ge(S_["mm"].h, MM_QKVDONE)
                    S_["ldw"].inc(e.dma_start(out=winm[:], in_=l0_winmask.rearrange("w p t -> p w t").bitcast(BF16)), 16)
                blk.sync(f_wm)
                LDW_WM = S_["ldw"].val

                for hs in range(8):
                    for qc in range(QC):
                        nj = 8 * qc + 8
                        ldk0, mm0, ac0, dv0 = (S_[k].val for k in ("ldk", "mm", "ac", "dv"))
                        QK = lambda j: mm0 + 1 if j == 0 else mm0 + 2 * j
                        AV = lambda j, nj=nj: mm0 + 2 * j + (3 if j < nj - 1 else 2)

                        def f_dma(e, qc=qc, nj=nj, ldk0=ldk0, mm0=mm0, hs=hs, AV=AV):
                            for j in range(nj):
                                if j >= 4:
                                    e.wait_ge(S_["mm"].h, AV(j - 4))
                                else:
                                    e.wait_ge(S_["mm"].h, mm0)
                                if hs == 0 and qc == 0 and j == 0:
                                    e.wait_ge(S_["kv"].h, KV_ALL)
                                g, tt = j // 4, j % 4
                                S_["ldk"].inc(e.dma_start(out=kst[:, j % 4, :],
                                                          in_=l0k[g, hs, :, tt * 128:(tt + 1) * 128]), 16)
                                S_["ldk"].inc(e.dma_start(out=vst[:, j % 4, :],
                                                          in_=l0v[g, tt, :, 130 * hs:130 * hs + 130]), 16)
                        blk.sync(f_dma)

                        def f_pe(e, qc=qc, nj=nj, ldk0=ldk0, mm0=mm0, ac0=ac0, dv0=dv0, hs=hs):
                            if hs == 0 and qc == 0:
                                e.wait_ge(S_["ldw"].h, LDW_WM)

                            def emit_qk(j):
                                strip = (psA, psB)[j % 2]
                                e.wait_ge(S_["ldk"].h, ldk0 + 32 * j + 16)
                                if j >= 2:
                                    e.wait_ge(S_["ac"].h, ac0 + j - 1)
                                elif not (hs == 0 and qc == 0):
                                    e.wait_ge(S_["ac"].h, ac0 - 2)
                                for hp in range(2):
                                    ins = e.matmul(strip[:, hp * 512:(hp + 1) * 512],
                                                   kst[hp * 64:(hp + 1) * 64, j % 4, :],
                                                   qfm[hp * 64:(hp + 1) * 64, hs, qc * 512:(qc + 1) * 512],
                                                   start=True, stop=True)
                                S_["mm"].inc(ins)

                            def emit_av(j):
                                e.wait_ge(S_["ldk"].h, ldk0 + 32 * j + 32)
                                e.wait_ge(S_["ac"].h, ac0 + j + 1)
                                if j == 0:
                                    e.wait_ge(S_["dv"].h, dv0)
                                if j >= 8 * qc:
                                    e.wait_ge(S_["dv"].h, dv0 + 2 * (j - 8 * qc) + 2)
                                for hp in range(2):
                                    avp = psC[0:65, hp * 512:(hp + 1) * 512]
                                    ins = e.matmul(avp, vst[:, j % 4, hp * 65:(hp + 1) * 65],
                                                   pT[:, j % 2, hp * 512:(hp + 1) * 512],
                                                   start=(j == 0), stop=(j == nj - 1))
                                S_["mm"].inc(ins)

                            emit_qk(0)
                            for j in range(1, nj):
                                emit_qk(j)
                                emit_av(j - 1)
                            emit_av(nj - 1)
                        blk.tensor(f_pe)

                        def f_exp(e, qc=qc, nj=nj, mm0=mm0, dv0=dv0, ac0=ac0):
                            for j in range(nj):
                                strip = (psA, psB)[j % 2]
                                e.wait_ge(S_["mm"].h, QK(j))
                                S_["ac"].inc(e.activation(out=pT[:, j % 2, :], in_=strip[:],
                                                          func=AF.Exp, scale=0.125))
                        blk.scalar(f_exp)

                        def f_mask(e, qc=qc, nj=nj, ac0=ac0, dv0=dv0):
                            for j in range(8 * qc, nj):
                                w = j - 8 * qc
                                e.wait_ge(S_["ac"].h, ac0 + j + 1)
                                for hp in range(2):
                                    S_["dv"].inc(e.tensor_mul(pT[:, j % 2, hp * 512:(hp + 1) * 512],
                                                              pT[:, j % 2, hp * 512:(hp + 1) * 512],
                                                              winm[:, w, :]))
                        blk.vector(f_mask)

                        MM_AVE = S_["mm"].val
                        dv1 = S_["dv"].val

                        def f_nv(e, qc=qc, hs=hs, MM_AVE=MM_AVE, dv1=dv1):
                            e.wait_ge(S_["mm"].h, MM_AVE)
                            S_["dv"].inc(e.reciprocal(rrow[0:1, 0:512], psC[64:65, 0:512]))
                            S_["dv"].inc(e.reciprocal(rrow[0:1, 512:1024], psC[64:65, 512:1024]))
                        blk.vector(f_nv)
                        DV_R = S_["dv"].val
                        mm1 = S_["mm"].val

                        def f_nt(e, DV_R=DV_R):
                            e.wait_ge(S_["dv"].h, DV_R)
                            rbdst = (psD[0:64, :], psD[64:128, :])
                            for hp in range(2):
                                S_["mm"].inc(e.matmul(rbdst[hp], onesf[0:1, :],
                                                      rrow[0:1, 512 * hp:512 * hp + 512], start=True, stop=True))
                        blk.tensor(f_nt)
                        AC_RB = S_["ac"].val

                        def f_nrb(e, mm1=mm1):
                            rbdst = (psD[0:64, :], psD[64:128, :])
                            for hp in range(2):
                                e.wait_ge(S_["mm"].h, mm1 + hp + 1)
                                S_["ac"].inc(e.copy(out=scr2[0:64, 512 * hp:512 * hp + 512], in_=rbdst[hp]))
                        blk.scalar(f_nrb)

                        def f_nv2(e, qc=qc, hs=hs, AC_RB=AC_RB):
                            for hp in range(2):
                                e.wait_ge(S_["ac"].h, AC_RB + hp + 1)
                                S_["dv"].inc(e.tensor_mul(
                                    ctxfm[hp * 64:(hp + 1) * 64, hs, qc * 512:(qc + 1) * 512],
                                    psC[0:64, hp * 512:(hp + 1) * 512],
                                    scr2[0:64, 512 * hp:512 * hp + 512]))
                        blk.vector(f_nv2)
                if "dbg_ctx" in dbg:
                    dc = dram_dbg("dbg_ctx", [128, DCH, TOK], U16)
                    dv0 = S_["dv"].val
                    blk.gpsimd(lambda e: (e.wait_ge(S_["dv"].h, dv0),
                                          S_["gst"].inc(e.dma_start(out=dc.bitcast(BF16), in_=ctxfm[:]), 16)))

                if "dbg_winm" in dbg:
                    dwm = dram_dbg("dbg_winm", [128, 8, 512], U16)
                    _ldwv = S_["ldw"].val
                    blk.gpsimd(lambda e: (e.wait_ge(S_["ldw"].h, _ldwv),
                                          S_["gst"].inc(e.dma_start(out=dwm.bitcast(BF16), in_=winm[:]), 16)))
                DV_CTX = S_["dv"].val
                MM_WO = S_["mm"].val
                def f_wo(e):
                    e.wait_ge(S_["mm"].h, MM_WO)
                    S_["gw"].inc(e.dma_start(out=wo_sb[:], in_=l0_wo.rearrange("(c p) o -> p c o", p=128)), 16)
                blk.gpsimd(f_wo)
                swap_proj_residual(ctxfm, wo_sb, DCH, 0, AF.Tanh,
                                   wait_sems=[(S_["dv"], DV_CTX), (S_["gw"], S_["gw"].val),
                                              (S_["ev"], EV_BC)])

        if "l0qkv" in phases or "l0attn" in phases:
            _esL0.close()

        if "moe" in phases:
            with ExitStack() as _es2:
                wgh = _es2.enter_context(nc.sbuf_tensor("wgh", [128, 2, DCH, 128], BF16))
                wuh = _es2.enter_context(nc.sbuf_tensor("wuh", [128, 2, DCH, 128], BF16))
                wde = _es2.enter_context(nc.sbuf_tensor("wde", [128, 16, D], BF16))
                hidb = _es2.enter_context(nc.sbuf_tensor("hidb", [128, 16, 512], BF16))
                sc8 = _es2.enter_context(nc.sbuf_tensor("sc8", [128, NT, 8], F32))
                scrE = _es2.enter_context(nc.sbuf_tensor("scrE", [128, NT, 8], F32))
                scrF = _es2.enter_context(nc.sbuf_tensor("scrF", [128, NT, 8], F32))
                stats2 = _es2.enter_context(nc.sbuf_tensor("stats2", [128, NT, 8], F32))
                wpe = _es2.enter_context(nc.sbuf_tensor("wpe", [128, NT, 9, 8], F32))
                gwsb = _es2.enter_context(nc.sbuf_tensor("gwsb", [128, DCH, 8], BF16))
                gscb = scr2[:].bitcast(BF16)
                gsil = scr[:].bitcast(BF16).rearrange("p (a t) -> p a t", a=4)

                MM_PRE = S_["mm"].val
                ln_sweep([("sbuf", x_res[:, t, :]) for t in range(NT)], xfm, 0,
                         first_evac_wait=(S_["mm"], MM_PRE))
                EV_LN2 = S_["ev"].val

                def f_gwl(e):
                    e.wait_ge(S_["mm"].h, MM_PRE)
                    S_["gw"].inc(e.dma_start(out=gwsb[:], in_=moe_gw.rearrange("(c p) o -> p c o", p=128)), 16)
                blk.gpsimd(f_gwl)
                GW_GATE = S_["gw"].val

                DV_Z0 = S_["dv"].val
                blk.vector(lambda e: S_["dv"].inc(e.memset(gscb[:], 0.0)))
                DV_Z = S_["dv"].val
                mmg0 = S_["mm"].val

                def f_gt(e):
                    e.wait_ge(S_["ev"].h, EV_LN2)
                    e.wait_ge(S_["gw"].h, GW_GATE)
                    for ts in range(4):
                        if ts >= 1:
                            e.wait_ge(S_["ac"].h, acg0 + ts)
                        for c in range(DCH):
                            ins = e.matmul(psD[0:8, :], gwsb[:, c, :], xfm[:, c, ts * 512:(ts + 1) * 512],
                                           start=(c == 0), stop=(c == DCH - 1))
                        S_["mm"].inc(ins)
                acg0 = S_["ac"].val
                blk.tensor(f_gt)

                def f_gs(e):
                    for ts in range(4):
                        e.wait_ge(S_["mm"].h, mmg0 + ts + 1)
                        e.wait_ge(S_["dv"].h, DV_Z)
                        S_["ac"].inc(e.copy(out=gscb[0:8, ts * 512:(ts + 1) * 512], in_=psD[0:8, :]))
                blk.scalar(f_gs)
                ACG_E = S_["ac"].val

                tpg0, evg0 = S_["tp"].val, S_["ev"].val

                def f_gt2(e):
                    e.wait_ge(S_["ac"].h, ACG_E)
                    for rnd in range(2):
                        if rnd == 1:
                            e.wait_ge(S_["ev"].h, evg0 + 1)
                        for c in range(8):
                            ins = e.transpose(psT[:, c * 128:(c + 1) * 128],
                                              gscb[:, rnd * 1024 + c * 128: rnd * 1024 + (c + 1) * 128],
                                              identt[:])
                        S_["tp"].inc(ins)
                blk.tensor(f_gt2)

                def f_gs2(e):
                    for rnd in range(2):
                        e.wait_ge(S_["tp"].h, tpg0 + rnd + 1)
                        S_["ev"].inc(e.copy(out=sc8[:, rnd * 8:(rnd + 1) * 8, :],
                                            in_=psT[:].rearrange("p (c x) -> p c x", c=8)[:, :, 0:8]))
                blk.scalar(f_gs2)
                EV_SC8 = S_["ev"].val

                dvt0, act0 = S_["dv"].val, S_["ac"].val

                def f_top_v(e):
                    e.wait_ge(S_["ev"].h, EV_SC8)
                    SS = lambda tt, k: stats[:, tt, k:k + 1]
                    for tt in range(NT):
                        e.reduce_max(SS(tt, 1), sc8[:, tt, :], axis=AX.X)
                    for tt in range(NT):
                        e.tensor_single_scalar(scrE[:, tt, :], sc8[:, tt, :], SS(tt, 1), ALU.is_equal)
                    for tt in range(NT):
                        e.scalar_tensor_tensor(scrF[:, tt, :], scrE[:, tt, :], -1e9, sc8[:, tt, :],
                                               ALU.mult, ALU.add)
                    for tt in range(NT):
                        e.reduce_max(SS(tt, 2), scrF[:, tt, :], axis=AX.X)
                    for tt in range(NT):
                        S_["dv"].inc(e.tensor_sub(SS(tt, 0), SS(tt, 2), SS(tt, 1)))
                    e.wait_ge(S_["ac"].h, act0 + NT)
                    for tt in range(NT):
                        e.tensor_scalar_add(SS(tt, 3), stats2[:, tt, 0:1], 1.0)
                    for tt in range(NT):
                        e.reciprocal(SS(tt, 4), SS(tt, 3))
                    for tt in range(NT):
                        e.tensor_mul(SS(tt, 5), stats2[:, tt, 0:1], SS(tt, 4))
                    for tt in range(NT):
                        e.tensor_sub(SS(tt, 6), SS(tt, 4), SS(tt, 5))
                    for tt in range(NT):
                        e.tensor_single_scalar(scrF[:, tt, :], sc8[:, tt, :], SS(tt, 2), ALU.is_ge)
                    for tt in range(NT):
                        e.tensor_single_scalar(scrF[:, tt, :], scrF[:, tt, :], SS(tt, 5), ALU.mult)
                    for tt in range(NT):
                        ins = e.scalar_tensor_tensor(
                            wpe[:, tt, 0:8, 0:1].squeeze(), scrE[:, tt, :], SS(tt, 6), scrF[:, tt, :],
                            ALU.mult, ALU.add)
                    S_["dv"].inc(ins, 1)
                blk.vector(f_top_v)

                def f_top_s(e):
                    e.wait_ge(S_["dv"].h, dvt0 + NT)
                    for tt in range(NT):
                        S_["ac"].inc(e.activation(out=stats2[:, tt, 0:1], in_=stats[:, tt, 0:1], func=AF.Exp))
                blk.scalar(f_top_s)
                DV_TOP = S_["dv"].val

                def f_ones2(e):
                    e.wait_ge(S_["dv"].h, DV_TOP)
                    S_["dv"].inc(e.memset(wpe[:, :, 8, 0:1], 1.0))
                blk.vector(f_ones2)
                DV_TOP2 = S_["dv"].val

                MM_LD = [S_["mm"].val]
                for e_i in range(MOE_E + 1):
                    shared = e_i == MOE_E
                    wsrc_g = moe_wsg if shared else moe_wg[e_i]
                    wsrc_u = moe_wsu if shared else moe_wu[e_i]
                    wsrc_d = moe_wsd if shared else moe_wd[e_i]
                    mm0 = S_["mm"].val
                    ac0 = S_["ac"].val
                    dv0 = S_["dv"].val
                    gw0 = S_["gw"].val
                    gslot = [psA[:, 0:512], psA[:, 512:1024]]
                    uslot = [psB[:, 0:512], psB[:, 512:1024]]
                    mm_g = lambda ts, hc, mm0=mm0: mm0 + 36 * ts + 2 * hc + 1
                    mm_u = lambda ts, hc, mm0=mm0: mm0 + 36 * ts + 2 * hc + 2
                    mm_d = lambda ts, tb, mm0=mm0: mm0 + 36 * ts + 33 + tb
                    ac_s = lambda ts, hc, ac0=ac0: ac0 + 16 * ts + hc + 1
                    dv_m = lambda ts, hc, dv0=dv0: dv0 + 20 * ts + hc + 1
                    dv_r = lambda ts, tb, dv0=dv0: dv0 + 20 * ts + 17 + tb
                    def gw_g(ts, hc, gw0=gw0):
                        return gw0 + 512 * ts + 32 * hc + 16 + (16 if ts >= 1 else 0)
                    def gw_u(ts, hc, gw0=gw0):
                        return gw_g(ts, hc) + 16
                    GW_WDE = gw0 + 32 * 16 + 16

                    def f_wl(e, wsrc_g=wsrc_g, wsrc_u=wsrc_u, wsrc_d=wsrc_d,
                             mm_g=mm_g, mm_u=mm_u, e_i=e_i, MM_LDP=MM_LD[0]):
                        if e_i == 0:
                            e.wait_ge(S_["mm"].h, MM_PRE)
                        for ts in range(4):
                            for hc in range(16):
                                if ts > 0 or hc >= 2:
                                    pts, phc = (ts, hc - 2) if hc >= 2 else (ts - 1, 14 + hc)
                                    e.wait_ge(S_["mm"].h, mm_u(pts, phc))
                                S_["gw"].inc(e.dma_start(
                                    out=wgh[:, hc % 2, :, :],
                                    in_=wsrc_g[:, hc * 128:(hc + 1) * 128].rearrange("(c p) o -> p c o", p=128)), 16)
                                S_["gw"].inc(e.dma_start(
                                    out=wuh[:, hc % 2, :, :],
                                    in_=wsrc_u[:, hc * 128:(hc + 1) * 128].rearrange("(c p) o -> p c o", p=128)), 16)
                            if ts == 0:
                                e.wait_ge(S_["mm"].h, MM_LDP)
                                S_["gw"].inc(e.dma_start(
                                    out=wde[:], in_=wsrc_d.rearrange("(hc p) o -> p hc o", p=128)), 16)
                    blk.gpsimd(f_wl)

                    def f_et(e, mm0=mm0, ac0=ac0, dv0=dv0, e_i=e_i,
                             mm_g=mm_g, mm_u=mm_u, mm_d=mm_d, ac_s=ac_s, dv_m=dv_m, dv_r=dv_r,
                             gw_g=gw_g, gw_u=gw_u, GW_WDE=GW_WDE):
                        e.wait_ge(S_["dv"].h, dv0 if e_i > 0 else DV_TOP2)
                        for ts in range(4):
                            for hc in range(16):
                                e.wait_ge(S_["gw"].h, gw_g(ts, hc))
                                if ts > 0 or hc >= 2:
                                    pts, phc = (ts, hc - 2) if hc >= 2 else (ts - 1, 14 + hc)
                                    e.wait_ge(S_["ac"].h, ac_s(pts, phc))
                                for c in range(DCH):
                                    ins = e.matmul(gslot[hc % 2], wgh[:, hc % 2, c, :],
                                                   xfm[:, c, ts * 512:(ts + 1) * 512],
                                                   start=(c == 0), stop=(c == DCH - 1))
                                S_["mm"].inc(ins)
                                e.wait_ge(S_["gw"].h, gw_u(ts, hc))
                                if ts > 0 or hc >= 2:
                                    pts, phc = (ts, hc - 2) if hc >= 2 else (ts - 1, 14 + hc)
                                    e.wait_ge(S_["dv"].h, dv_m(pts, phc))
                                for c in range(DCH):
                                    ins = e.matmul(uslot[hc % 2], wuh[:, hc % 2, c, :],
                                                   xfm[:, c, ts * 512:(ts + 1) * 512],
                                                   start=(c == 0), stop=(c == DCH - 1))
                                S_["mm"].inc(ins)
                            if ts == 0:
                                e.wait_ge(S_["gw"].h, GW_WDE)
                            e.wait_ge(S_["dv"].h, dv_m(ts, 15))
                            for tb in range(4):
                                if ts > 0 or tb > 0:
                                    pts, ptb = (ts, tb - 1) if tb > 0 else (ts - 1, 3)
                                    e.wait_ge(S_["dv"].h, dv_r(pts, ptb))
                                for hc in range(16):
                                    e.matmul(psC[:, 0:512], hidb[:, hc, tb * 128:(tb + 1) * 128],
                                             wde[:, hc, 0:512], start=(hc == 0), stop=(hc == 15))
                                    ins = e.matmul(psC[:, 512:1024], hidb[:, hc, tb * 128:(tb + 1) * 128],
                                                   wde[:, hc, 512:1024], start=(hc == 0), stop=(hc == 15))
                                S_["mm"].inc(ins)
                    blk.tensor(f_et)

                    def f_es(e, mm_g=mm_g, dv_m=dv_m):
                        for ts in range(4):
                            for hc in range(16):
                                e.wait_ge(S_["mm"].h, mm_g(ts, hc))
                                if ts > 0 or hc >= 2:
                                    pts, phc = (ts, hc - 2) if hc >= 2 else (ts - 1, 14 + hc)
                                    e.wait_ge(S_["dv"].h, dv_m(pts, phc))
                                S_["ac"].inc(e.activation(out=gsil[:, hc % 2, :], in_=gslot[hc % 2], func=AF.Silu))
                    blk.scalar(f_es)

                    def f_ev(e, mm_u=mm_u, mm_d=mm_d, ac_s=ac_s, e_i=e_i, shared=shared):
                        for ts in range(4):
                            for hc in range(16):
                                e.wait_ge(S_["mm"].h, mm_u(ts, hc))
                                e.wait_ge(S_["ac"].h, ac_s(ts, hc))
                                S_["dv"].inc(e.tensor_mul(hidb[:, hc, :], uslot[hc % 2], gsil[:, hc % 2, :]))
                            for tb in range(4):
                                tile = ts * 4 + tb
                                e.wait_ge(S_["mm"].h, mm_d(ts, tb))
                                wcol = wpe[:, tile, 8 if shared else e_i, 0:1]
                                S_["dv"].inc(e.scalar_tensor_tensor(
                                    x_res[:, tile, :], psC[:], wcol, x_res[:, tile, :],
                                    ALU.mult, ALU.add))
                    blk.vector(f_ev)
                    MM_LD[0] = S_["mm"].val

                if "dbg_wpe" in dbg:
                    dw = dram_dbg("dbg_wpe", [128, NT, 9, 8])
                    dv0 = S_["dv"].val
                    blk.gpsimd(lambda e: (e.wait_ge(S_["dv"].h, dv0),
                                          S_["gst"].inc(e.dma_start(out=dw, in_=wpe[:]), 16)))

        if "l1qkv" in phases:
            with ExitStack() as _es2:
                qfm1g = _es.enter_context(ExitStack())
                qfm1 = qfm1g.enter_context(nc.sbuf_tensor("qfm1", [128, DCH, TOK], BF16))
                w3 = _es2.enter_context(nc.sbuf_tensor("w3", [128, DCH, D], BF16))
                kvo1 = _es2.enter_context(nc.sbuf_tensor("kvo1", [128, 2, 512], BF16))
                vfm1 = _es2.enter_context(nc.sbuf_tensor("vfm1", [128, DCH, 512], BF16))
                vto1 = _es2.enter_context(nc.sbuf_tensor("vto1", [128, 1, VCOL * H], BF16))

                MM_PRE1 = S_["mm"].val
                DV_PRE1 = S_["dv"].val
                ln_sweep([("sbuf", x_res[:, t, :]) for t in range(NT)], xfm, 0,
                         first_evac_wait=(S_["mm"], MM_PRE1))
                EV_LN3 = S_["ev"].val

                def f_b1(e):
                    e.wait_ge(S_["mm"].h, MM_PRE1)
                    e.wait_ge(S_["dv"].h, DV_PRE1)
                    for third in range(3):
                        S_["ldw"].inc(e.dma_start(
                            out=bias1[:, BQK + 8 * third:BQK + 8 * third + 8, 0:1],
                            in_=l1_bqkv[third * D:(third + 1) * D].rearrange("(c p) -> p c", p=128).unsqueeze(2),
                            allow_slow_non_contiguous=True), 16)
                    S_["ldw"].inc(e.dma_start(
                        out=bias1[:, BG1:BG1 + 8, 0:1],
                        in_=l1_bgate.rearrange("(c p) -> p c", p=128).unsqueeze(2),
                        allow_slow_non_contiguous=True), 16)
                blk.sync(f_b1)
                LDW_B1 = S_["ldw"].val
                DV_QB = S_["dv"].val

                def f_qb(e):
                    e.wait_ge(S_["ldw"].h, LDW_B1)
                    S_["dv"].inc(e.tensor_scalar_mul(bias1[:, BQK:BQK + 8, 0:1], bias1[:, BQK:BQK + 8, 0:1], 0.125))
                blk.vector(f_qb)
                DV_QB2 = S_["dv"].val

                def f_w3q(e):
                    e.wait_ge(S_["mm"].h, MM_PRE1)
                    e.wait_ge(S_["dv"].h, DV_PRE1)
                    S_["gw"].inc(e.dma_start(out=w3[:], in_=l1_wqkvT.rearrange("(c p) o -> p c o", p=128)[:, :, 0:D]), 16)
                blk.gpsimd(f_w3q)
                GW_Q1 = S_["gw"].val

                def f_qwait(e):
                    e.wait_ge(S_["dv"].h, DV_QB2)
                blk.scalar(f_qwait)
                std_proj(w3, lambda ts: xfm[:, :, ts * 512:(ts + 1) * 512],
                         lambda oi, ts: qfm1[:, oi, ts * 512:(ts + 1) * 512],
                         list(range(DCH)), list(range(4)), AF.Identity, BQK, scale=0.125,
                         wait_sems=[(S_["ev"], EV_LN3), (S_["gw"], GW_Q1)])
                MM_Q1 = S_["mm"].val

                def f_w3k(e):
                    e.wait_ge(S_["mm"].h, MM_Q1)
                    S_["gw"].inc(e.dma_start(out=w3[:], in_=l1_wqkvT.rearrange("(c p) o -> p c o", p=128)[:, :, D:2 * D]), 16)
                blk.gpsimd(f_w3k)
                GW_K1 = S_["gw"].val

                MM_KDONE = S_["mm"].val
                for qc in range(QC):
                    KV_K0 = S_["kv"].val
                    AC_K0 = S_["ac"].val
                    std_proj(w3, lambda ts, qc=qc: xfm[:, :, qc * 512:(qc + 1) * 512],
                             lambda oi, ts: kvo1[:, oi % 2, :], list(range(DCH)), [0], AF.Identity, BQK + 8,
                             wait_sems=[(S_["gw"], GW_K1)],
                             slot_free=lambda idx, KV_K0=KV_K0: (S_["kv"], KV_K0 + 16 * (idx - 1)))

                    def f_spk1(e, qc=qc, AC_K0=AC_K0):
                        for oc in range(DCH):
                            e.wait_ge(S_["ac"].h, AC_K0 + oc + 1)
                            dst = cc_in[qc, 0:CHK].rearrange("(c p t) -> c p t", c=DCH, p=128)[oc]
                            S_["kv"].inc(e.dma_start(out=dst, in_=kvo1[:, oc % 2, :]), 16)
                    blk.sync(f_spk1)
                MM_KALL = S_["mm"].val

                def f_w3v(e):
                    e.wait_ge(S_["mm"].h, MM_KALL)
                    S_["gw"].inc(e.dma_start(out=w3[:], in_=l1_wqkvT.rearrange("(c p) o -> p c o", p=128)[:, :, 2 * D:3 * D]), 16)
                blk.gpsimd(f_w3v)
                GW_V1 = S_["gw"].val

                first_v = [True]
                for qc in range(QC):
                    std_proj(w3, lambda ts, qc=qc: xfm[:, :, qc * 512:(qc + 1) * 512],
                             lambda oi, ts: vfm1[:, oi, :], list(range(DCH)), [0], AF.Identity, BQK + 16,
                             wait_sems=[(S_["gw"], GW_V1), (S_["kv"], S_["kv"].val)])
                    AC_V1 = S_["ac"].val
                    tp0, ev0 = S_["tp"].val, S_["ev"].val

                    def f_tv1(e, AC_V1=AC_V1, tp0=tp0, ev0=ev0):
                        e.wait_ge(S_["ac"].h, AC_V1)
                        for tt in range(4):
                            e.wait_ge(S_["ev"].h, ev0 + tt)
                            for c in range(DCH):
                                ins = e.transpose(psT[:, c * 128:(c + 1) * 128],
                                                  vfm1[:, c, tt * 128:(tt + 1) * 128], identt[:])
                            S_["tp"].inc(ins)
                    blk.tensor(f_tv1)
                    if first_v[0]:
                        def f_ones1(e):
                            S_["dv"].inc(e.memset(
                                vto1[:, 0, :].rearrange("p (h x) -> p h x", x=VCOL)[:, :, 64:65], 1.0))
                        blk.vector(f_ones1)
                        first_v[0] = False
                    DV_ONES1 = S_["dv"].val
                    kv0v = S_["kv"].val
                    EV_V0 = S_["ev"].val

                    def f_sv1(e, tp0=tp0, kv0v=kv0v):
                        for tt in range(4):
                            e.wait_ge(S_["tp"].h, tp0 + tt + 1)
                            e.wait_ge(S_["kv"].h, kv0v + 16 * tt)
                            dst = vto1[:, 0, :].rearrange("p (h x) -> p h x", x=VCOL)[:, :, 0:64]
                            S_["ev"].inc(e.copy(out=dst, in_=psT[:].rearrange("p (h x) -> p h x", x=64)))
                    blk.scalar(f_sv1)

                    def f_spv1(e, qc=qc, EV_V0=EV_V0, DV_ONES1=DV_ONES1):
                        e.wait_ge(S_["dv"].h, DV_ONES1)
                        for tt in range(4):
                            e.wait_ge(S_["ev"].h, EV_V0 + tt + 1)
                            dst = cc_in[qc, CHK:CHK + CHV].rearrange("(f p t) -> f p t", f=4, p=128)[tt]
                            S_["kv"].inc(e.dma_start(out=dst, in_=vto1[:, 0, :]), 16)
                    blk.sync(f_spv1)

                if "dbg_q1" in dbg:
                    dq1 = dram_dbg("dbg_q1", [128, DCH, TOK], U16)
                    _mmq = S_["mm"].val
                    _acq = S_["ac"].val
                    blk.gpsimd(lambda e: (e.wait_ge(S_["ac"].h, _acq),
                                          S_["gst"].inc(e.dma_start(out=dq1.bitcast(BF16), in_=qfm1[:]), 16)))
                MM_G1 = S_["mm"].val

                def f_wg1(e):
                    e.wait_ge(S_["mm"].h, MM_G1)
                    S_["gw"].inc(e.dma_start(out=w3[:], in_=l1_wgateT.rearrange("(c p) o -> p c o", p=128)), 16)
                blk.gpsimd(f_wg1)
                GW_G1 = S_["gw"].val
                for ts in range(4):
                    KV_G0 = S_["kv"].val
                    AC_G0 = S_["ac"].val
                    std_proj(w3, lambda _ts, ts=ts: xfm[:, :, ts * 512:(ts + 1) * 512],
                             lambda oi, _ts: kvo1[:, oi % 2, :], list(range(DCH)), [0], AF.Sigmoid, BG1,
                             wait_sems=[(S_["gw"], GW_G1)],
                             slot_free=lambda idx, KV_G0=KV_G0: (S_["kv"], KV_G0 + 16 * (idx - 1)))

                    def f_spg(e, ts=ts, AC_G0=AC_G0):
                        for oc in range(DCH):
                            e.wait_ge(S_["ac"].h, AC_G0 + oc + 1)
                            S_["kv"].inc(e.dma_start(out=l1g[oc, :, ts * 512:(ts + 1) * 512],
                                                     in_=kvo1[:, oc % 2, :]), 16)
                    blk.sync(f_spg)

                if "dbg_cc" in dbg:
                    dcc = dram_dbg("dbg_cc", [QC, 2048], U16)
                    _kvc = S_["kv"].val
                    blk.gpsimd(lambda e: (e.wait_ge(S_["kv"].h, _kvc),
                                          S_["gst"].inc(e.dma_start(out=dcc.bitcast(BF16),
                                                                    in_=cc_in[:, 0:2048]), 16)))
                KV_CC = S_["kv"].val

                def f_ag(e):
                    e.wait_ge(S_["kv"].h, KV_CC)
                    e.wait_ge(S_["gz"].h, KV_PAD)
                    S_["cc"].inc(e.collective_compute(
                        "AllGather", ALU.bypass,
                        ins=[cc_in[:]], outs=[cc_out.rearrange("a b c -> (a b) c")],
                        replica_groups=[[0, 1], [2, 3], [4, 5], [6, 7]]), 1)
                blk.gpsimd(f_ag)
                CC_DONE = S_["cc"].val

        if "l1attn" in phases:
            with ExitStack() as _es2:
                kst1 = _es2.enter_context(nc.sbuf_tensor("kst1", [128, 4, 128], BF16))
                vst1 = _es2.enter_context(nc.sbuf_tensor("vst1", [128, 4, 130], BF16))
                btst = _es2.enter_context(nc.sbuf_tensor("btst", [128, 3, 2, 512], BF16))
                pT1 = _es2.enter_context(nc.sbuf_tensor("pT1", [128, 2, 1024], BF16))
                sbcol = _es2.enter_context(nc.sbuf_tensor("sbcol", [128, QC, L1_NJ, 8], F32))
                wog = _es2.enter_context(nc.sbuf_tensor("wog", [128, DCH, D], BF16))
                ctx1 = xfm

                MM_A1 = S_["mm"].val

                def f_sb(e):
                    e.wait_ge(S_["mm"].h, MM_A1)
                    e.wait_ge(S_["cc"].h, CC_DONE)
                    S_["ldw"].inc(e.dma_start(out=sbcol[:], in_=l1_sb[:]), 16)
                blk.sync(f_sb)
                LDW_SB = S_["ldw"].val

                for hs in range(8):
                    for qc in range(QC):
                        nj = L1_NJ
                        ldk0, mm0, ac0, dv0 = (S_[k].val for k in ("ldk", "mm", "ac", "dv"))
                        QK1 = lambda j, mm0=mm0: mm0 + 1 if j == 0 else mm0 + 2 * j
                        AV1 = lambda j, mm0=mm0, nj=nj: mm0 + 2 * j + (3 if j < nj - 1 else 2)

                        gw0b = S_["gw"].val

                        def f_dma1(e, qc=qc, nj=nj, mm0=mm0, hs=hs, AV1=AV1):
                            for j in range(nj):
                                if j >= 4:
                                    e.wait_ge(S_["mm"].h, AV1(j - 4))
                                else:
                                    e.wait_ge(S_["mm"].h, mm0)
                                b13 = 8 * qc + j - 13
                                g, tt = b13 // 4, b13 % 4
                                if g < 0:
                                    ksrc = l1pad[(g % 4), 0:CHK].rearrange("(c p t) -> c p t", c=DCH, p=128)
                                    vsrc = l1pad[(g % 4), CHK:CHK + CHV].rearrange("(f p t) -> f p t", f=4, p=128)
                                else:
                                    ksrc = cc_out[g % 2, g // 2, 0:CHK].rearrange("(c p t) -> c p t", c=DCH, p=128)
                                    vsrc = cc_out[g % 2, g // 2, CHK:CHK + CHV].rearrange("(f p t) -> f p t", f=4, p=128)
                                S_["ldk"].inc(e.dma_start(out=kst1[:, j % 4, :],
                                                          in_=ksrc[hs, :, tt * 128:(tt + 1) * 128]), 16)
                                S_["ldk"].inc(e.dma_start(out=vst1[:, j % 4, :],
                                                          in_=vsrc[tt, :, 130 * hs:130 * hs + 130]), 16)
                        blk.sync(f_dma1)

                        def f_bdma1(e, qc=qc, nj=nj, mm0=mm0, hs=hs, AV1=AV1):
                            for j in range(nj):
                                if j >= 3:
                                    e.wait_ge(S_["mm"].h, AV1(j - 3))
                                else:
                                    e.wait_ge(S_["mm"].h, mm0)
                                S_["gw"].inc(e.dma_start(out=btst[:, j % 3, :, :],
                                                          in_=l1_btile[j, 2 * hs:2 * hs + 2].rearrange("h p t -> p h t").bitcast(BF16)), 16)
                        blk.gpsimd(f_bdma1)

                        def f_pe1(e, qc=qc, nj=nj, ldk0=ldk0, mm0=mm0, ac0=ac0, dv0=dv0, hs=hs):
                            if hs == 0 and qc == 0:
                                e.wait_ge(S_["ldw"].h, LDW_SB)

                            def emit_qk(j):
                                strip = (psA, psB)[j % 2]
                                e.wait_ge(S_["ldk"].h, ldk0 + 32 * j + 32)
                                e.wait_ge(S_["gw"].h, gw0b + 16 * (j + 1))
                                if j >= 2:
                                    e.wait_ge(S_["ac"].h, ac0 + j - 1)
                                elif not (hs == 0 and qc == 0):
                                    e.wait_ge(S_["ac"].h, ac0 - 2)
                                for hp in range(2):
                                    e.matmul(strip[:, hp * 512:(hp + 1) * 512],
                                             kst1[hp * 64:(hp + 1) * 64, j % 4, :],
                                             qfm1[hp * 64:(hp + 1) * 64, hs, qc * 512:(qc + 1) * 512],
                                             start=True, stop=False)
                                    ins = e.matmul(strip[:, hp * 512:(hp + 1) * 512],
                                                   identt[:], btst[:, j % 3, hp, :],
                                                   start=False, stop=True)
                                S_["mm"].inc(ins)

                            def emit_av(j):
                                e.wait_ge(S_["ac"].h, ac0 + j + 1)
                                if j == 0:
                                    e.wait_ge(S_["dv"].h, dv0)
                                for hp in range(2):
                                    avp = psC[0:65, hp * 512:(hp + 1) * 512]
                                    ins = e.matmul(avp, vst1[:, j % 4, hp * 65:(hp + 1) * 65],
                                                   pT1[:, j % 2, hp * 512:(hp + 1) * 512],
                                                   start=(j == 0), stop=(j == nj - 1))
                                S_["mm"].inc(ins)

                            emit_qk(0)
                            for j in range(1, nj):
                                emit_qk(j)
                                emit_av(j - 1)
                            emit_av(nj - 1)
                        blk.tensor(f_pe1)

                        def f_exp1(e, qc=qc, nj=nj, mm0=mm0, QK1=QK1, AV1=AV1):
                            for j in range(nj):
                                strip = (psA, psB)[j % 2]
                                e.wait_ge(S_["mm"].h, QK1(j))
                                S_["ac"].inc(e.activation(out=pT1[:, j % 2, :], in_=strip[:],
                                                          func=AF.Exp, bias=sbcol[:, qc, j, 0:1]))
                        blk.scalar(f_exp1)

                        MM_AVE1 = S_["mm"].val

                        def f_nv1(e, MM_AVE1=MM_AVE1):
                            e.wait_ge(S_["mm"].h, MM_AVE1)
                            S_["dv"].inc(e.reciprocal(rrow[0:1, 0:512], psC[64:65, 0:512]))
                            S_["dv"].inc(e.reciprocal(rrow[0:1, 512:1024], psC[64:65, 512:1024]))
                        blk.vector(f_nv1)
                        DV_R1 = S_["dv"].val
                        mm1 = S_["mm"].val

                        def f_nt1(e, DV_R1=DV_R1):
                            e.wait_ge(S_["dv"].h, DV_R1)
                            rbdst = (psD[0:64, :], psD[64:128, :])
                            for hp in range(2):
                                S_["mm"].inc(e.matmul(rbdst[hp], onesf[0:1, :],
                                                      rrow[0:1, 512 * hp:512 * hp + 512], start=True, stop=True))
                        blk.tensor(f_nt1)
                        AC_RB1 = S_["ac"].val

                        def f_nrb1(e, mm1=mm1):
                            rbdst = (psD[0:64, :], psD[64:128, :])
                            for hp in range(2):
                                e.wait_ge(S_["mm"].h, mm1 + hp + 1)
                                S_["ac"].inc(e.copy(out=scr2[0:64, 512 * hp:512 * hp + 512], in_=rbdst[hp]))
                        blk.scalar(f_nrb1)

                        def f_nv21(e, qc=qc, hs=hs, AC_RB1=AC_RB1):
                            for hp in range(2):
                                e.wait_ge(S_["ac"].h, AC_RB1 + hp + 1)
                                S_["dv"].inc(e.tensor_mul(
                                    ctx1[hp * 64:(hp + 1) * 64, hs, qc * 512:(qc + 1) * 512],
                                    psC[0:64, hp * 512:(hp + 1) * 512],
                                    scr2[0:64, 512 * hp:512 * hp + 512]))
                        blk.vector(f_nv21)

                DV_CTX1 = S_["dv"].val
                if "dbg_ctx1" in dbg:
                    dc1 = dram_dbg("dbg_ctx1", [128, DCH, TOK], U16)
                    blk.gpsimd(lambda e: (e.wait_ge(S_["dv"].h, DV_CTX1),
                                          S_["gst"].inc(e.dma_start(out=dc1.bitcast(BF16), in_=ctx1[:]), 16)))

                def f_wgl(e):
                    e.wait_ge(S_["mm"].h, MM_A1)
                    S_["gw"].inc(e.dma_start(out=wog[:], in_=l1_woutT.rearrange("(c p) o -> p c o", p=128)), 16)
                blk.gpsimd(f_wgl)
                GW_WOUT = S_["gw"].val
                ldk0g = S_["ldk"].val
                dv0g = S_["dv"].val

                def f_gld(e, ldk0g=ldk0g, dv0g=dv0g):
                    for t in range(NT):
                        if t == 0:
                            e.wait_ge(S_["dv"].h, DV_CTX1)
                        if t >= 2:
                            e.wait_ge(S_["dv"].h, dv0g + t - 1)
                        S_["ldk"].inc(e.dma_start(
                            out=btst[:, t % 2, :, :].rearrange("p a b -> p (a b)").rearrange("p (c x) -> p c x", c=8),
                            in_=l1g[:, :, t * 128:(t + 1) * 128].rearrange("c p t -> p c t")), 16)
                blk.sync(f_gld)

                def f_gmul(e, ldk0g=ldk0g, dv0g=dv0g):
                    for t in range(NT):
                        e.wait_ge(S_["ldk"].h, ldk0g + 16 * (t + 1))
                        S_["dv"].inc(e.tensor_mul(
                            ctx1[:, :, t * 128:(t + 1) * 128],
                            ctx1[:, :, t * 128:(t + 1) * 128],
                            btst[:, t % 2, :, :].rearrange("p a b -> p (a b)").rearrange("p (c x) -> p c x", c=8)))
                blk.vector(f_gmul)
                DV_GM = S_["dv"].val

                swap_proj_residual(ctx1, wog, DCH, 1, AF.Identity,
                                   wait_sems=[(S_["dv"], DV_GM), (S_["gw"], GW_WOUT)])
            qfm1g.close()

        if "l1ffn" in phases:
            with ExitStack() as _es2:
                h1 = _es2.enter_context(nc.sbuf_tensor("h1", [128, FCH, 512], BF16))
                cup = _es2.enter_context(nc.sbuf_tensor("cup", [128, FCH, 128], BF16))
                cdn = _es2.enter_context(nc.sbuf_tensor("cdn", [128, FCH, 128], BF16))
                mstg = _es2.enter_context(nc.sbuf_tensor("mstg", [128, 7, 128], BF16))
                xsc = _es2.enter_context(nc.sbuf_tensor("xsc", [128, 128], BF16))
                btmp = _es2.enter_context(nc.sbuf_tensor("btmp", [128, 4, 3, 512], BF16))
                wexh = _es2.enter_context(nc.sbuf_tensor("wexh", [128, 4, DCH, 128], BF16))
                wconh = _es2.enter_context(nc.sbuf_tensor("wconh", [128, 4, D], BF16))
                bwal = _es2.enter_context(nc.sbuf_tensor("bwal", [128, 2, 5, 16, 2, 2], F32))

                MM_F0 = S_["mm"].val
                def f_fb(e):
                    e.wait_ge(S_["mm"].h, MM_F0)
                    S_["ldw"].inc(e.dma_start(out=bias1[:, 0:32, 0:1],
                                              in_=ff_bexp.rearrange("(c p) -> p c", p=128).unsqueeze(2),
                                              allow_slow_non_contiguous=True), 16)
                    S_["ldw"].inc(e.dma_start(out=bias1[:, 32:64, 0:1],
                                              in_=ff_upb.rearrange("(c p) -> p c", p=128).unsqueeze(2),
                                              allow_slow_non_contiguous=True), 16)
                    for ud, wsrc in ((0, ff_upw), (1, ff_dnw)):
                        for s5 in range(5):
                            sa = s5 + 7
                            for ai, ch in enumerate([c for c in range(FCH) if not (c >> (sa - 7)) & 1]):
                                kbase = (ch >> (sa - 6)) * (2 ** sa) + (ch % (2 ** (sa - 7))) * 128
                                S_["ldw"].inc(e.dma_start(
                                    out=bwal[:, ud, s5, ai, :, :],
                                    in_=wsrc[sa, kbase:kbase + 128, :, :],
                                    allow_slow_non_contiguous=True), 16)
                blk.sync(f_fb)
                LDW_FB = S_["ldw"].val

                def compose(c_dst, stsrc, tag):
                    ldk0, tp0, ev0, mm0, ac0 = (S_[k].val for k in ("ldk", "tp", "ev", "mm", "ac"))

                    def f_cd(e, ldk0=ldk0, ac0=ac0):
                        e.wait_ge(S_["mm"].h, MM_F0)
                        for ch in range(FCH):
                            if ch >= 1:
                                e.wait_ge(S_["ac"].h, ac0 + 7 * ch)
                            S_["gz"].inc(e.dma_start(out=mstg[:],
                                                      in_=stsrc[:, ch, :, :].rearrange("s p t -> p s t")), 16)
                    blk.gpsimd(f_cd)

                    gz0 = S_["gz"].val - 16 * FCH

                    def f_ct(e, gz0=gz0, tp0=tp0, ev0=ev0, mm0=mm0, ac0=ac0):
                        for ch in range(FCH):
                            e.wait_ge(S_["gz"].h, gz0 + 16 * (ch + 1))
                            if ch >= 1:
                                e.wait_ge(S_["ac"].h, ac0 + 7 * ch)
                            ins = e.transpose(psT[:, 0:128], mstg[:, 6, :], identt[:])
                            S_["tp"].inc(ins)
                            for s in range(5, -1, -1):
                                e.wait_ge(S_["ac"].h, ac0 + 7 * ch + (6 - s))
                                ins = e.matmul(psA[:, 0:128], mstg[:, s, :], xsc[:], start=True, stop=True)
                                S_["mm"].inc(ins)
                    blk.tensor(f_ct)

                    def f_cs(e, tp0=tp0, mm0=mm0, ac0=ac0, c_dst=c_dst):
                        for ch in range(FCH):
                            e.wait_ge(S_["tp"].h, tp0 + ch + 1)
                            S_["ac"].inc(e.copy(out=xsc[:], in_=psT[:, 0:128]))
                            for s in range(5, -1, -1):
                                e.wait_ge(S_["mm"].h, mm0 + 6 * ch + (6 - s))
                                dst = xsc[:] if s > 0 else c_dst[:, ch, :]
                                S_["ac"].inc(e.copy(out=dst, in_=psA[:, 0:128]))
                    blk.scalar(f_cs)
                compose(cup, ff_upT, "up")
                compose(cdn, ff_dnT, "dn")
                LDW_C = S_["ldw"].val

                MM_PRE4 = S_["mm"].val
                ln_sweep([("sbuf", x_res[:, t, :]) for t in range(NT)], xfm, 0,
                         first_evac_wait=(S_["mm"], MM_PRE4))
                EV_LN4 = S_["ev"].val

                slots = [psA[:, 0:512], psA[:, 512:1024], psB[:, 0:512], psB[:, 512:1024]]

                def bfly_pairs():
                    out = []
                    for s5 in range(5):
                        step = 2 ** s5
                        achs = [c for c in range(FCH) if not (c >> s5) & 1]
                        for ai, ch in enumerate(achs):
                            out.append((ch, ch + step, s5, ai))
                    return out

                def butterfly_large(ts, ud, cols):
                    pairs = bfly_pairs()
                    dvb = S_["dv"].val
                    acb = S_["ac"].val

                    def f_a(e, pairs=pairs, dvb=dvb, ud=ud):
                        e.wait_ge(S_["dv"].h, dvb)
                        for q, (a, b, s5, ai) in enumerate(pairs):
                            u = q % 4
                            if q >= 4:
                                e.wait_ge(S_["dv"].h, dvb + 6 * (q // 2 - 1))
                            if q % 16 < 2 and q >= 16:
                                e.wait_ge(S_["dv"].h, dvb + 48 * (q // 16))
                            S_["ac"].inc(e.activation(out=btmp[:, u, 1, :],
                                                      in_=h1[:, b, :], func=AF.Identity,
                                                      scale=cols[:, ud, s5, ai, 0, 1:2]))
                            S_["ac"].inc(e.activation(out=btmp[:, u, 2, :],
                                                      in_=h1[:, b, :], func=AF.Identity,
                                                      scale=cols[:, ud, s5, ai, 1, 1:2]))
                    blk.scalar(f_a)

                    def f_v(e, pairs=pairs, acb=acb, ud=ud):
                        for g in range(len(pairs) // 2):
                            e.wait_ge(S_["ac"].h, acb + 4 * (g + 1))
                            P = (pairs[2 * g], pairs[2 * g + 1])
                            U = ((2 * g) % 4, (2 * g + 1) % 4)
                            for u in range(2):
                                S_["dv"].inc(e.tensor_copy(out=btmp[:, U[u], 0, :], in_=h1[:, P[u][0], :]))
                            for u in range(2):
                                a, b, s5, ai = P[u]
                                S_["dv"].inc(e.scalar_tensor_tensor(
                                    h1[:, a, :], btmp[:, U[u], 0, :], cols[:, ud, s5, ai, 0, 0:1],
                                    btmp[:, U[u], 1, :], ALU.mult, ALU.add))
                            for u in range(2):
                                a, b, s5, ai = P[u]
                                S_["dv"].inc(e.scalar_tensor_tensor(
                                    h1[:, b, :], btmp[:, U[u], 0, :], cols[:, ud, s5, ai, 1, 0:1],
                                    btmp[:, U[u], 2, :], ALU.mult, ALU.add))
                    blk.vector(f_v)

                for ts in range(4):
                    mm0, ac0, gw0, lk0 = S_["mm"].val, S_["ac"].val, S_["gw"].val, S_["ldk"].val

                    def xw_gate(e, oc, mm0):
                        if oc < 4:
                            e.wait_ge(S_["mm"].h, mm0)
                        else:
                            e.wait_ge(S_["mm"].h, mm0 + (oc - 4) + 1)

                    def f_xw_g(e, ts=ts, mm0=mm0):
                        for oc in range(FCH):
                            xw_gate(e, oc, mm0)
                            S_["gw"].inc(e.dma_start(
                                out=wexh[:, oc % 4, :, :],
                                in_=ff_wexp[:, oc * 128:(oc + 1) * 128].rearrange("(c p) o -> p c o", p=128)), 16)
                    blk.gpsimd(f_xw_g)

                    DV_PREV = S_["dv"].val

                    def f_xt(e, ts=ts, mm0=mm0, ac0=ac0, gw0=gw0, lk0=lk0, DV_PREV=DV_PREV):
                        if ts == 0:
                            e.wait_ge(S_["ev"].h, EV_LN4)
                            e.wait_ge(S_["ldw"].h, LDW_C)
                        e.wait_ge(S_["dv"].h, DV_PREV)
                        for oc in range(FCH):
                            e.wait_ge(S_["gw"].h, gw0 + 16 * (oc + 1))
                            if oc >= 4:
                                e.wait_ge(S_["ac"].h, ac0 + oc - 3)
                            for c in range(DCH):
                                ins = e.matmul(slots[oc % 4], wexh[:, oc % 4, c, :],
                                               xfm[:, c, ts * 512:(ts + 1) * 512],
                                               start=(c == 0), stop=(c == DCH - 1))
                            S_["mm"].inc(ins)
                    blk.tensor(f_xt)

                    def f_xs(e, mm0=mm0, ac0=ac0):
                        for oc in range(FCH):
                            e.wait_ge(S_["mm"].h, mm0 + oc + 1)
                            S_["ac"].inc(e.activation(out=h1[:, oc, :], in_=slots[oc % 4],
                                                      func=AF.Identity, bias=bias1[:, oc, 0:1]))
                    blk.scalar(f_xs)
                    AC_EXP = S_["ac"].val

                    mm1 = S_["mm"].val
                    ac1 = S_["ac"].val

                    def f_bu_t(e, mm1=mm1, ac1=ac1, AC_EXP=AC_EXP):
                        e.wait_ge(S_["ac"].h, AC_EXP)
                        for ch in range(FCH):
                            if ch >= 4:
                                e.wait_ge(S_["ac"].h, ac1 + ch - 3)
                            ins = e.matmul(slots[ch % 4], cup[:, ch, :], h1[:, ch, :], start=True, stop=True)
                            S_["mm"].inc(ins)
                    blk.tensor(f_bu_t)

                    def f_bu_s(e, mm1=mm1):
                        for ch in range(FCH):
                            e.wait_ge(S_["mm"].h, mm1 + ch + 1)
                            S_["ac"].inc(e.copy(out=h1[:, ch, :], in_=slots[ch % 4]))
                    blk.scalar(f_bu_s)
                    if ts == 0:
                        blk.scalar(lambda e: e.wait_ge(S_["ldw"].h, LDW_FB))
                    butterfly_large(ts, 0, bwal)
                    DV_BU = S_["dv"].val

                    ac2 = S_["ac"].val

                    def f_gel(e, DV_BU=DV_BU):
                        e.wait_ge(S_["dv"].h, DV_BU)
                        for ch in range(FCH):
                            S_["ac"].inc(e.activation(out=h1[:, ch, :], in_=h1[:, ch, :],
                                                      func=AF.Gelu, bias=bias1[:, 32 + ch, 0:1]))
                    blk.scalar(f_gel)
                    AC_GEL = S_["ac"].val

                    mm2 = S_["mm"].val
                    ac3 = S_["ac"].val

                    def f_bd_t(e, mm2=mm2, ac3=ac3, AC_GEL=AC_GEL):
                        e.wait_ge(S_["ac"].h, AC_GEL)
                        for ch in range(FCH):
                            if ch >= 4:
                                e.wait_ge(S_["ac"].h, ac3 + ch - 3)
                            ins = e.matmul(slots[ch % 4], cdn[:, ch, :], h1[:, ch, :], start=True, stop=True)
                            S_["mm"].inc(ins)
                    blk.tensor(f_bd_t)

                    def f_bd_s(e, mm2=mm2):
                        for ch in range(FCH):
                            e.wait_ge(S_["mm"].h, mm2 + ch + 1)
                            S_["ac"].inc(e.copy(out=h1[:, ch, :], in_=slots[ch % 4]))
                    blk.scalar(f_bd_s)
                    butterfly_large(ts, 1, bwal)
                    DV_BD = S_["dv"].val

                    for rnd in range(2):
                        mm3, gw3, dv3 = S_["mm"].val, S_["gw"].val, S_["dv"].val
                        pss = (psC, psB)

                        def f_cw(e, ts=ts, rnd=rnd, mm3=mm3, DV_BD=DV_BD):
                            for hc in range(FCH):
                                if hc >= 4:
                                    e.wait_ge(S_["mm"].h, mm3 + (hc - 4) + 1)
                                else:
                                    e.wait_ge(S_["mm"].h, mm3)
                                S_["gw"].inc(e.dma_start(
                                    out=wconh[:, hc % 4, :],
                                    in_=ff_wcon[hc * 128:(hc + 1) * 128, :]), 16)
                        blk.gpsimd(f_cw)

                        def f_c_t(e, ts=ts, rnd=rnd, mm3=mm3, gw3=gw3, dv3=dv3, DV_BD=DV_BD):
                            e.wait_ge(S_["dv"].h, DV_BD)
                            if rnd == 1:
                                e.wait_ge(S_["dv"].h, dv3)
                            for hc in range(FCH):
                                e.wait_ge(S_["gw"].h, gw3 + 16 * (hc + 1))
                                for t2 in range(2):
                                    tt = ts * 4 + rnd * 2 + t2
                                    lt = tt % 4
                                    e.matmul(pss[t2][:, 0:512], h1[:, hc, lt * 128:(lt + 1) * 128],
                                             wconh[:, hc % 4, 0:512], start=(hc == 0), stop=(hc == FCH - 1))
                                    ins = e.matmul(pss[t2][:, 512:1024], h1[:, hc, lt * 128:(lt + 1) * 128],
                                                   wconh[:, hc % 4, 512:1024], start=(hc == 0), stop=(hc == FCH - 1))
                                S_["mm"].inc(ins, 1)
                        blk.tensor(f_c_t)
                        MM_C = S_["mm"].val

                        def f_c_v(e, ts=ts, rnd=rnd, MM_C=MM_C):
                            e.wait_ge(S_["mm"].h, MM_C)
                            for t2 in range(2):
                                tt = ts * 4 + rnd * 2 + t2
                                e.tensor_add(scr2[:], pss[t2][:], bcast[:, 2, :])
                                S_["dv"].inc(e.tensor_add(x_res[:, tt, :], x_res[:, tt, :], scr2[:]))
                        blk.vector(f_c_v)

        if "dbg_res2" in dbg:
            dr2 = dram_dbg("dbg_res2", [128, NT, D])
            dv0 = S_["dv"].val
            blk.gpsimd(lambda e: (e.wait_ge(S_["dv"].h, dv0),
                                  S_["gst"].inc(e.dma_start(out=dr2, in_=x_res[:]), 16)))

        if "dbg_res1" in dbg:
            dr = dram_dbg("dbg_res1", [128, NT, D])
            dv0 = S_["dv"].val
            blk.gpsimd(lambda e: (e.wait_ge(S_["dv"].h, dv0),
                                  S_["gst"].inc(e.dma_start(out=dr, in_=x_res[:]), 16)))

        DV_END = S_["dv"].val
        MM_END = S_["mm"].val
        AC_END = S_["ac"].val

        def f_out(e):
            e.wait_ge(S_["dv"].h, DV_END)
            e.wait_ge(S_["mm"].h, MM_END)
            e.wait_ge(S_["ac"].h, AC_END)
            S_["st"].inc(e.dma_start(out=out_own.rearrange("(t p) d -> p t d", p=128), in_=x_res[:]), 16)
        blk.sync(f_out)
        ST_END = S_["st"].val
        blk.gpsimd(lambda e: (e.wait_ge(S_["st"].h, ST_END), e.wait_ge(S_["gst"].h, S_["gst"].val)))

    return nc, inputs, dbg_names


OFFSETS = sorted(set(range(0, 65)) | {96, 128, 192, 256, 384, 512, 768, 1024, 1536})


def own_rows(core):
    seq, r = core // 2, core % 2
    rows = []
    for qc in range(QC):
        g = 2 * qc + r
        rows.append(np.arange(seq * S + g * 512, seq * S + g * 512 + 512))
    return np.concatenate(rows)


def make_winmask(r):
    w = np.arange(8)[:, None, None]
    p = np.arange(128)[None, :, None]
    f = np.arange(512)[None, None, :]
    return ((w * 128 + p) <= (r * 512 + f)).astype(bfd).view(np.uint16)


def make_l1_btile(r, pos_bias):
    o2idx = -np.ones(4096, dtype=np.int64)
    for i, o in enumerate(OFFSETS):
        o2idx[o] = i
    jj = np.arange(L1_NJ)[:, None, None]
    p = np.arange(128)[None, :, None]
    f = np.arange(512)[None, None, :]
    o = (r * 512 + (13 - jj) * 128) + f - p
    valid = (o >= 0) & (o <= 1536)
    idx = np.where(valid, o2idx[np.clip(o, 0, 4095)], -1)
    valid = idx >= 0
    pb = pos_bias[np.clip(idx, 0, 73), :]
    out = np.where(valid[..., None], pb, np.float32(-1e9)).astype(bfd)
    return np.ascontiguousarray(out.transpose(0, 3, 1, 2)).view(np.uint16)


def make_l1_sb(r):
    out = np.full((QC, L1_NJ, 8, 128), -1e9, dtype=np.float32)
    for qc in range(QC):
        g = 2 * qc + r
        for jj in range(L1_NJ):
            ab = 8 * qc + jj - 13
            if ab < 4 * g - 13 - 1 or ab > 4 * g + 3:
                pass
            kpos = ab * 128 + np.arange(128)
            ok = (kpos >= 0) & (ab >= 4 * g - 12) & (ab <= 4 * g + 3)
            out[qc, jj, 0, :] = np.where(ok, np.float32(-4.0), np.float32(-1e9))
    return np.ascontiguousarray(out.transpose(3, 0, 1, 2))


def scatter_stageT(wts):
    out = np.zeros((7, FF_H // 128, 128, 128), dtype=np.float32)
    for st in range(7):
        s = 2 ** st
        k = np.arange(FF_H // 2)
        i = (k // s) * 2 * s + (k % s)
        j = i + s
        w = wts[st]
        ch = i // 128
        il, jl = i % 128, j % 128
        assert np.all(j // 128 == ch)
        out[st, ch, il, il] = w[:, 0, 0]
        out[st, ch, il, jl] = w[:, 0, 1]
        out[st, ch, jl, il] = w[:, 1, 0]
        out[st, ch, jl, jl] = w[:, 1, 1]
    return out


def core_inputs(core, inp):
    seq, r = core // 2, core % 2
    x = np.asarray(inp["L0_attn_states"]).reshape(B * S, D)
    rows = own_rows(core)
    d = {
        "x_own": np.ascontiguousarray(x[rows]),
        "x_seq": np.ascontiguousarray(x[seq * S:(seq + 1) * S]),
        "l0_wq": np.asarray(inp["L0_attn_Wq"]), "l0_bq": np.asarray(inp["L0_attn_bq"]),
        "l0_wk": np.asarray(inp["L0_attn_Wk"]), "l0_bk": np.asarray(inp["L0_attn_bk"]),
        "l0_wv": np.asarray(inp["L0_attn_Wv"]), "l0_bv": np.asarray(inp["L0_attn_bv"]),
        "l0_wo": np.asarray(inp["L0_attn_Wo"]), "l0_bo": np.asarray(inp["L0_attn_bo"]),
        "l0_winmask": make_winmask(r),
        "moe_gw": np.asarray(inp["L0_ffn_gate_w"]),
        "moe_wg": np.asarray(inp["L0_ffn_w_gate"]),
        "moe_wu": np.asarray(inp["L0_ffn_w_up"]),
        "moe_wd": np.asarray(inp["L0_ffn_w_down"]),
        "moe_wsg": np.asarray(inp["L0_ffn_ws_gate"]),
        "moe_wsu": np.asarray(inp["L0_ffn_ws_up"]),
        "moe_wsd": np.asarray(inp["L0_ffn_ws_down"]),
        "l1_wqkvT": np.ascontiguousarray(np.asarray(inp["L1_attn_Wqkv"]).T), "l1_bqkv": np.asarray(inp["L1_attn_bqkv"]),
        "l1_wgateT": np.ascontiguousarray(np.asarray(inp["L1_attn_Wgate"]).T), "l1_bgate": np.asarray(inp["L1_attn_bgate"]),
        "l1_woutT": np.ascontiguousarray(np.asarray(inp["L1_attn_Wout"]).T), "l1_bout": np.asarray(inp["L1_attn_bout"]),
        "l1_btile": make_l1_btile(r, np.asarray(inp["L1_attn_pos_bias"])),
        "l1_sb": make_l1_sb(r),
        "ff_wexp": np.asarray(inp["L1_ffn_W_exp"]), "ff_bexp": np.asarray(inp["L1_ffn_b_exp"]),
        "ff_upT": scatter_stageT(np.asarray(inp["L1_ffn_up_weights"])),
        "ff_dnT": scatter_stageT(np.asarray(inp["L1_ffn_down_weights"])),
        "ff_upw": np.asarray(inp["L1_ffn_up_weights"]),
        "ff_dnw": np.asarray(inp["L1_ffn_down_weights"]),
        "ff_upb": np.asarray(inp["L1_ffn_up_bias"]),
        "ff_wcon": np.asarray(inp["L1_ffn_W_con"]),
        "ff_bsum": np.stack([np.asarray(inp["L1_ffn_b_con"]), np.asarray(inp["L1_ffn_down_bias"])]),
    }
    return d


def assemble_output(results):
    out = np.zeros((B * S, D), dtype=np.float32)
    for core in range(8):
        out[own_rows(core)] = results[core]["out_own"]
    return out.reshape(B, S, D)


from concourse.bass_utils import run_bass_kernel_spmd

_CACHE = {}


def kernel(**inputs):
    if "nc" not in _CACHE:
        nc, inspecs, _ = build()
        _CACHE["nc"] = (nc, inspecs)
    nc, inspecs = _CACHE["nc"]
    in_maps = []
    for c in range(8):
        m = core_inputs(c, inputs)
        in_maps.append({k: m[k] for k in inspecs})
    res = run_bass_kernel_spmd(nc, in_maps, core_ids=list(range(8)))
    return assemble_output(res.results)


# version 08
# speedup vs baseline: 1.5219x; 1.0659x over previous
import numpy as np
import ml_dtypes
import concourse.bass as bass
import concourse.mybir as mybir

F32 = mybir.dt.float32
BF16 = mybir.dt.bfloat16
AF = mybir.ActivationFunctionType
ALU = mybir.AluOpType
AX = mybir.AxisListType
bfd = ml_dtypes.bfloat16
U16 = mybir.dt.uint16

B, S, D = 4, 4096, 1024
H, HD = 16, 64
TOK, NT, QC, NG = 2048, 16, 4, 8
DCH = 8
MOE_D, MOE_E = 2048, 8
FF_H, FCH = 4096, 32
L1_NJ = 21
VCOL = 65
PH_ALL = ("l0qkv", "l0attn", "moe", "l1qkv", "l1attn", "l1ffn")


class Sem:
    def __init__(self, nc, name):
        self.h = nc.alloc_semaphore(name)
        self.val = 0

    def inc(self, ins, n=1):
        ins.then_inc(self.h, n)
        self.val += n
        return ins


def build(phases=PH_ALL, dbg=()):
    phases = list(phases)
    dbg = set(dbg)
    nc = bass.Bass("TRN2", target_bir_lowering=False, detect_race_conditions=False)
    inputs = {}
    dbg_names = []

    def dram_in(name, shape, dtype=F32):
        inputs[name] = (tuple(shape), dtype)
        return nc.dram_tensor(name, list(shape), dtype, kind="ExternalInput").ap()

    def dram_dbg(name, shape, dtype=F32):
        dbg_names.append(name)
        return nc.dram_tensor(name, list(shape), dtype, kind="ExternalOutput").ap()

    x_own = dram_in("x_own", [TOK, D])
    x_seq = dram_in("x_seq", [S, D])
    l0_wq = dram_in("l0_wq", [D, D]); l0_bq = dram_in("l0_bq", [D])
    l0_wk = dram_in("l0_wk", [D, D]); l0_bk = dram_in("l0_bk", [D])
    l0_wv = dram_in("l0_wv", [D, D]); l0_bv = dram_in("l0_bv", [D])
    l0_wo = dram_in("l0_wo", [D, D]); l0_bo = dram_in("l0_bo", [D])
    l0_winmask = dram_in("l0_winmask", [8, 128, 512], U16)
    moe_gw = dram_in("moe_gw", [D, MOE_E])
    moe_wg = dram_in("moe_wg", [MOE_E, D, MOE_D])
    moe_wu = dram_in("moe_wu", [MOE_E, D, MOE_D])
    moe_wd = dram_in("moe_wd", [MOE_E, MOE_D, D])
    moe_wsg = dram_in("moe_wsg", [D, MOE_D])
    moe_wsu = dram_in("moe_wsu", [D, MOE_D])
    moe_wsd = dram_in("moe_wsd", [MOE_D, D])
    l1_wqkvT = dram_in("l1_wqkvT", [D, 3 * D]); l1_bqkv = dram_in("l1_bqkv", [3 * D])
    l1_wgateT = dram_in("l1_wgateT", [D, D]); l1_bgate = dram_in("l1_bgate", [D])
    l1_woutT = dram_in("l1_woutT", [D, D]); l1_bout = dram_in("l1_bout", [D])
    l1_btile = dram_in("l1_btile", [L1_NJ, H, 128, 512], U16)
    l1_sb = dram_in("l1_sb", [128, QC, L1_NJ, 8])
    ff_wexp = dram_in("ff_wexp", [D, FF_H]); ff_bexp = dram_in("ff_bexp", [FF_H])
    ff_upT = dram_in("ff_upT", [7, FCH, 128, 128])
    ff_dnT = dram_in("ff_dnT", [7, FCH, 128, 128])
    ff_upw = dram_in("ff_upw", [12, FF_H // 2, 2, 2])
    ff_dnw = dram_in("ff_dnw", [12, FF_H // 2, 2, 2])
    ff_upb = dram_in("ff_upb", [FF_H])
    ff_wcon = dram_in("ff_wcon", [FF_H, D])
    ff_bsum = dram_in("ff_bsum", [2, D])
    out_own = nc.dram_tensor("out_own", [TOK, D], F32, kind="ExternalOutput").ap()

    l0k = nc.dram_tensor("l0k", [NG, DCH, 128, 512], BF16, kind="Internal").ap()
    l0v = nc.dram_tensor("l0v", [NG, 4, 128, VCOL * H], BF16, kind="Internal").ap()
    l1g = nc.dram_tensor("l1g", [DCH, 128, TOK], BF16, kind="Internal").ap()
    CHK = DCH * 128 * 512
    CHV = 4 * 128 * (VCOL * H)
    cc_in = nc.dram_tensor("cc_in", [QC, CHK + CHV], BF16, kind="Internal").ap()
    cc_out = nc.dram_tensor("cc_out", [2, QC, CHK + CHV], BF16, kind="Internal").ap()
    l1pad = nc.dram_tensor("l1pad", [4, CHK + CHV], BF16, kind="Internal").ap()

    def l1_kv(g):
        base = l1pad[(g % 4), :] if g < 0 else cc_out[g % 2, g // 2, :]
        k = base[0:CHK].rearrange("(c p t) -> c p t", c=DCH, p=128)
        v = base[CHK:CHK + CHV].rearrange("(f p t) -> f p t", f=4, p=128)
        return k, v

    ident_np = nc.inline_tensor(np.eye(128, dtype=bfd), "ident_bf")
    ones_np = nc.inline_tensor(np.ones((1, 128), dtype=bfd), "ones_row2")
    onesf_np = nc.inline_tensor(np.ones((1, 64), dtype=np.float32), "ones_f32")

    from contextlib import ExitStack
    with ExitStack() as _es:
        x_res = _es.enter_context(nc.sbuf_tensor("x_res", [128, NT, D], F32))
        identt = _es.enter_context(nc.sbuf_tensor("identt", [128, 128], BF16))
        onesr = _es.enter_context(nc.sbuf_tensor("onesr", [1, 128], BF16))
        onesf = _es.enter_context(nc.sbuf_tensor("onesf", [1, 64], F32))
        stats = _es.enter_context(nc.sbuf_tensor("stats", [128, 16, 8], F32))
        epscol = _es.enter_context(nc.sbuf_tensor("epscol", [128, 8], F32))
        scr = _es.enter_context(nc.sbuf_tensor("scr", [128, 1024], F32))
        scr2 = _es.enter_context(nc.sbuf_tensor("scr2", [128, 1024], F32))
        bias1 = _es.enter_context(nc.sbuf_tensor("bias1", [128, 64, 8], F32))
        bcast = _es.enter_context(nc.sbuf_tensor("bcast", [128, 3, 1024], BF16))
        lnbf = _es.enter_context(nc.sbuf_tensor("lnbf", [128, 1, 1024], BF16))
        xfm = _es.enter_context(nc.sbuf_tensor("xfm", [128, DCH, TOK], BF16))
        psA = _es.enter_context(nc.psum_tensor("psA", [128, 1024], F32))
        psB = _es.enter_context(nc.psum_tensor("psB", [128, 1024], F32))
        psC = _es.enter_context(nc.psum_tensor("psC", [128, 1024], F32))
        psD = _es.enter_context(nc.psum_tensor("psD", [128, 512], F32))
        psT = _es.enter_context(nc.psum_tensor("psT", [128, 1024], BF16))
        rrow = scr
        blk = _es.enter_context(nc.Block())
        S_ = {k: Sem(nc, k) for k in
              ["ldw", "ldx", "ldk", "mm", "tp", "ev", "dv", "ac", "kv", "cc", "gp", "st",
               "gw", "gx", "gz", "gst"]}
        BQ, BK, BV, BG1, BQK = 0, 8, 16, 24, 32

        def ln_sweep(tiles, dst_fm, dst_off, first_evac_wait=None, xsrc=None):
            n = len(tiles)
            ldx0, dv0, ac0, tp0, ev0 = (S_[k].val for k in ("ldx", "dv", "ac", "tp", "ev"))

            def f_sync(e):
                hseen = 0
                for i, (kind, ap_) in enumerate(tiles):
                    if kind == "hbm":
                        if hseen >= 2:
                            e.wait_ge(S_["dv"].h, dv0 + 2 * (i - 2) + 2)
                        S_["ldx"].inc(e.dma_start(out=xsrc[:, i % 2, :], in_=ap_), 16)
                        hseen += 1
            blk.sync(f_sync)

            def f_v(e):
                hseen = 0
                for i, (kind, ap_) in enumerate(tiles):
                    if kind == "hbm":
                        hseen += 1
                        e.wait_ge(S_["ldx"].h, ldx0 + 16 * hseen)
                        src = xsrc[:, i % 2, :]
                    else:
                        src = ap_
                    a = stats[:, (2 * i) % 8, :]
                    b = stats[:, (2 * i) % 8 + 1, :]
                    dmy = stats[:, 15, :]
                    e.tensor_scalar(scr[:], src, 1.0, 0.0, ALU.mult, ALU.add, accum_out=a[:, 2:3])
                    e.scalar_tensor_tensor(scr2[:], src, 1.0, src, ALU.mult, ALU.mult, accum_out=a[:, 3:4])
                    e.tensor_scalar_mul(a[:, 1:2], a[:, 2:3], 1.0 / D)
                    e.tensor_scalar_mul(a[:, 4:5], a[:, 3:4], 1.0 / D)
                    e.tensor_mul(a[:, 5:6], a[:, 1:2], a[:, 1:2])
                    e.tensor_copy(dmy[:, 0:1], a[:, 2:3])
                    e.tensor_copy(dmy[:, 1:2], a[:, 3:4])
                    S_["dv"].inc(e.tensor_sub(a[:, 0:1], a[:, 4:5], a[:, 5:6]))
                    e.wait_ge(S_["ac"].h, ac0 + i + 1)
                    if i >= 1:
                        e.wait_ge(S_["tp"].h, tp0 + i)
                    e.reciprocal(b[:, 1:2], b[:, 0:1])
                    e.tensor_copy(dmy[:, 0:1], a[:, 2:3])
                    e.tensor_copy(dmy[:, 1:2], a[:, 3:4])
                    S_["dv"].inc(e.tensor_scalar(lnbf[:, 0, :], src, a[:, 1:2], b[:, 1:2],
                                                 ALU.subtract, ALU.mult))

            def f_s(e):
                def evac(i):
                    e.wait_ge(S_["tp"].h, tp0 + i + 1)
                    if i == 0 and first_evac_wait is not None:
                        e.wait_ge(first_evac_wait[0].h, first_evac_wait[1])
                    o = dst_off + i * 128
                    S_["ev"].inc(e.copy(out=dst_fm[:, :, o:o + 128],
                                        in_=psT[:].rearrange("p (c x) -> p c x", c=DCH)))
                for i in range(n):
                    a = stats[:, (2 * i) % 8, :]
                    b = stats[:, (2 * i) % 8 + 1, :]
                    e.wait_ge(S_["dv"].h, dv0 + 2 * i + 1)
                    S_["ac"].inc(e.activation(out=b[:, 0:1], in_=a[:, 0:1], func=AF.Sqrt, bias=epscol[:, 0:1]))
                    if i >= 1:
                        evac(i - 1)
                evac(n - 1)

            def f_t(e):
                for i in range(n):
                    e.wait_ge(S_["dv"].h, dv0 + 2 * i + 2)
                    e.wait_ge(S_["ev"].h, ev0 + i)
                    for c in range(DCH):
                        ins = e.transpose(psT[:, c * 128:(c + 1) * 128],
                                          lnbf[:, 0, c * 128:(c + 1) * 128], identt[:])
                    S_["tp"].inc(ins)
            blk.vector(f_v)
            blk.scalar(f_s)
            blk.tensor(f_t)

        def std_proj(w_sb, x_fm_sl, dst_slices, oc_list, ts_list, func, bias_base,
                     wait_sems=(), scale=1.0, slot_free=None):
            mm0, ac0 = S_["mm"].val, S_["ac"].val
            slots = [psA[:, 0:512], psA[:, 512:1024], psB[:, 0:512], psB[:, 512:1024]]
            combos = [(oi, oc, ts) for oi, oc in enumerate(oc_list) for ts in ts_list]

            def f_t(e):
                for sem, val in wait_sems:
                    e.wait_ge(sem.h, val)
                for idx, (oi, oc, ts) in enumerate(combos):
                    if idx >= 4:
                        e.wait_ge(S_["ac"].h, ac0 + idx - 3)
                    x = x_fm_sl(ts)
                    for c in range(DCH):
                        ins = e.matmul(slots[idx % 4], w_sb[:, c, oc * 128:(oc + 1) * 128],
                                       x[:, c, :], start=(c == 0), stop=(c == DCH - 1))
                    S_["mm"].inc(ins)

            def f_s(e):
                for idx, (oi, oc, ts) in enumerate(combos):
                    e.wait_ge(S_["mm"].h, mm0 + idx + 1)
                    if slot_free is not None and idx >= 2:
                        semv = slot_free(idx)
                        e.wait_ge(semv[0].h, semv[1])
                    bias = bias1[:, bias_base + oi, 0:1] if bias_base is not None else 0.0
                    S_["ac"].inc(e.activation(out=dst_slices(oi, ts), in_=slots[idx % 4],
                                              func=func, bias=bias, scale=scale))
            blk.tensor(f_t)
            blk.scalar(f_s)

        def swap_proj_residual(src_fm, w_sb, nch, bcast_idx, act_func, extra_mul_col=None,
                               wait_sems=(), gate_fm=None):
            mm0, dv0, ac0 = S_["mm"].val, S_["dv"].val, S_["ac"].val

            def f_t(e):
                for sem, val in wait_sems:
                    e.wait_ge(sem.h, val)
                for tt in range(NT):
                    if tt >= 2:
                        e.wait_ge(S_["dv"].h, dv0 + 2 * tt - 3)
                    ps = (psC, psB)[tt % 2]
                    for c in range(nch):
                        e.matmul(ps[:, 0:512], src_fm[:, c, tt * 128:(tt + 1) * 128],
                                 w_sb[:, c, 0:512], start=(c == 0), stop=(c == nch - 1))
                        ins = e.matmul(ps[:, 512:1024], src_fm[:, c, tt * 128:(tt + 1) * 128],
                                       w_sb[:, c, 512:1024], start=(c == 0), stop=(c == nch - 1))
                    S_["mm"].inc(ins)

            def f_v(e):
                for tt in range(NT):
                    ps = (psC, psB)[tt % 2]
                    e.wait_ge(S_["mm"].h, mm0 + tt + 1)
                    if bcast_idx is not None:
                        S_["dv"].inc(e.tensor_add(scr2[:], ps[:], bcast[:, bcast_idx, :]))
                    else:
                        S_["dv"].inc(e.tensor_copy(out=scr2[:], in_=ps[:]))
                    e.wait_ge(S_["ac"].h, ac0 + tt + 1)
                    if extra_mul_col is not None:
                        S_["dv"].inc(e.scalar_tensor_tensor(x_res[:, tt, :], scr[:], extra_mul_col(tt),
                                                            x_res[:, tt, :], ALU.mult, ALU.add))
                    else:
                        S_["dv"].inc(e.tensor_add(x_res[:, tt, :], x_res[:, tt, :], scr[:]))

            def f_s(e):
                for tt in range(NT):
                    e.wait_ge(S_["dv"].h, dv0 + 2 * tt + 1)
                    S_["ac"].inc(e.activation(out=scr[:], in_=scr2[:], func=act_func))
            blk.tensor(f_t)
            blk.vector(f_v)
            blk.scalar(f_s)

        def f_p0_sync(e):
            S_["ldw"].inc(e.dma_start(out=identt[:], in_=ident_np.ap()), 16)
            S_["ldw"].inc(e.dma_start(out=onesr[:], in_=ones_np.ap()), 16)
            S_["ldw"].inc(e.dma_start(out=onesf[:], in_=onesf_np.ap()), 16)
            for ap_, base in ((l0_bq, BQ), (l0_bk, BK), (l0_bv, BV)):
                S_["ldw"].inc(e.dma_start(out=bias1[:, base:base + 8, 0:1],
                                          in_=ap_.rearrange("(c p) -> p c", p=128).unsqueeze(2), allow_slow_non_contiguous=True), 16)
            S_["ldw"].inc(e.dma_start(out=scr2[0:1, 0:1024], in_=l0_bo[None, :]), 16)
            S_["ldw"].inc(e.dma_start(out=scr2[32:33, 0:1024], in_=l1_bout[None, :]), 16)
            S_["ldw"].inc(e.dma_start(out=scr2[64:65, 0:1024], in_=ff_bsum[0:1, :]), 16)
            S_["ldw"].inc(e.dma_start(out=scr2[96:97, 0:1024], in_=ff_bsum[1:2, :]), 16)
        blk.sync(f_p0_sync)
        LDW_P0 = S_["ldw"].val

        def f_p0_g(e):
            S_["gx"].inc(e.dma_start(out=x_res[:], in_=x_own.rearrange("(t p) d -> p t d", p=128)), 16)
        blk.gpsimd(f_p0_g)
        LDX_RES = S_["gx"].val

        def f_p0_v(e):
            e.wait_ge(S_["ldw"].h, LDW_P0)
            e.memset(epscol[:, 0:1], 1e-5)
            S_["dv"].inc(e.memset(scr[:], 0.0))
        blk.vector(f_p0_v)
        DV_P0 = S_["dv"].val

        def f_p0_g2(e):
            e.wait_ge(S_["dv"].h, DV_P0)
            zb = scr[:].bitcast(BF16)
            total = 4 * (CHK + CHV)
            flat = l1pad.rearrange("a b -> (a b)")
            per = 128 * 2048
            off = 0
            while off < total:
                n = min(per, total - off)
                S_["gz"].inc(e.dma_start(out=flat[off:off + n].rearrange("(p x) -> p x", p=128),
                                         in_=zb[:, 0:n // 128]), 16)
                off += n
        blk.gpsimd(f_p0_g2)
        KV_PAD = S_["gz"].val
        TP_P00 = S_["tp"].val
        DV_P00 = S_["dv"].val

        def f_p0_v2(e):
            e.wait_ge(S_["gz"].h, KV_PAD)
            for i in range(4):
                e.wait_ge(S_["tp"].h, TP_P00 + i)
                S_["dv"].inc(e.tensor_copy(out=scr[0:1, 0:512].bitcast(BF16),
                                           in_=scr2[32 * i:32 * i + 1, 0:1024]))
        blk.vector(f_p0_v2)
        EV_BC0 = S_["ev"].val

        def f_p0_t(e):
            for i in range(4):
                e.wait_ge(S_["dv"].h, DV_P00 + i + 1)
                if i > 0:
                    e.wait_ge(S_["ev"].h, EV_BC0 + i)
                for half in range(2):
                    ins = e.matmul(psA[:, half * 512:(half + 1) * 512], onesr[0:1, :],
                                   scr[0:1, 0:512].bitcast(BF16)[:, half * 512:(half + 1) * 512],
                                   start=True, stop=True)
                S_["tp"].inc(ins)
        blk.tensor(f_p0_t)

        def f_p0_s(e):
            for i in range(4):
                e.wait_ge(S_["tp"].h, TP_P00 + i + 1)
                if i < 3:
                    S_["ev"].inc(e.copy(out=bcast[:, i, :], in_=psA[:]))
                else:
                    S_["ev"].inc(e.copy(out=scr2[:], in_=psA[:]))
        blk.scalar(f_p0_s)
        EV_BC4 = S_["ev"].val

        def f_p0_v3(e):
            e.wait_ge(S_["ev"].h, EV_BC4)
            S_["dv"].inc(e.tensor_add(bcast[:, 2, :], bcast[:, 2, :], scr2[:]))
        blk.vector(f_p0_v3)
        EV_BC = S_["ev"].val

        if "l0qkv" in phases or "l0attn" in phases:
            _esL0 = _es.enter_context(ExitStack())
            qfm = _esL0.enter_context(nc.sbuf_tensor("qfm", [128, DCH, TOK], BF16))
            with ExitStack() as _es2:
                xsrcL = _es2.enter_context(nc.sbuf_tensor("xsrcL", [128, 2, 1024], F32))
                wk_sb = _es2.enter_context(nc.sbuf_tensor("wk_sb", [128, DCH, D], BF16))
                wv_sb = _es2.enter_context(nc.sbuf_tensor("wv_sb", [128, DCH, D], BF16))
                kvo = _es2.enter_context(nc.sbuf_tensor("kvo", [128, 2, 512], BF16))
                vfm = _es2.enter_context(nc.sbuf_tensor("vfm", [128, DCH, 512], BF16))
                vto = _es2.enter_context(nc.sbuf_tensor("vto", [128, 1, VCOL * H], BF16))
                wq_sb = wk_sb
                def f_wl(e):
                    S_["gw"].inc(e.dma_start(out=wk_sb[:], in_=l0_wq.rearrange("(c p) o -> p c o", p=128)), 16)
                    S_["gw"].inc(e.dma_start(out=wv_sb[:], in_=l0_wv.rearrange("(c p) o -> p c o", p=128)), 16)
                blk.gpsimd(f_wl)
                LDW_QKV = S_["gw"].val

                def f_wait_res(e):
                    e.wait_ge(S_["gx"].h, LDX_RES)
                blk.vector(f_wait_res)
                ln_sweep([("sbuf", x_res[:, t, :]) for t in range(NT)], xfm, 0)
                EV_LN1 = S_["ev"].val
                if "dbg_xfm" in dbg:
                    dx = dram_dbg("dbg_xfm", [128, DCH, TOK], U16)
                    blk.gpsimd(lambda e, v=EV_LN1: (e.wait_ge(S_["ev"].h, v),
                                S_["gst"].inc(e.dma_start(out=dx.bitcast(BF16), in_=xfm[:]), 16)))

                std_proj(wq_sb, lambda ts: xfm[:, :, ts * 512:(ts + 1) * 512],
                         lambda oi, ts: qfm[:, oi, ts * 512:(ts + 1) * 512],
                         list(range(DCH)), list(range(4)), AF.Tanh, BQ,
                         wait_sems=[(S_["ev"], EV_LN1), (S_["gw"], LDW_QKV)])
                MM_Q = S_["mm"].val

                def f_wl2(e):
                    e.wait_ge(S_["mm"].h, MM_Q)
                    S_["gw"].inc(e.dma_start(out=wk_sb[:], in_=l0_wk.rearrange("(c p) o -> p c o", p=128)), 16)
                blk.gpsimd(f_wl2)
                LDW_K = S_["gw"].val

                if "dbg_q" in dbg:
                    dq = dram_dbg("dbg_q", [128, DCH, TOK])
                    ac0 = S_["ac"].val
                    blk.gpsimd(lambda e: S_["gst"].inc(
                        (e.wait_ge(S_["ac"].h, ac0), e.dma_start(out=dq, in_=qfm[:]))[1], 16))

                for g in range(NG):
                    fw = (S_["mm"], MM_Q) if g == 0 else None
                    ln_sweep([("hbm", x_seq[g * 512 + t * 128:g * 512 + (t + 1) * 128, :])
                              for t in range(4)], xfm, 0, first_evac_wait=fw, xsrc=xsrcL)
                    EV_LNG = S_["ev"].val
                    KV_K0 = S_["kv"].val
                    AC_K0 = S_["ac"].val
                    std_proj(wk_sb, lambda ts: xfm[:, :, 0:512],
                             lambda oi, ts: kvo[:, oi % 2, :], list(range(DCH)), [0], AF.Tanh, BK,
                             wait_sems=[(S_["ev"], EV_LNG), (S_["gw"], LDW_K)],
                             slot_free=lambda idx: (S_["kv"], KV_K0 + 16 * (idx - 1)))
                    def f_spk(e, g=g, AC_K0=AC_K0):
                        for oc in range(DCH):
                            e.wait_ge(S_["ac"].h, AC_K0 + oc + 1)
                            S_["kv"].inc(e.dma_start(out=l0k[g, oc], in_=kvo[:, oc % 2, :]), 16)
                    blk.sync(f_spk)
                    std_proj(wv_sb, lambda ts: xfm[:, :, 0:512],
                             lambda oi, ts: vfm[:, oi, :], list(range(DCH)), [0], AF.Tanh, BV,
                             wait_sems=[(S_["kv"], S_["kv"].val)])
                    AC_V = S_["ac"].val
                    tp0, ev0, dv0 = S_["tp"].val, S_["ev"].val, S_["dv"].val

                    def f_tv(e, AC_V=AC_V, tp0=tp0, ev0=ev0):
                        e.wait_ge(S_["ac"].h, AC_V)
                        for tt in range(4):
                            e.wait_ge(S_["ev"].h, ev0 + tt)
                            for c in range(DCH):
                                ins = e.transpose(psT[:, c * 128:(c + 1) * 128],
                                                  vfm[:, c, tt * 128:(tt + 1) * 128], identt[:])
                            S_["tp"].inc(ins)
                    blk.tensor(f_tv)

                    if g == 0:
                        def f_ones(e):
                            S_["dv"].inc(e.memset(
                                vto[:, 0, :].rearrange("p (h x) -> p h x", x=VCOL)[:, :, 64:65], 1.0))
                        blk.vector(f_ones)
                    DV_ONES = S_["dv"].val
                    kv0v = S_["kv"].val
                    EV_V0 = S_["ev"].val

                    def f_sv(e, tp0=tp0, kv0v=kv0v):
                        for tt in range(4):
                            e.wait_ge(S_["tp"].h, tp0 + tt + 1)
                            e.wait_ge(S_["kv"].h, kv0v + 16 * tt)
                            dst = vto[:, 0, :].rearrange("p (h x) -> p h x", x=VCOL)[:, :, 0:64]
                            S_["ev"].inc(e.copy(out=dst, in_=psT[:].rearrange("p (h x) -> p h x", x=64)))
                    blk.scalar(f_sv)

                    def f_spv(e, g=g, EV_V0=EV_V0, DV_ONES=DV_ONES):
                        e.wait_ge(S_["dv"].h, DV_ONES)
                        for tt in range(4):
                            e.wait_ge(S_["ev"].h, EV_V0 + tt + 1)
                            S_["kv"].inc(e.dma_start(out=l0v[g, tt], in_=vto[:, 0, :]), 16)
                    blk.sync(f_spv)

                if "dbg_kv" in dbg:
                    dk = dram_dbg("dbg_k", [NG, DCH, 128, 512], U16)
                    dvv = dram_dbg("dbg_v", [NG, 4, 128, VCOL * H], U16)
                    kv0 = S_["kv"].val
                    blk.gpsimd(lambda e: (e.wait_ge(S_["kv"].h, kv0),
                                          S_["gst"].inc(e.dma_start(out=dk.bitcast(BF16), in_=l0k[:]), 16),
                                          S_["gst"].inc(e.dma_start(out=dvv.bitcast(BF16), in_=l0v[:]), 16)))

        if "l0attn" in phases:
            with ExitStack() as _es2:
                wo_sb = _es2.enter_context(nc.sbuf_tensor("wo_sb", [128, DCH, D], BF16))
                winm = _es2.enter_context(nc.sbuf_tensor("winm", [128, 8, 512], BF16))
                kst = _es2.enter_context(nc.sbuf_tensor("kst", [128, 4, 128], BF16))
                vst = _es2.enter_context(nc.sbuf_tensor("vst", [128, 4, 130], BF16))
                pT = _es2.enter_context(nc.sbuf_tensor("pT", [128, 2, 1024], BF16))
                ctxfm = xfm
                KV_ALL = S_["kv"].val
                MM_QKVDONE = S_["mm"].val
                def f_wm(e):
                    e.wait_ge(S_["kv"].h, KV_ALL)
                    e.wait_ge(S_["mm"].h, MM_QKVDONE)
                    S_["ldw"].inc(e.dma_start(out=winm[:], in_=l0_winmask.rearrange("w p t -> p w t").bitcast(BF16)), 16)
                blk.sync(f_wm)
                LDW_WM = S_["ldw"].val

                for hs in range(8):
                    for qc in range(QC):
                        nj = 8 * qc + 8
                        ldk0, mm0, ac0, dv0 = (S_[k].val for k in ("ldk", "mm", "ac", "dv"))
                        QK = lambda j: mm0 + 1 if j == 0 else mm0 + 2 * j
                        AV = lambda j, nj=nj: mm0 + 2 * j + (3 if j < nj - 1 else 2)

                        def f_dma(e, qc=qc, nj=nj, ldk0=ldk0, mm0=mm0, hs=hs, AV=AV):
                            for j in range(nj):
                                if j >= 4:
                                    e.wait_ge(S_["mm"].h, AV(j - 4))
                                else:
                                    e.wait_ge(S_["mm"].h, mm0)
                                if hs == 0 and qc == 0 and j == 0:
                                    e.wait_ge(S_["kv"].h, KV_ALL)
                                g, tt = j // 4, j % 4
                                S_["ldk"].inc(e.dma_start(out=kst[:, j % 4, :],
                                                          in_=l0k[g, hs, :, tt * 128:(tt + 1) * 128]), 16)
                                S_["ldk"].inc(e.dma_start(out=vst[:, j % 4, :],
                                                          in_=l0v[g, tt, :, 130 * hs:130 * hs + 130]), 16)
                        blk.sync(f_dma)

                        def f_pe(e, qc=qc, nj=nj, ldk0=ldk0, mm0=mm0, ac0=ac0, dv0=dv0, hs=hs):
                            if hs == 0 and qc == 0:
                                e.wait_ge(S_["ldw"].h, LDW_WM)

                            def emit_qk(j):
                                strip = (psA, psB)[j % 2]
                                e.wait_ge(S_["ldk"].h, ldk0 + 32 * j + 16)
                                if j >= 2:
                                    e.wait_ge(S_["ac"].h, ac0 + j - 1)
                                elif not (hs == 0 and qc == 0):
                                    e.wait_ge(S_["ac"].h, ac0 - 2)
                                for hp in range(2):
                                    ins = e.matmul(strip[:, hp * 512:(hp + 1) * 512],
                                                   kst[hp * 64:(hp + 1) * 64, j % 4, :],
                                                   qfm[hp * 64:(hp + 1) * 64, hs, qc * 512:(qc + 1) * 512],
                                                   start=True, stop=True)
                                S_["mm"].inc(ins)

                            def emit_av(j):
                                e.wait_ge(S_["ldk"].h, ldk0 + 32 * j + 32)
                                e.wait_ge(S_["ac"].h, ac0 + j + 1)
                                if j == 0:
                                    e.wait_ge(S_["dv"].h, dv0)
                                if j >= 8 * qc:
                                    e.wait_ge(S_["dv"].h, dv0 + 2 * (j - 8 * qc) + 2)
                                for hp in range(2):
                                    avp = psC[0:65, hp * 512:(hp + 1) * 512]
                                    ins = e.matmul(avp, vst[:, j % 4, hp * 65:(hp + 1) * 65],
                                                   pT[:, j % 2, hp * 512:(hp + 1) * 512],
                                                   start=(j == 0), stop=(j == nj - 1))
                                S_["mm"].inc(ins)

                            emit_qk(0)
                            for j in range(1, nj):
                                emit_qk(j)
                                emit_av(j - 1)
                            emit_av(nj - 1)
                        blk.tensor(f_pe)

                        def f_exp(e, qc=qc, nj=nj, mm0=mm0, dv0=dv0, ac0=ac0):
                            for j in range(nj):
                                strip = (psA, psB)[j % 2]
                                e.wait_ge(S_["mm"].h, QK(j))
                                S_["ac"].inc(e.activation(out=pT[:, j % 2, :], in_=strip[:],
                                                          func=AF.Exp, scale=0.125))
                        blk.scalar(f_exp)

                        def f_mask(e, qc=qc, nj=nj, ac0=ac0, dv0=dv0):
                            for j in range(8 * qc, nj):
                                w = j - 8 * qc
                                e.wait_ge(S_["ac"].h, ac0 + j + 1)
                                for hp in range(2):
                                    S_["dv"].inc(e.tensor_mul(pT[:, j % 2, hp * 512:(hp + 1) * 512],
                                                              pT[:, j % 2, hp * 512:(hp + 1) * 512],
                                                              winm[:, w, :]))
                        blk.vector(f_mask)

                        MM_AVE = S_["mm"].val
                        dv1 = S_["dv"].val

                        def f_nv(e, qc=qc, hs=hs, MM_AVE=MM_AVE, dv1=dv1):
                            e.wait_ge(S_["mm"].h, MM_AVE)
                            S_["dv"].inc(e.reciprocal(rrow[0:1, 0:512], psC[64:65, 0:512]))
                            S_["dv"].inc(e.reciprocal(rrow[0:1, 512:1024], psC[64:65, 512:1024]))
                        blk.vector(f_nv)
                        DV_R = S_["dv"].val
                        mm1 = S_["mm"].val

                        def f_nt(e, DV_R=DV_R):
                            e.wait_ge(S_["dv"].h, DV_R)
                            rbdst = (psD[0:64, :], psD[64:128, :])
                            for hp in range(2):
                                S_["mm"].inc(e.matmul(rbdst[hp], onesf[0:1, :],
                                                      rrow[0:1, 512 * hp:512 * hp + 512], start=True, stop=True))
                        blk.tensor(f_nt)
                        AC_RB = S_["ac"].val

                        def f_nrb(e, mm1=mm1):
                            rbdst = (psD[0:64, :], psD[64:128, :])
                            for hp in range(2):
                                e.wait_ge(S_["mm"].h, mm1 + hp + 1)
                                S_["ac"].inc(e.copy(out=scr2[0:64, 512 * hp:512 * hp + 512], in_=rbdst[hp]))
                        blk.scalar(f_nrb)

                        def f_nv2(e, qc=qc, hs=hs, AC_RB=AC_RB):
                            for hp in range(2):
                                e.wait_ge(S_["ac"].h, AC_RB + hp + 1)
                                S_["dv"].inc(e.tensor_mul(
                                    ctxfm[hp * 64:(hp + 1) * 64, hs, qc * 512:(qc + 1) * 512],
                                    psC[0:64, hp * 512:(hp + 1) * 512],
                                    scr2[0:64, 512 * hp:512 * hp + 512]))
                        blk.vector(f_nv2)
                if "dbg_ctx" in dbg:
                    dc = dram_dbg("dbg_ctx", [128, DCH, TOK], U16)
                    dv0 = S_["dv"].val
                    blk.gpsimd(lambda e: (e.wait_ge(S_["dv"].h, dv0),
                                          S_["gst"].inc(e.dma_start(out=dc.bitcast(BF16), in_=ctxfm[:]), 16)))

                if "dbg_winm" in dbg:
                    dwm = dram_dbg("dbg_winm", [128, 8, 512], U16)
                    _ldwv = S_["ldw"].val
                    blk.gpsimd(lambda e: (e.wait_ge(S_["ldw"].h, _ldwv),
                                          S_["gst"].inc(e.dma_start(out=dwm.bitcast(BF16), in_=winm[:]), 16)))
                DV_CTX = S_["dv"].val
                MM_WO = S_["mm"].val
                def f_wo(e):
                    e.wait_ge(S_["mm"].h, MM_WO)
                    S_["gw"].inc(e.dma_start(out=wo_sb[:], in_=l0_wo.rearrange("(c p) o -> p c o", p=128)), 16)
                blk.gpsimd(f_wo)
                swap_proj_residual(ctxfm, wo_sb, DCH, 0, AF.Tanh,
                                   wait_sems=[(S_["dv"], DV_CTX), (S_["gw"], S_["gw"].val),
                                              (S_["ev"], EV_BC)])

        if "l0qkv" in phases or "l0attn" in phases:
            _esL0.close()

        if "moe" in phases:
            with ExitStack() as _es2:
                wgh = _es2.enter_context(nc.sbuf_tensor("wgh", [128, 4, DCH, 128], BF16))
                wuh = _es2.enter_context(nc.sbuf_tensor("wuh", [128, 4, DCH, 128], BF16))
                wde = _es2.enter_context(nc.sbuf_tensor("wde", [128, 16, D], BF16))
                hidb = _es2.enter_context(nc.sbuf_tensor("hidb", [128, 16, 512], BF16))
                sc8 = _es2.enter_context(nc.sbuf_tensor("sc8", [128, NT, 8], F32))
                scrE = _es2.enter_context(nc.sbuf_tensor("scrE", [128, NT, 8], F32))
                scrF = _es2.enter_context(nc.sbuf_tensor("scrF", [128, NT, 8], F32))
                stats2 = _es2.enter_context(nc.sbuf_tensor("stats2", [128, NT, 8], F32))
                wpe = _es2.enter_context(nc.sbuf_tensor("wpe", [128, NT, 9, 8], F32))
                gwsb = _es2.enter_context(nc.sbuf_tensor("gwsb", [128, DCH, 8], BF16))
                gscb = scr2[:].bitcast(BF16)
                gsil = scr[:].bitcast(BF16).rearrange("p (a t) -> p a t", a=4)

                MM_PRE = S_["mm"].val
                ln_sweep([("sbuf", x_res[:, t, :]) for t in range(NT)], xfm, 0,
                         first_evac_wait=(S_["mm"], MM_PRE))
                EV_LN2 = S_["ev"].val

                def f_gwl(e):
                    e.wait_ge(S_["mm"].h, MM_PRE)
                    S_["gw"].inc(e.dma_start(out=gwsb[:], in_=moe_gw.rearrange("(c p) o -> p c o", p=128)), 16)
                blk.gpsimd(f_gwl)
                GW_GATE = S_["gw"].val

                DV_Z0 = S_["dv"].val
                blk.vector(lambda e: S_["dv"].inc(e.memset(gscb[:], 0.0)))
                DV_Z = S_["dv"].val
                mmg0 = S_["mm"].val

                def f_gt(e):
                    e.wait_ge(S_["ev"].h, EV_LN2)
                    e.wait_ge(S_["gw"].h, GW_GATE)
                    for ts in range(4):
                        if ts >= 1:
                            e.wait_ge(S_["ac"].h, acg0 + ts)
                        for c in range(DCH):
                            ins = e.matmul(psD[0:8, :], gwsb[:, c, :], xfm[:, c, ts * 512:(ts + 1) * 512],
                                           start=(c == 0), stop=(c == DCH - 1))
                        S_["mm"].inc(ins)
                acg0 = S_["ac"].val
                blk.tensor(f_gt)

                def f_gs(e):
                    for ts in range(4):
                        e.wait_ge(S_["mm"].h, mmg0 + ts + 1)
                        e.wait_ge(S_["dv"].h, DV_Z)
                        S_["ac"].inc(e.copy(out=gscb[0:8, ts * 512:(ts + 1) * 512], in_=psD[0:8, :]))
                blk.scalar(f_gs)
                ACG_E = S_["ac"].val

                tpg0, evg0 = S_["tp"].val, S_["ev"].val

                def f_gt2(e):
                    e.wait_ge(S_["ac"].h, ACG_E)
                    for rnd in range(2):
                        if rnd == 1:
                            e.wait_ge(S_["ev"].h, evg0 + 1)
                        for c in range(8):
                            ins = e.transpose(psT[:, c * 128:(c + 1) * 128],
                                              gscb[:, rnd * 1024 + c * 128: rnd * 1024 + (c + 1) * 128],
                                              identt[:])
                        S_["tp"].inc(ins)
                blk.tensor(f_gt2)

                def f_gs2(e):
                    for rnd in range(2):
                        e.wait_ge(S_["tp"].h, tpg0 + rnd + 1)
                        S_["ev"].inc(e.copy(out=sc8[:, rnd * 8:(rnd + 1) * 8, :],
                                            in_=psT[:].rearrange("p (c x) -> p c x", c=8)[:, :, 0:8]))
                blk.scalar(f_gs2)
                EV_SC8 = S_["ev"].val

                dvt0, act0 = S_["dv"].val, S_["ac"].val

                def f_top_v(e):
                    e.wait_ge(S_["ev"].h, EV_SC8)
                    SS = lambda tt, k: stats[:, tt, k:k + 1]
                    for tt in range(NT):
                        e.reduce_max(SS(tt, 1), sc8[:, tt, :], axis=AX.X)
                    for tt in range(NT):
                        e.tensor_single_scalar(scrE[:, tt, :], sc8[:, tt, :], SS(tt, 1), ALU.is_equal)
                    for tt in range(NT):
                        e.scalar_tensor_tensor(scrF[:, tt, :], scrE[:, tt, :], -1e9, sc8[:, tt, :],
                                               ALU.mult, ALU.add)
                    for tt in range(NT):
                        e.reduce_max(SS(tt, 2), scrF[:, tt, :], axis=AX.X)
                    for tt in range(NT):
                        S_["dv"].inc(e.tensor_sub(SS(tt, 0), SS(tt, 2), SS(tt, 1)))
                    e.wait_ge(S_["ac"].h, act0 + NT)
                    for tt in range(NT):
                        e.tensor_scalar_add(SS(tt, 3), stats2[:, tt, 0:1], 1.0)
                    for tt in range(NT):
                        e.reciprocal(SS(tt, 4), SS(tt, 3))
                    for tt in range(NT):
                        e.tensor_mul(SS(tt, 5), stats2[:, tt, 0:1], SS(tt, 4))
                    for tt in range(NT):
                        e.tensor_sub(SS(tt, 6), SS(tt, 4), SS(tt, 5))
                    for tt in range(NT):
                        e.tensor_single_scalar(scrF[:, tt, :], sc8[:, tt, :], SS(tt, 2), ALU.is_ge)
                    for tt in range(NT):
                        e.tensor_single_scalar(scrF[:, tt, :], scrF[:, tt, :], SS(tt, 5), ALU.mult)
                    for tt in range(NT):
                        ins = e.scalar_tensor_tensor(
                            wpe[:, tt, 0:8, 0:1].squeeze(), scrE[:, tt, :], SS(tt, 6), scrF[:, tt, :],
                            ALU.mult, ALU.add)
                    S_["dv"].inc(ins, 1)
                blk.vector(f_top_v)

                def f_top_s(e):
                    e.wait_ge(S_["dv"].h, dvt0 + NT)
                    for tt in range(NT):
                        S_["ac"].inc(e.activation(out=stats2[:, tt, 0:1], in_=stats[:, tt, 0:1], func=AF.Exp))
                blk.scalar(f_top_s)
                DV_TOP = S_["dv"].val

                def f_ones2(e):
                    e.wait_ge(S_["dv"].h, DV_TOP)
                    S_["dv"].inc(e.memset(wpe[:, :, 8, 0:1], 1.0))
                blk.vector(f_ones2)
                DV_TOP2 = S_["dv"].val

                MM_LD = [S_["mm"].val]
                for e_i in range(MOE_E + 1):
                    shared = e_i == MOE_E
                    wsrc_g = moe_wsg if shared else moe_wg[e_i]
                    wsrc_u = moe_wsu if shared else moe_wu[e_i]
                    wsrc_d = moe_wsd if shared else moe_wd[e_i]
                    mm0 = S_["mm"].val
                    ac0 = S_["ac"].val
                    dv0 = S_["dv"].val
                    gw0 = S_["gw"].val
                    gslot = [psA[:, 0:512], psA[:, 512:1024]]
                    uslot = [psB[:, 0:512], psB[:, 512:1024]]
                    mm_g = lambda ts, hc, mm0=mm0: mm0 + 36 * ts + 2 * hc + 1
                    mm_u = lambda ts, hc, mm0=mm0: mm0 + 36 * ts + 2 * hc + 2
                    mm_d = lambda ts, tb, mm0=mm0: mm0 + 36 * ts + 33 + tb
                    ac_s = lambda ts, hc, ac0=ac0: ac0 + 16 * ts + hc + 1
                    dv_m = lambda ts, hc, dv0=dv0: dv0 + 20 * ts + hc + 1
                    dv_r = lambda ts, tb, dv0=dv0: dv0 + 20 * ts + 17 + tb
                    def gw_g(ts, hc, gw0=gw0):
                        return gw0 + 512 * ts + 32 * hc + 16 + (16 if ts >= 1 else 0)
                    def gw_u(ts, hc, gw0=gw0):
                        return gw_g(ts, hc) + 16
                    GW_WDE = gw0 + 32 * 16 + 16

                    def f_wl(e, wsrc_g=wsrc_g, wsrc_u=wsrc_u, wsrc_d=wsrc_d,
                             mm_g=mm_g, mm_u=mm_u, e_i=e_i, MM_LDP=MM_LD[0], mm0=mm0):
                        if e_i == 0:
                            e.wait_ge(S_["mm"].h, MM_PRE)
                        for ts in range(4):
                            for hc in range(16):
                                if ts > 0 or hc >= 4:
                                    pts, phc = (ts, hc - 4) if hc >= 4 else (ts - 1, 12 + hc)
                                    e.wait_ge(S_["mm"].h, mm_u(pts, phc))
                                elif e_i > 0:
                                    e.wait_ge(S_["mm"].h, mm0 - 10 + 2 * hc)
                                S_["gw"].inc(e.dma_start(
                                    out=wgh[:, hc % 4, :, :],
                                    in_=wsrc_g[:, hc * 128:(hc + 1) * 128].rearrange("(c p) o -> p c o", p=128)), 16)
                                S_["gw"].inc(e.dma_start(
                                    out=wuh[:, hc % 4, :, :],
                                    in_=wsrc_u[:, hc * 128:(hc + 1) * 128].rearrange("(c p) o -> p c o", p=128)), 16)
                            if ts == 0:
                                e.wait_ge(S_["mm"].h, MM_LDP)
                                S_["gw"].inc(e.dma_start(
                                    out=wde[:], in_=wsrc_d.rearrange("(hc p) o -> p hc o", p=128)), 16)
                    blk.gpsimd(f_wl)

                    def f_et(e, mm0=mm0, ac0=ac0, dv0=dv0, e_i=e_i,
                             mm_g=mm_g, mm_u=mm_u, mm_d=mm_d, ac_s=ac_s, dv_m=dv_m, dv_r=dv_r,
                             gw_g=gw_g, gw_u=gw_u, GW_WDE=GW_WDE):
                        e.wait_ge(S_["dv"].h, dv0 if e_i > 0 else DV_TOP2)
                        for ts in range(4):
                            for hc in range(16):
                                e.wait_ge(S_["gw"].h, gw_g(ts, hc))
                                if ts > 0 or hc >= 2:
                                    pts, phc = (ts, hc - 2) if hc >= 2 else (ts - 1, 14 + hc)
                                    e.wait_ge(S_["ac"].h, ac_s(pts, phc))
                                for c in range(DCH):
                                    ins = e.matmul(gslot[hc % 2], wgh[:, hc % 4, c, :],
                                                   xfm[:, c, ts * 512:(ts + 1) * 512],
                                                   start=(c == 0), stop=(c == DCH - 1))
                                S_["mm"].inc(ins)
                                e.wait_ge(S_["gw"].h, gw_u(ts, hc))
                                if ts > 0 or hc >= 2:
                                    pts, phc = (ts, hc - 2) if hc >= 2 else (ts - 1, 14 + hc)
                                    e.wait_ge(S_["dv"].h, dv_m(pts, phc))
                                for c in range(DCH):
                                    ins = e.matmul(uslot[hc % 2], wuh[:, hc % 4, c, :],
                                                   xfm[:, c, ts * 512:(ts + 1) * 512],
                                                   start=(c == 0), stop=(c == DCH - 1))
                                S_["mm"].inc(ins)
                            if ts == 0:
                                e.wait_ge(S_["gw"].h, GW_WDE)
                            e.wait_ge(S_["dv"].h, dv_m(ts, 15))
                            for tb in range(4):
                                if ts > 0 or tb > 0:
                                    pts, ptb = (ts, tb - 1) if tb > 0 else (ts - 1, 3)
                                    e.wait_ge(S_["dv"].h, dv_r(pts, ptb))
                                for hc in range(16):
                                    e.matmul(psC[:, 0:512], hidb[:, hc, tb * 128:(tb + 1) * 128],
                                             wde[:, hc, 0:512], start=(hc == 0), stop=(hc == 15))
                                    ins = e.matmul(psC[:, 512:1024], hidb[:, hc, tb * 128:(tb + 1) * 128],
                                                   wde[:, hc, 512:1024], start=(hc == 0), stop=(hc == 15))
                                S_["mm"].inc(ins)
                    blk.tensor(f_et)

                    def f_es(e, mm_g=mm_g, dv_m=dv_m):
                        for ts in range(4):
                            for hc in range(16):
                                e.wait_ge(S_["mm"].h, mm_g(ts, hc))
                                if ts > 0 or hc >= 2:
                                    pts, phc = (ts, hc - 2) if hc >= 2 else (ts - 1, 14 + hc)
                                    e.wait_ge(S_["dv"].h, dv_m(pts, phc))
                                S_["ac"].inc(e.activation(out=gsil[:, hc % 2, :], in_=gslot[hc % 2], func=AF.Silu))
                    blk.scalar(f_es)

                    def f_ev(e, mm_u=mm_u, mm_d=mm_d, ac_s=ac_s, e_i=e_i, shared=shared):
                        for ts in range(4):
                            for hc in range(16):
                                e.wait_ge(S_["mm"].h, mm_u(ts, hc))
                                e.wait_ge(S_["ac"].h, ac_s(ts, hc))
                                S_["dv"].inc(e.tensor_mul(hidb[:, hc, :], uslot[hc % 2], gsil[:, hc % 2, :]))
                            for tb in range(4):
                                tile = ts * 4 + tb
                                e.wait_ge(S_["mm"].h, mm_d(ts, tb))
                                wcol = wpe[:, tile, 8 if shared else e_i, 0:1]
                                S_["dv"].inc(e.scalar_tensor_tensor(
                                    x_res[:, tile, :], psC[:], wcol, x_res[:, tile, :],
                                    ALU.mult, ALU.add))
                    blk.vector(f_ev)
                    MM_LD[0] = S_["mm"].val

                if "dbg_wpe" in dbg:
                    dw = dram_dbg("dbg_wpe", [128, NT, 9, 8])
                    dv0 = S_["dv"].val
                    blk.gpsimd(lambda e: (e.wait_ge(S_["dv"].h, dv0),
                                          S_["gst"].inc(e.dma_start(out=dw, in_=wpe[:]), 16)))

        if "l1qkv" in phases:
            with ExitStack() as _es2:
                qfm1g = _es.enter_context(ExitStack())
                qfm1 = qfm1g.enter_context(nc.sbuf_tensor("qfm1", [128, DCH, TOK], BF16))
                w3 = _es2.enter_context(nc.sbuf_tensor("w3", [128, DCH, D], BF16))
                kvo1 = _es2.enter_context(nc.sbuf_tensor("kvo1", [128, 2, 512], BF16))
                vfm1 = _es2.enter_context(nc.sbuf_tensor("vfm1", [128, DCH, 512], BF16))
                vto1 = _es2.enter_context(nc.sbuf_tensor("vto1", [128, 1, VCOL * H], BF16))

                MM_PRE1 = S_["mm"].val
                DV_PRE1 = S_["dv"].val
                ln_sweep([("sbuf", x_res[:, t, :]) for t in range(NT)], xfm, 0,
                         first_evac_wait=(S_["mm"], MM_PRE1))
                EV_LN3 = S_["ev"].val

                def f_b1(e):
                    e.wait_ge(S_["mm"].h, MM_PRE1)
                    e.wait_ge(S_["dv"].h, DV_PRE1)
                    for third in range(3):
                        S_["ldw"].inc(e.dma_start(
                            out=bias1[:, BQK + 8 * third:BQK + 8 * third + 8, 0:1],
                            in_=l1_bqkv[third * D:(third + 1) * D].rearrange("(c p) -> p c", p=128).unsqueeze(2),
                            allow_slow_non_contiguous=True), 16)
                    S_["ldw"].inc(e.dma_start(
                        out=bias1[:, BG1:BG1 + 8, 0:1],
                        in_=l1_bgate.rearrange("(c p) -> p c", p=128).unsqueeze(2),
                        allow_slow_non_contiguous=True), 16)
                blk.sync(f_b1)
                LDW_B1 = S_["ldw"].val
                DV_QB = S_["dv"].val

                def f_qb(e):
                    e.wait_ge(S_["ldw"].h, LDW_B1)
                    S_["dv"].inc(e.tensor_scalar_mul(bias1[:, BQK:BQK + 8, 0:1], bias1[:, BQK:BQK + 8, 0:1], 0.125))
                blk.vector(f_qb)
                DV_QB2 = S_["dv"].val

                def f_w3q(e):
                    e.wait_ge(S_["mm"].h, MM_PRE1)
                    e.wait_ge(S_["dv"].h, DV_PRE1)
                    S_["gw"].inc(e.dma_start(out=w3[:], in_=l1_wqkvT.rearrange("(c p) o -> p c o", p=128)[:, :, 0:D]), 16)
                blk.gpsimd(f_w3q)
                GW_Q1 = S_["gw"].val

                def f_qwait(e):
                    e.wait_ge(S_["dv"].h, DV_QB2)
                blk.scalar(f_qwait)
                std_proj(w3, lambda ts: xfm[:, :, ts * 512:(ts + 1) * 512],
                         lambda oi, ts: qfm1[:, oi, ts * 512:(ts + 1) * 512],
                         list(range(DCH)), list(range(4)), AF.Identity, BQK, scale=0.125,
                         wait_sems=[(S_["ev"], EV_LN3), (S_["gw"], GW_Q1)])
                MM_Q1 = S_["mm"].val

                def f_w3k(e):
                    e.wait_ge(S_["mm"].h, MM_Q1)
                    S_["gw"].inc(e.dma_start(out=w3[:], in_=l1_wqkvT.rearrange("(c p) o -> p c o", p=128)[:, :, D:2 * D]), 16)
                blk.gpsimd(f_w3k)
                GW_K1 = S_["gw"].val

                MM_KDONE = S_["mm"].val
                for qc in range(QC):
                    KV_K0 = S_["kv"].val
                    AC_K0 = S_["ac"].val
                    std_proj(w3, lambda ts, qc=qc: xfm[:, :, qc * 512:(qc + 1) * 512],
                             lambda oi, ts: kvo1[:, oi % 2, :], list(range(DCH)), [0], AF.Identity, BQK + 8,
                             wait_sems=[(S_["gw"], GW_K1)],
                             slot_free=lambda idx, KV_K0=KV_K0: (S_["kv"], KV_K0 + 16 * (idx - 1)))

                    def f_spk1(e, qc=qc, AC_K0=AC_K0):
                        for oc in range(DCH):
                            e.wait_ge(S_["ac"].h, AC_K0 + oc + 1)
                            dst = cc_in[qc, 0:CHK].rearrange("(c p t) -> c p t", c=DCH, p=128)[oc]
                            S_["kv"].inc(e.dma_start(out=dst, in_=kvo1[:, oc % 2, :]), 16)
                    blk.sync(f_spk1)
                MM_KALL = S_["mm"].val

                def f_w3v(e):
                    e.wait_ge(S_["mm"].h, MM_KALL)
                    S_["gw"].inc(e.dma_start(out=w3[:], in_=l1_wqkvT.rearrange("(c p) o -> p c o", p=128)[:, :, 2 * D:3 * D]), 16)
                blk.gpsimd(f_w3v)
                GW_V1 = S_["gw"].val

                first_v = [True]
                for qc in range(QC):
                    std_proj(w3, lambda ts, qc=qc: xfm[:, :, qc * 512:(qc + 1) * 512],
                             lambda oi, ts: vfm1[:, oi, :], list(range(DCH)), [0], AF.Identity, BQK + 16,
                             wait_sems=[(S_["gw"], GW_V1), (S_["kv"], S_["kv"].val)])
                    AC_V1 = S_["ac"].val
                    tp0, ev0 = S_["tp"].val, S_["ev"].val

                    def f_tv1(e, AC_V1=AC_V1, tp0=tp0, ev0=ev0):
                        e.wait_ge(S_["ac"].h, AC_V1)
                        for tt in range(4):
                            e.wait_ge(S_["ev"].h, ev0 + tt)
                            for c in range(DCH):
                                ins = e.transpose(psT[:, c * 128:(c + 1) * 128],
                                                  vfm1[:, c, tt * 128:(tt + 1) * 128], identt[:])
                            S_["tp"].inc(ins)
                    blk.tensor(f_tv1)
                    if first_v[0]:
                        def f_ones1(e):
                            S_["dv"].inc(e.memset(
                                vto1[:, 0, :].rearrange("p (h x) -> p h x", x=VCOL)[:, :, 64:65], 1.0))
                        blk.vector(f_ones1)
                        first_v[0] = False
                    DV_ONES1 = S_["dv"].val
                    kv0v = S_["kv"].val
                    EV_V0 = S_["ev"].val

                    def f_sv1(e, tp0=tp0, kv0v=kv0v):
                        for tt in range(4):
                            e.wait_ge(S_["tp"].h, tp0 + tt + 1)
                            e.wait_ge(S_["kv"].h, kv0v + 16 * tt)
                            dst = vto1[:, 0, :].rearrange("p (h x) -> p h x", x=VCOL)[:, :, 0:64]
                            S_["ev"].inc(e.copy(out=dst, in_=psT[:].rearrange("p (h x) -> p h x", x=64)))
                    blk.scalar(f_sv1)

                    def f_spv1(e, qc=qc, EV_V0=EV_V0, DV_ONES1=DV_ONES1):
                        e.wait_ge(S_["dv"].h, DV_ONES1)
                        for tt in range(4):
                            e.wait_ge(S_["ev"].h, EV_V0 + tt + 1)
                            dst = cc_in[qc, CHK:CHK + CHV].rearrange("(f p t) -> f p t", f=4, p=128)[tt]
                            S_["kv"].inc(e.dma_start(out=dst, in_=vto1[:, 0, :]), 16)
                    blk.sync(f_spv1)

                if "dbg_q1" in dbg:
                    dq1 = dram_dbg("dbg_q1", [128, DCH, TOK], U16)
                    _mmq = S_["mm"].val
                    _acq = S_["ac"].val
                    blk.gpsimd(lambda e: (e.wait_ge(S_["ac"].h, _acq),
                                          S_["gst"].inc(e.dma_start(out=dq1.bitcast(BF16), in_=qfm1[:]), 16)))
                MM_G1 = S_["mm"].val

                def f_wg1(e):
                    e.wait_ge(S_["mm"].h, MM_G1)
                    S_["gw"].inc(e.dma_start(out=w3[:], in_=l1_wgateT.rearrange("(c p) o -> p c o", p=128)), 16)
                blk.gpsimd(f_wg1)
                GW_G1 = S_["gw"].val
                for ts in range(4):
                    KV_G0 = S_["kv"].val
                    AC_G0 = S_["ac"].val
                    std_proj(w3, lambda _ts, ts=ts: xfm[:, :, ts * 512:(ts + 1) * 512],
                             lambda oi, _ts: kvo1[:, oi % 2, :], list(range(DCH)), [0], AF.Sigmoid, BG1,
                             wait_sems=[(S_["gw"], GW_G1)],
                             slot_free=lambda idx, KV_G0=KV_G0: (S_["kv"], KV_G0 + 16 * (idx - 1)))

                    def f_spg(e, ts=ts, AC_G0=AC_G0):
                        for oc in range(DCH):
                            e.wait_ge(S_["ac"].h, AC_G0 + oc + 1)
                            S_["kv"].inc(e.dma_start(out=l1g[oc, :, ts * 512:(ts + 1) * 512],
                                                     in_=kvo1[:, oc % 2, :]), 16)
                    blk.sync(f_spg)

                if "dbg_cc" in dbg:
                    dcc = dram_dbg("dbg_cc", [QC, 2048], U16)
                    _kvc = S_["kv"].val
                    blk.gpsimd(lambda e: (e.wait_ge(S_["kv"].h, _kvc),
                                          S_["gst"].inc(e.dma_start(out=dcc.bitcast(BF16),
                                                                    in_=cc_in[:, 0:2048]), 16)))
                KV_CC = S_["kv"].val

                def f_ag(e):
                    e.wait_ge(S_["kv"].h, KV_CC)
                    e.wait_ge(S_["gz"].h, KV_PAD)
                    S_["cc"].inc(e.collective_compute(
                        "AllGather", ALU.bypass,
                        ins=[cc_in[:]], outs=[cc_out.rearrange("a b c -> (a b) c")],
                        replica_groups=[[0, 1], [2, 3], [4, 5], [6, 7]]), 1)
                blk.gpsimd(f_ag)
                CC_DONE = S_["cc"].val

        if "l1attn" in phases:
            with ExitStack() as _es2:
                kst1 = _es2.enter_context(nc.sbuf_tensor("kst1", [128, 4, 128], BF16))
                vst1 = _es2.enter_context(nc.sbuf_tensor("vst1", [128, 4, 130], BF16))
                btst = _es2.enter_context(nc.sbuf_tensor("btst", [128, 3, 2, 512], BF16))
                pT1 = _es2.enter_context(nc.sbuf_tensor("pT1", [128, 2, 1024], BF16))
                sbcol = _es2.enter_context(nc.sbuf_tensor("sbcol", [128, QC, L1_NJ, 8], F32))
                wog = _es2.enter_context(nc.sbuf_tensor("wog", [128, DCH, D], BF16))
                ctx1 = xfm

                MM_A1 = S_["mm"].val

                def f_sb(e):
                    e.wait_ge(S_["mm"].h, MM_A1)
                    e.wait_ge(S_["cc"].h, CC_DONE)
                    S_["ldw"].inc(e.dma_start(out=sbcol[:], in_=l1_sb[:]), 16)
                blk.sync(f_sb)
                LDW_SB = S_["ldw"].val

                for hs in range(8):
                    for qc in range(QC):
                        nj = L1_NJ
                        ldk0, mm0, ac0, dv0 = (S_[k].val for k in ("ldk", "mm", "ac", "dv"))
                        QK1 = lambda j, mm0=mm0: mm0 + 1 if j == 0 else mm0 + 2 * j
                        AV1 = lambda j, mm0=mm0, nj=nj: mm0 + 2 * j + (3 if j < nj - 1 else 2)

                        gw0b = S_["gw"].val

                        def f_dma1(e, qc=qc, nj=nj, mm0=mm0, hs=hs, AV1=AV1):
                            for j in range(nj):
                                if j >= 4:
                                    e.wait_ge(S_["mm"].h, AV1(j - 4))
                                else:
                                    e.wait_ge(S_["mm"].h, mm0)
                                b13 = 8 * qc + j - 13
                                g, tt = b13 // 4, b13 % 4
                                if g < 0:
                                    ksrc = l1pad[(g % 4), 0:CHK].rearrange("(c p t) -> c p t", c=DCH, p=128)
                                    vsrc = l1pad[(g % 4), CHK:CHK + CHV].rearrange("(f p t) -> f p t", f=4, p=128)
                                else:
                                    ksrc = cc_out[g % 2, g // 2, 0:CHK].rearrange("(c p t) -> c p t", c=DCH, p=128)
                                    vsrc = cc_out[g % 2, g // 2, CHK:CHK + CHV].rearrange("(f p t) -> f p t", f=4, p=128)
                                S_["ldk"].inc(e.dma_start(out=kst1[:, j % 4, :],
                                                          in_=ksrc[hs, :, tt * 128:(tt + 1) * 128]), 16)
                                S_["ldk"].inc(e.dma_start(out=vst1[:, j % 4, :],
                                                          in_=vsrc[tt, :, 130 * hs:130 * hs + 130]), 16)
                        blk.sync(f_dma1)

                        def f_bdma1(e, qc=qc, nj=nj, mm0=mm0, hs=hs, AV1=AV1):
                            for j in range(nj):
                                if j >= 3:
                                    e.wait_ge(S_["mm"].h, AV1(j - 3))
                                else:
                                    e.wait_ge(S_["mm"].h, mm0)
                                S_["gw"].inc(e.dma_start(out=btst[:, j % 3, :, :],
                                                          in_=l1_btile[j, 2 * hs:2 * hs + 2].rearrange("h p t -> p h t").bitcast(BF16)), 16)
                        blk.gpsimd(f_bdma1)

                        def f_pe1(e, qc=qc, nj=nj, ldk0=ldk0, mm0=mm0, ac0=ac0, dv0=dv0, hs=hs):
                            if hs == 0 and qc == 0:
                                e.wait_ge(S_["ldw"].h, LDW_SB)

                            def emit_qk(j):
                                strip = (psA, psB)[j % 2]
                                e.wait_ge(S_["ldk"].h, ldk0 + 32 * j + 32)
                                e.wait_ge(S_["gw"].h, gw0b + 16 * (j + 1))
                                if j >= 2:
                                    e.wait_ge(S_["ac"].h, ac0 + j - 1)
                                elif not (hs == 0 and qc == 0):
                                    e.wait_ge(S_["ac"].h, ac0 - 2)
                                for hp in range(2):
                                    e.matmul(strip[:, hp * 512:(hp + 1) * 512],
                                             kst1[hp * 64:(hp + 1) * 64, j % 4, :],
                                             qfm1[hp * 64:(hp + 1) * 64, hs, qc * 512:(qc + 1) * 512],
                                             start=True, stop=False)
                                    ins = e.matmul(strip[:, hp * 512:(hp + 1) * 512],
                                                   identt[:], btst[:, j % 3, hp, :],
                                                   start=False, stop=True)
                                S_["mm"].inc(ins)

                            def emit_av(j):
                                e.wait_ge(S_["ac"].h, ac0 + j + 1)
                                if j == 0:
                                    e.wait_ge(S_["dv"].h, dv0)
                                for hp in range(2):
                                    avp = psC[0:65, hp * 512:(hp + 1) * 512]
                                    ins = e.matmul(avp, vst1[:, j % 4, hp * 65:(hp + 1) * 65],
                                                   pT1[:, j % 2, hp * 512:(hp + 1) * 512],
                                                   start=(j == 0), stop=(j == nj - 1))
                                S_["mm"].inc(ins)

                            emit_qk(0)
                            for j in range(1, nj):
                                emit_qk(j)
                                emit_av(j - 1)
                            emit_av(nj - 1)
                        blk.tensor(f_pe1)

                        def f_exp1(e, qc=qc, nj=nj, mm0=mm0, QK1=QK1, AV1=AV1):
                            for j in range(nj):
                                strip = (psA, psB)[j % 2]
                                e.wait_ge(S_["mm"].h, QK1(j))
                                S_["ac"].inc(e.activation(out=pT1[:, j % 2, :], in_=strip[:],
                                                          func=AF.Exp, bias=sbcol[:, qc, j, 0:1]))
                        blk.scalar(f_exp1)

                        MM_AVE1 = S_["mm"].val

                        def f_nv1(e, MM_AVE1=MM_AVE1):
                            e.wait_ge(S_["mm"].h, MM_AVE1)
                            S_["dv"].inc(e.reciprocal(rrow[0:1, 0:512], psC[64:65, 0:512]))
                            S_["dv"].inc(e.reciprocal(rrow[0:1, 512:1024], psC[64:65, 512:1024]))
                        blk.vector(f_nv1)
                        DV_R1 = S_["dv"].val
                        mm1 = S_["mm"].val

                        def f_nt1(e, DV_R1=DV_R1):
                            e.wait_ge(S_["dv"].h, DV_R1)
                            rbdst = (psD[0:64, :], psD[64:128, :])
                            for hp in range(2):
                                S_["mm"].inc(e.matmul(rbdst[hp], onesf[0:1, :],
                                                      rrow[0:1, 512 * hp:512 * hp + 512], start=True, stop=True))
                        blk.tensor(f_nt1)
                        AC_RB1 = S_["ac"].val

                        def f_nrb1(e, mm1=mm1):
                            rbdst = (psD[0:64, :], psD[64:128, :])
                            for hp in range(2):
                                e.wait_ge(S_["mm"].h, mm1 + hp + 1)
                                S_["ac"].inc(e.copy(out=scr2[0:64, 512 * hp:512 * hp + 512], in_=rbdst[hp]))
                        blk.scalar(f_nrb1)

                        def f_nv21(e, qc=qc, hs=hs, AC_RB1=AC_RB1):
                            for hp in range(2):
                                e.wait_ge(S_["ac"].h, AC_RB1 + hp + 1)
                                S_["dv"].inc(e.tensor_mul(
                                    ctx1[hp * 64:(hp + 1) * 64, hs, qc * 512:(qc + 1) * 512],
                                    psC[0:64, hp * 512:(hp + 1) * 512],
                                    scr2[0:64, 512 * hp:512 * hp + 512]))
                        blk.vector(f_nv21)

                DV_CTX1 = S_["dv"].val
                if "dbg_ctx1" in dbg:
                    dc1 = dram_dbg("dbg_ctx1", [128, DCH, TOK], U16)
                    blk.gpsimd(lambda e: (e.wait_ge(S_["dv"].h, DV_CTX1),
                                          S_["gst"].inc(e.dma_start(out=dc1.bitcast(BF16), in_=ctx1[:]), 16)))

                def f_wgl(e):
                    e.wait_ge(S_["mm"].h, MM_A1)
                    S_["gw"].inc(e.dma_start(out=wog[:], in_=l1_woutT.rearrange("(c p) o -> p c o", p=128)), 16)
                blk.gpsimd(f_wgl)
                GW_WOUT = S_["gw"].val
                ldk0g = S_["ldk"].val
                dv0g = S_["dv"].val

                def f_gld(e, ldk0g=ldk0g, dv0g=dv0g):
                    for t in range(NT):
                        if t == 0:
                            e.wait_ge(S_["dv"].h, DV_CTX1)
                        if t >= 2:
                            e.wait_ge(S_["dv"].h, dv0g + t - 1)
                        S_["ldk"].inc(e.dma_start(
                            out=btst[:, t % 2, :, :].rearrange("p a b -> p (a b)").rearrange("p (c x) -> p c x", c=8),
                            in_=l1g[:, :, t * 128:(t + 1) * 128].rearrange("c p t -> p c t")), 16)
                blk.sync(f_gld)

                def f_gmul(e, ldk0g=ldk0g, dv0g=dv0g):
                    for t in range(NT):
                        e.wait_ge(S_["ldk"].h, ldk0g + 16 * (t + 1))
                        S_["dv"].inc(e.tensor_mul(
                            ctx1[:, :, t * 128:(t + 1) * 128],
                            ctx1[:, :, t * 128:(t + 1) * 128],
                            btst[:, t % 2, :, :].rearrange("p a b -> p (a b)").rearrange("p (c x) -> p c x", c=8)))
                blk.vector(f_gmul)
                DV_GM = S_["dv"].val

                swap_proj_residual(ctx1, wog, DCH, 1, AF.Identity,
                                   wait_sems=[(S_["dv"], DV_GM), (S_["gw"], GW_WOUT)])
            qfm1g.close()

        if "l1ffn" in phases:
            with ExitStack() as _es2:
                h1 = _es2.enter_context(nc.sbuf_tensor("h1", [128, FCH, 512], BF16))
                cup = _es2.enter_context(nc.sbuf_tensor("cup", [128, FCH, 128], BF16))
                cdn = _es2.enter_context(nc.sbuf_tensor("cdn", [128, FCH, 128], BF16))
                mstg = _es2.enter_context(nc.sbuf_tensor("mstg", [128, 7, 128], BF16))
                xsc = _es2.enter_context(nc.sbuf_tensor("xsc", [128, 128], BF16))
                btmp = _es2.enter_context(nc.sbuf_tensor("btmp", [128, 4, 3, 512], BF16))
                wexh = _es2.enter_context(nc.sbuf_tensor("wexh", [128, 4, DCH, 128], BF16))
                wconh = _es2.enter_context(nc.sbuf_tensor("wconh", [128, 4, D], BF16))
                bwal = _es2.enter_context(nc.sbuf_tensor("bwal", [128, 2, 5, 16, 2, 2], F32))

                MM_F0 = S_["mm"].val
                def f_fb(e):
                    e.wait_ge(S_["mm"].h, MM_F0)
                    S_["ldw"].inc(e.dma_start(out=bias1[:, 0:32, 0:1],
                                              in_=ff_bexp.rearrange("(c p) -> p c", p=128).unsqueeze(2),
                                              allow_slow_non_contiguous=True), 16)
                    S_["ldw"].inc(e.dma_start(out=bias1[:, 32:64, 0:1],
                                              in_=ff_upb.rearrange("(c p) -> p c", p=128).unsqueeze(2),
                                              allow_slow_non_contiguous=True), 16)
                    for ud, wsrc in ((0, ff_upw), (1, ff_dnw)):
                        for s5 in range(5):
                            sa = s5 + 7
                            for ai, ch in enumerate([c for c in range(FCH) if not (c >> (sa - 7)) & 1]):
                                kbase = (ch >> (sa - 6)) * (2 ** sa) + (ch % (2 ** (sa - 7))) * 128
                                S_["ldw"].inc(e.dma_start(
                                    out=bwal[:, ud, s5, ai, :, :],
                                    in_=wsrc[sa, kbase:kbase + 128, :, :],
                                    allow_slow_non_contiguous=True), 16)
                blk.sync(f_fb)
                LDW_FB = S_["ldw"].val

                def compose(c_dst, stsrc, tag):
                    ldk0, tp0, ev0, mm0, ac0 = (S_[k].val for k in ("ldk", "tp", "ev", "mm", "ac"))

                    def f_cd(e, ldk0=ldk0, ac0=ac0):
                        e.wait_ge(S_["mm"].h, MM_F0)
                        for ch in range(FCH):
                            if ch >= 1:
                                e.wait_ge(S_["ac"].h, ac0 + 7 * ch)
                            S_["gz"].inc(e.dma_start(out=mstg[:],
                                                      in_=stsrc[:, ch, :, :].rearrange("s p t -> p s t")), 16)
                    blk.gpsimd(f_cd)

                    gz0 = S_["gz"].val - 16 * FCH

                    def f_ct(e, gz0=gz0, tp0=tp0, ev0=ev0, mm0=mm0, ac0=ac0):
                        for ch in range(FCH):
                            e.wait_ge(S_["gz"].h, gz0 + 16 * (ch + 1))
                            if ch >= 1:
                                e.wait_ge(S_["ac"].h, ac0 + 7 * ch)
                            ins = e.transpose(psT[:, 0:128], mstg[:, 6, :], identt[:])
                            S_["tp"].inc(ins)
                            for s in range(5, -1, -1):
                                e.wait_ge(S_["ac"].h, ac0 + 7 * ch + (6 - s))
                                ins = e.matmul(psA[:, 0:128], mstg[:, s, :], xsc[:], start=True, stop=True)
                                S_["mm"].inc(ins)
                    blk.tensor(f_ct)

                    def f_cs(e, tp0=tp0, mm0=mm0, ac0=ac0, c_dst=c_dst):
                        for ch in range(FCH):
                            e.wait_ge(S_["tp"].h, tp0 + ch + 1)
                            S_["ac"].inc(e.copy(out=xsc[:], in_=psT[:, 0:128]))
                            for s in range(5, -1, -1):
                                e.wait_ge(S_["mm"].h, mm0 + 6 * ch + (6 - s))
                                dst = xsc[:] if s > 0 else c_dst[:, ch, :]
                                S_["ac"].inc(e.copy(out=dst, in_=psA[:, 0:128]))
                    blk.scalar(f_cs)
                compose(cup, ff_upT, "up")
                compose(cdn, ff_dnT, "dn")
                LDW_C = S_["ldw"].val

                MM_PRE4 = S_["mm"].val
                ln_sweep([("sbuf", x_res[:, t, :]) for t in range(NT)], xfm, 0,
                         first_evac_wait=(S_["mm"], MM_PRE4))
                EV_LN4 = S_["ev"].val

                slots = [psA[:, 0:512], psA[:, 512:1024], psB[:, 0:512], psB[:, 512:1024]]

                def bfly_pairs():
                    out = []
                    for s5 in range(5):
                        step = 2 ** s5
                        achs = [c for c in range(FCH) if not (c >> s5) & 1]
                        for ai, ch in enumerate(achs):
                            out.append((ch, ch + step, s5, ai))
                    return out

                def butterfly_large(ts, ud, cols):
                    pairs = bfly_pairs()
                    dvb = S_["dv"].val
                    acb = S_["ac"].val
                    lk0 = S_["ldk"].val

                    def f_a(e, pairs=pairs, dvb=dvb, ud=ud):
                        e.wait_ge(S_["dv"].h, dvb)
                        for q, (a, b, s5, ai) in enumerate(pairs):
                            u = q % 4
                            if q >= 4:
                                e.wait_ge(S_["dv"].h, dvb + 4 * (q // 2 - 1))
                            if q % 16 < 2 and q >= 16:
                                e.wait_ge(S_["dv"].h, dvb + 32 * (q // 16))
                            S_["ac"].inc(e.activation(out=btmp[:, u, 1, :],
                                                      in_=h1[:, b, :], func=AF.Identity,
                                                      scale=cols[:, ud, s5, ai, 0, 1:2]))
                            S_["ac"].inc(e.activation(out=btmp[:, u, 2, :],
                                                      in_=h1[:, b, :], func=AF.Identity,
                                                      scale=cols[:, ud, s5, ai, 1, 1:2]))
                    blk.scalar(f_a)

                    def f_cp(e, pairs=pairs, dvb=dvb, acb=acb, ud=ud):
                        e.wait_ge(S_["dv"].h, dvb)
                        e.wait_ge(S_["ac"].h, acb)
                        for q, (a, b, s5, ai) in enumerate(pairs):
                            u = q % 4
                            if q >= 4:
                                e.wait_ge(S_["dv"].h, dvb + 4 * (q // 2 - 1))
                            if q % 16 < 2 and q >= 16:
                                e.wait_ge(S_["dv"].h, dvb + 32 * (q // 16))
                            S_["ldk"].inc(e.dma_start(out=btmp[:, u, 0, :], in_=h1[:, a, :]), 16)
                    blk.sync(f_cp)

                    def f_v(e, pairs=pairs, acb=acb, lk0=lk0, ud=ud):
                        for g in range(len(pairs) // 2):
                            e.wait_ge(S_["ac"].h, acb + 4 * (g + 1))
                            e.wait_ge(S_["ldk"].h, lk0 + 16 * (2 * g + 2))
                            P = (pairs[2 * g], pairs[2 * g + 1])
                            U = ((2 * g) % 4, (2 * g + 1) % 4)
                            for u in range(2):
                                a, b, s5, ai = P[u]
                                S_["dv"].inc(e.scalar_tensor_tensor(
                                    h1[:, a, :], btmp[:, U[u], 0, :], cols[:, ud, s5, ai, 0, 0:1],
                                    btmp[:, U[u], 1, :], ALU.mult, ALU.add))
                            for u in range(2):
                                a, b, s5, ai = P[u]
                                S_["dv"].inc(e.scalar_tensor_tensor(
                                    h1[:, b, :], btmp[:, U[u], 0, :], cols[:, ud, s5, ai, 1, 0:1],
                                    btmp[:, U[u], 2, :], ALU.mult, ALU.add))
                    blk.vector(f_v)

                for ts in range(4):
                    mm0, ac0, gw0, lk0 = S_["mm"].val, S_["ac"].val, S_["gw"].val, S_["ldk"].val

                    def xw_gate(e, oc, mm0):
                        if oc < 4:
                            e.wait_ge(S_["mm"].h, mm0)
                        else:
                            e.wait_ge(S_["mm"].h, mm0 + (oc - 4) + 1)

                    def f_xw_g(e, ts=ts, mm0=mm0):
                        for oc in range(FCH):
                            xw_gate(e, oc, mm0)
                            S_["gw"].inc(e.dma_start(
                                out=wexh[:, oc % 4, :, :],
                                in_=ff_wexp[:, oc * 128:(oc + 1) * 128].rearrange("(c p) o -> p c o", p=128)), 16)
                    blk.gpsimd(f_xw_g)

                    DV_PREV = S_["dv"].val

                    def f_xt(e, ts=ts, mm0=mm0, ac0=ac0, gw0=gw0, lk0=lk0, DV_PREV=DV_PREV):
                        if ts == 0:
                            e.wait_ge(S_["ev"].h, EV_LN4)
                            e.wait_ge(S_["ldw"].h, LDW_C)
                        e.wait_ge(S_["dv"].h, DV_PREV)
                        for oc in range(FCH):
                            e.wait_ge(S_["gw"].h, gw0 + 16 * (oc + 1))
                            if oc >= 4:
                                e.wait_ge(S_["ac"].h, ac0 + oc - 3)
                            for c in range(DCH):
                                ins = e.matmul(slots[oc % 4], wexh[:, oc % 4, c, :],
                                               xfm[:, c, ts * 512:(ts + 1) * 512],
                                               start=(c == 0), stop=(c == DCH - 1))
                            S_["mm"].inc(ins)
                    blk.tensor(f_xt)

                    def f_xs(e, mm0=mm0, ac0=ac0):
                        for oc in range(FCH):
                            e.wait_ge(S_["mm"].h, mm0 + oc + 1)
                            S_["ac"].inc(e.activation(out=h1[:, oc, :], in_=slots[oc % 4],
                                                      func=AF.Identity, bias=bias1[:, oc, 0:1]))
                    blk.scalar(f_xs)
                    AC_EXP = S_["ac"].val

                    mm1 = S_["mm"].val
                    ac1 = S_["ac"].val

                    def f_bu_t(e, mm1=mm1, ac1=ac1, AC_EXP=AC_EXP):
                        e.wait_ge(S_["ac"].h, AC_EXP)
                        for ch in range(FCH):
                            if ch >= 4:
                                e.wait_ge(S_["ac"].h, ac1 + ch - 3)
                            ins = e.matmul(slots[ch % 4], cup[:, ch, :], h1[:, ch, :], start=True, stop=True)
                            S_["mm"].inc(ins)
                    blk.tensor(f_bu_t)

                    def f_bu_s(e, mm1=mm1):
                        for ch in range(FCH):
                            e.wait_ge(S_["mm"].h, mm1 + ch + 1)
                            S_["ac"].inc(e.copy(out=h1[:, ch, :], in_=slots[ch % 4]))
                    blk.scalar(f_bu_s)
                    if ts == 0:
                        blk.scalar(lambda e: e.wait_ge(S_["ldw"].h, LDW_FB))
                    butterfly_large(ts, 0, bwal)
                    DV_BU = S_["dv"].val

                    ac2 = S_["ac"].val

                    def f_gel(e, DV_BU=DV_BU):
                        e.wait_ge(S_["dv"].h, DV_BU)
                        for ch in range(FCH):
                            S_["ac"].inc(e.activation(out=h1[:, ch, :], in_=h1[:, ch, :],
                                                      func=AF.Gelu, bias=bias1[:, 32 + ch, 0:1]))
                    blk.scalar(f_gel)
                    AC_GEL = S_["ac"].val

                    mm2 = S_["mm"].val
                    ac3 = S_["ac"].val

                    def f_bd_t(e, mm2=mm2, ac3=ac3, AC_GEL=AC_GEL):
                        e.wait_ge(S_["ac"].h, AC_GEL)
                        for ch in range(FCH):
                            if ch >= 4:
                                e.wait_ge(S_["ac"].h, ac3 + ch - 3)
                            ins = e.matmul(slots[ch % 4], cdn[:, ch, :], h1[:, ch, :], start=True, stop=True)
                            S_["mm"].inc(ins)
                    blk.tensor(f_bd_t)

                    def f_bd_s(e, mm2=mm2):
                        for ch in range(FCH):
                            e.wait_ge(S_["mm"].h, mm2 + ch + 1)
                            S_["ac"].inc(e.copy(out=h1[:, ch, :], in_=slots[ch % 4]))
                    blk.scalar(f_bd_s)
                    butterfly_large(ts, 1, bwal)
                    DV_BD = S_["dv"].val

                    for rnd in range(2):
                        mm3, gw3, dv3 = S_["mm"].val, S_["gw"].val, S_["dv"].val
                        pss = (psC, psB)

                        def f_cw(e, ts=ts, rnd=rnd, mm3=mm3, DV_BD=DV_BD):
                            for hc in range(FCH):
                                if hc >= 4:
                                    e.wait_ge(S_["mm"].h, mm3 + (hc - 4) + 1)
                                else:
                                    e.wait_ge(S_["mm"].h, mm3)
                                S_["gw"].inc(e.dma_start(
                                    out=wconh[:, hc % 4, :],
                                    in_=ff_wcon[hc * 128:(hc + 1) * 128, :]), 16)
                        blk.gpsimd(f_cw)

                        def f_c_t(e, ts=ts, rnd=rnd, mm3=mm3, gw3=gw3, dv3=dv3, DV_BD=DV_BD):
                            e.wait_ge(S_["dv"].h, DV_BD)
                            if rnd == 1:
                                e.wait_ge(S_["dv"].h, dv3)
                            for hc in range(FCH):
                                e.wait_ge(S_["gw"].h, gw3 + 16 * (hc + 1))
                                for t2 in range(2):
                                    tt = ts * 4 + rnd * 2 + t2
                                    lt = tt % 4
                                    e.matmul(pss[t2][:, 0:512], h1[:, hc, lt * 128:(lt + 1) * 128],
                                             wconh[:, hc % 4, 0:512], start=(hc == 0), stop=(hc == FCH - 1))
                                    ins = e.matmul(pss[t2][:, 512:1024], h1[:, hc, lt * 128:(lt + 1) * 128],
                                                   wconh[:, hc % 4, 512:1024], start=(hc == 0), stop=(hc == FCH - 1))
                                S_["mm"].inc(ins, 1)
                        blk.tensor(f_c_t)
                        MM_C = S_["mm"].val

                        def f_c_v(e, ts=ts, rnd=rnd, MM_C=MM_C):
                            e.wait_ge(S_["mm"].h, MM_C)
                            for t2 in range(2):
                                tt = ts * 4 + rnd * 2 + t2
                                e.tensor_add(scr2[:], pss[t2][:], bcast[:, 2, :])
                                S_["dv"].inc(e.tensor_add(x_res[:, tt, :], x_res[:, tt, :], scr2[:]))
                        blk.vector(f_c_v)

        if "dbg_res2" in dbg:
            dr2 = dram_dbg("dbg_res2", [128, NT, D])
            dv0 = S_["dv"].val
            blk.gpsimd(lambda e: (e.wait_ge(S_["dv"].h, dv0),
                                  S_["gst"].inc(e.dma_start(out=dr2, in_=x_res[:]), 16)))

        if "dbg_res1" in dbg:
            dr = dram_dbg("dbg_res1", [128, NT, D])
            dv0 = S_["dv"].val
            blk.gpsimd(lambda e: (e.wait_ge(S_["dv"].h, dv0),
                                  S_["gst"].inc(e.dma_start(out=dr, in_=x_res[:]), 16)))

        DV_END = S_["dv"].val
        MM_END = S_["mm"].val
        AC_END = S_["ac"].val

        def f_out(e):
            e.wait_ge(S_["dv"].h, DV_END)
            e.wait_ge(S_["mm"].h, MM_END)
            e.wait_ge(S_["ac"].h, AC_END)
            S_["st"].inc(e.dma_start(out=out_own.rearrange("(t p) d -> p t d", p=128), in_=x_res[:]), 16)
        blk.sync(f_out)
        ST_END = S_["st"].val
        blk.gpsimd(lambda e: (e.wait_ge(S_["st"].h, ST_END), e.wait_ge(S_["gst"].h, S_["gst"].val)))

    return nc, inputs, dbg_names


OFFSETS = sorted(set(range(0, 65)) | {96, 128, 192, 256, 384, 512, 768, 1024, 1536})


def own_rows(core):
    seq, r = core // 2, core % 2
    rows = []
    for qc in range(QC):
        g = 2 * qc + r
        rows.append(np.arange(seq * S + g * 512, seq * S + g * 512 + 512))
    return np.concatenate(rows)


def make_winmask(r):
    w = np.arange(8)[:, None, None]
    p = np.arange(128)[None, :, None]
    f = np.arange(512)[None, None, :]
    return ((w * 128 + p) <= (r * 512 + f)).astype(bfd).view(np.uint16)


def make_l1_btile(r, pos_bias):
    o2idx = -np.ones(4096, dtype=np.int64)
    for i, o in enumerate(OFFSETS):
        o2idx[o] = i
    jj = np.arange(L1_NJ)[:, None, None]
    p = np.arange(128)[None, :, None]
    f = np.arange(512)[None, None, :]
    o = (r * 512 + (13 - jj) * 128) + f - p
    valid = (o >= 0) & (o <= 1536)
    idx = np.where(valid, o2idx[np.clip(o, 0, 4095)], -1)
    valid = idx >= 0
    pb = pos_bias[np.clip(idx, 0, 73), :]
    out = np.where(valid[..., None], pb, np.float32(-1e9)).astype(bfd)
    return np.ascontiguousarray(out.transpose(0, 3, 1, 2)).view(np.uint16)


def make_l1_sb(r):
    out = np.full((QC, L1_NJ, 8, 128), -1e9, dtype=np.float32)
    for qc in range(QC):
        g = 2 * qc + r
        for jj in range(L1_NJ):
            ab = 8 * qc + jj - 13
            if ab < 4 * g - 13 - 1 or ab > 4 * g + 3:
                pass
            kpos = ab * 128 + np.arange(128)
            ok = (kpos >= 0) & (ab >= 4 * g - 12) & (ab <= 4 * g + 3)
            out[qc, jj, 0, :] = np.where(ok, np.float32(-4.0), np.float32(-1e9))
    return np.ascontiguousarray(out.transpose(3, 0, 1, 2))


def scatter_stageT(wts):
    out = np.zeros((7, FF_H // 128, 128, 128), dtype=np.float32)
    for st in range(7):
        s = 2 ** st
        k = np.arange(FF_H // 2)
        i = (k // s) * 2 * s + (k % s)
        j = i + s
        w = wts[st]
        ch = i // 128
        il, jl = i % 128, j % 128
        assert np.all(j // 128 == ch)
        out[st, ch, il, il] = w[:, 0, 0]
        out[st, ch, il, jl] = w[:, 0, 1]
        out[st, ch, jl, il] = w[:, 1, 0]
        out[st, ch, jl, jl] = w[:, 1, 1]
    return out


def core_inputs(core, inp):
    seq, r = core // 2, core % 2
    x = np.asarray(inp["L0_attn_states"]).reshape(B * S, D)
    rows = own_rows(core)
    d = {
        "x_own": np.ascontiguousarray(x[rows]),
        "x_seq": np.ascontiguousarray(x[seq * S:(seq + 1) * S]),
        "l0_wq": np.asarray(inp["L0_attn_Wq"]), "l0_bq": np.asarray(inp["L0_attn_bq"]),
        "l0_wk": np.asarray(inp["L0_attn_Wk"]), "l0_bk": np.asarray(inp["L0_attn_bk"]),
        "l0_wv": np.asarray(inp["L0_attn_Wv"]), "l0_bv": np.asarray(inp["L0_attn_bv"]),
        "l0_wo": np.asarray(inp["L0_attn_Wo"]), "l0_bo": np.asarray(inp["L0_attn_bo"]),
        "l0_winmask": make_winmask(r),
        "moe_gw": np.asarray(inp["L0_ffn_gate_w"]),
        "moe_wg": np.asarray(inp["L0_ffn_w_gate"]),
        "moe_wu": np.asarray(inp["L0_ffn_w_up"]),
        "moe_wd": np.asarray(inp["L0_ffn_w_down"]),
        "moe_wsg": np.asarray(inp["L0_ffn_ws_gate"]),
        "moe_wsu": np.asarray(inp["L0_ffn_ws_up"]),
        "moe_wsd": np.asarray(inp["L0_ffn_ws_down"]),
        "l1_wqkvT": np.ascontiguousarray(np.asarray(inp["L1_attn_Wqkv"]).T), "l1_bqkv": np.asarray(inp["L1_attn_bqkv"]),
        "l1_wgateT": np.ascontiguousarray(np.asarray(inp["L1_attn_Wgate"]).T), "l1_bgate": np.asarray(inp["L1_attn_bgate"]),
        "l1_woutT": np.ascontiguousarray(np.asarray(inp["L1_attn_Wout"]).T), "l1_bout": np.asarray(inp["L1_attn_bout"]),
        "l1_btile": make_l1_btile(r, np.asarray(inp["L1_attn_pos_bias"])),
        "l1_sb": make_l1_sb(r),
        "ff_wexp": np.asarray(inp["L1_ffn_W_exp"]), "ff_bexp": np.asarray(inp["L1_ffn_b_exp"]),
        "ff_upT": scatter_stageT(np.asarray(inp["L1_ffn_up_weights"])),
        "ff_dnT": scatter_stageT(np.asarray(inp["L1_ffn_down_weights"])),
        "ff_upw": np.asarray(inp["L1_ffn_up_weights"]),
        "ff_dnw": np.asarray(inp["L1_ffn_down_weights"]),
        "ff_upb": np.asarray(inp["L1_ffn_up_bias"]),
        "ff_wcon": np.asarray(inp["L1_ffn_W_con"]),
        "ff_bsum": np.stack([np.asarray(inp["L1_ffn_b_con"]), np.asarray(inp["L1_ffn_down_bias"])]),
    }
    return d


def assemble_output(results):
    out = np.zeros((B * S, D), dtype=np.float32)
    for core in range(8):
        out[own_rows(core)] = results[core]["out_own"]
    return out.reshape(B, S, D)


from concourse.bass_utils import run_bass_kernel_spmd

_CACHE = {}


def kernel(**inputs):
    if "nc" not in _CACHE:
        nc, inspecs, _ = build()
        _CACHE["nc"] = (nc, inspecs)
    nc, inspecs = _CACHE["nc"]
    in_maps = []
    for c in range(8):
        m = core_inputs(c, inputs)
        in_maps.append({k: m[k] for k in inspecs})
    res = run_bass_kernel_spmd(nc, in_maps, core_ids=list(range(8)))
    return assemble_output(res.results)


# version 09
# speedup vs baseline: 1.6412x; 1.0784x over previous
import numpy as np
import ml_dtypes
import concourse.bass as bass
import concourse.mybir as mybir

F32 = mybir.dt.float32
BF16 = mybir.dt.bfloat16
AF = mybir.ActivationFunctionType
ALU = mybir.AluOpType
AX = mybir.AxisListType
bfd = ml_dtypes.bfloat16
U16 = mybir.dt.uint16

B, S, D = 4, 4096, 1024
H, HD = 16, 64
TOK, NT, QC, NG = 2048, 16, 4, 8
DCH = 8
MOE_D, MOE_E = 2048, 8
FF_H, FCH = 4096, 32
L1_NJ = 21
VCOL = 65
PH_ALL = ("l0qkv", "l0attn", "moe", "l1qkv", "l1attn", "l1ffn")


class Sem:
    def __init__(self, nc, name):
        self.h = nc.alloc_semaphore(name)
        self.val = 0

    def inc(self, ins, n=1):
        ins.then_inc(self.h, n)
        self.val += n
        return ins


def build(phases=PH_ALL, dbg=()):
    phases = list(phases)
    dbg = set(dbg)
    nc = bass.Bass("TRN2", target_bir_lowering=False, detect_race_conditions=False)
    inputs = {}
    dbg_names = []

    def dram_in(name, shape, dtype=F32):
        inputs[name] = (tuple(shape), dtype)
        return nc.dram_tensor(name, list(shape), dtype, kind="ExternalInput").ap()

    def dram_dbg(name, shape, dtype=F32):
        dbg_names.append(name)
        return nc.dram_tensor(name, list(shape), dtype, kind="ExternalOutput").ap()

    x_own = dram_in("x_own", [TOK, D])
    x_seq = dram_in("x_seq", [S, D])
    l0_wq = dram_in("l0_wq", [D, D]); l0_bq = dram_in("l0_bq", [D])
    l0_wk = dram_in("l0_wk", [D, D]); l0_bk = dram_in("l0_bk", [D])
    l0_wv = dram_in("l0_wv", [D, D]); l0_bv = dram_in("l0_bv", [D])
    l0_wo = dram_in("l0_wo", [D, D]); l0_bo = dram_in("l0_bo", [D])
    l0_winmask = dram_in("l0_winmask", [8, 128, 512], U16)
    moe_gw = dram_in("moe_gw", [D, MOE_E])
    moe_wg = dram_in("moe_wg", [MOE_E, D, MOE_D])
    moe_wu = dram_in("moe_wu", [MOE_E, D, MOE_D])
    moe_wd = dram_in("moe_wd", [MOE_E, MOE_D, D])
    moe_wsg = dram_in("moe_wsg", [D, MOE_D])
    moe_wsu = dram_in("moe_wsu", [D, MOE_D])
    moe_wsd = dram_in("moe_wsd", [MOE_D, D])
    l1_wqkvT = dram_in("l1_wqkvT", [D, 3 * D]); l1_bqkv = dram_in("l1_bqkv", [3 * D])
    l1_wgateT = dram_in("l1_wgateT", [D, D]); l1_bgate = dram_in("l1_bgate", [D])
    l1_woutT = dram_in("l1_woutT", [D, D]); l1_bout = dram_in("l1_bout", [D])
    l1_btile = dram_in("l1_btile", [L1_NJ, H, 128, 512], U16)
    l1_sb = dram_in("l1_sb", [128, QC, L1_NJ, 8])
    ff_wexp = dram_in("ff_wexp", [D, FF_H]); ff_bexp = dram_in("ff_bexp", [FF_H])
    ff_upT = dram_in("ff_upT", [7, FCH, 128, 128])
    ff_dnT = dram_in("ff_dnT", [7, FCH, 128, 128])
    ff_upw = dram_in("ff_upw", [12, FF_H // 2, 2, 2])
    ff_dnw = dram_in("ff_dnw", [12, FF_H // 2, 2, 2])
    ff_upb = dram_in("ff_upb", [FF_H])
    ff_wcon = dram_in("ff_wcon", [FF_H, D])
    ff_bsum = dram_in("ff_bsum", [2, D])
    out_own = nc.dram_tensor("out_own", [TOK, D], F32, kind="ExternalOutput").ap()

    l0k = nc.dram_tensor("l0k", [NG, DCH, 128, 512], BF16, kind="Internal").ap()
    l0v = nc.dram_tensor("l0v", [NG, 4, 128, VCOL * H], BF16, kind="Internal").ap()
    l1g = nc.dram_tensor("l1g", [DCH, 128, TOK], BF16, kind="Internal").ap()
    CHK = DCH * 128 * 512
    CHV = 4 * 128 * (VCOL * H)
    cc_in = nc.dram_tensor("cc_in", [QC, CHK + CHV], BF16, kind="Internal").ap()
    cc_out = nc.dram_tensor("cc_out", [2, QC, CHK + CHV], BF16, kind="Internal").ap()
    l1pad = nc.dram_tensor("l1pad", [4, CHK + CHV], BF16, kind="Internal").ap()

    def l1_kv(g):
        base = l1pad[(g % 4), :] if g < 0 else cc_out[g % 2, g // 2, :]
        k = base[0:CHK].rearrange("(c p t) -> c p t", c=DCH, p=128)
        v = base[CHK:CHK + CHV].rearrange("(f p t) -> f p t", f=4, p=128)
        return k, v

    ident_np = nc.inline_tensor(np.eye(128, dtype=bfd), "ident_bf")
    ones_np = nc.inline_tensor(np.ones((1, 128), dtype=bfd), "ones_row2")
    onesf_np = nc.inline_tensor(np.ones((1, 64), dtype=np.float32), "ones_f32")

    from contextlib import ExitStack
    with ExitStack() as _es:
        x_res = _es.enter_context(nc.sbuf_tensor("x_res", [128, NT, D], F32))
        identt = _es.enter_context(nc.sbuf_tensor("identt", [128, 128], BF16))
        onesr = _es.enter_context(nc.sbuf_tensor("onesr", [1, 128], BF16))
        onesf = _es.enter_context(nc.sbuf_tensor("onesf", [1, 64], F32))
        stats = _es.enter_context(nc.sbuf_tensor("stats", [128, 16, 8], F32))
        epscol = _es.enter_context(nc.sbuf_tensor("epscol", [128, 8], F32))
        scr = _es.enter_context(nc.sbuf_tensor("scr", [128, 1024], F32))
        scr2 = _es.enter_context(nc.sbuf_tensor("scr2", [128, 1024], F32))
        bias1 = _es.enter_context(nc.sbuf_tensor("bias1", [128, 64, 8], F32))
        bcast = _es.enter_context(nc.sbuf_tensor("bcast", [128, 3, 1024], BF16))
        lnbf = _es.enter_context(nc.sbuf_tensor("lnbf", [128, 1, 1024], BF16))
        xfm = _es.enter_context(nc.sbuf_tensor("xfm", [128, DCH, TOK], BF16))
        psA = _es.enter_context(nc.psum_tensor("psA", [128, 1024], F32))
        psB = _es.enter_context(nc.psum_tensor("psB", [128, 1024], F32))
        psC = _es.enter_context(nc.psum_tensor("psC", [128, 1024], F32))
        psD = _es.enter_context(nc.psum_tensor("psD", [128, 512], F32))
        psT = _es.enter_context(nc.psum_tensor("psT", [128, 1024], BF16))
        rrow = scr
        blk = _es.enter_context(nc.Block())
        S_ = {k: Sem(nc, k) for k in
              ["ldw", "ldx", "ldk", "mm", "tp", "ev", "dv", "ac", "kv", "cc", "gp", "st",
               "gw", "gx", "gz", "gst"]}
        BQ, BK, BV, BG1, BQK = 0, 8, 16, 24, 32

        def ln_sweep(tiles, dst_fm, dst_off, first_evac_wait=None, xsrc=None):
            n = len(tiles)
            ldx0, dv0, ac0, tp0, ev0 = (S_[k].val for k in ("ldx", "dv", "ac", "tp", "ev"))

            def f_sync(e):
                hseen = 0
                for i, (kind, ap_) in enumerate(tiles):
                    if kind == "hbm":
                        if hseen >= 2:
                            e.wait_ge(S_["dv"].h, dv0 + 2 * (i - 2) + 2)
                        S_["ldx"].inc(e.dma_start(out=xsrc[:, i % 2, :], in_=ap_), 16)
                        hseen += 1
            blk.sync(f_sync)

            def f_v(e):
                hseen = 0
                for i, (kind, ap_) in enumerate(tiles):
                    if kind == "hbm":
                        hseen += 1
                        e.wait_ge(S_["ldx"].h, ldx0 + 16 * hseen)
                        src = xsrc[:, i % 2, :]
                    else:
                        src = ap_
                    a = stats[:, (2 * i) % 8, :]
                    b = stats[:, (2 * i) % 8 + 1, :]
                    dmy = stats[:, 15, :]
                    e.tensor_scalar(scr[:], src, 1.0, 0.0, ALU.mult, ALU.add, accum_out=a[:, 2:3])
                    e.scalar_tensor_tensor(scr2[:], src, 1.0, src, ALU.mult, ALU.mult, accum_out=a[:, 3:4])
                    e.tensor_scalar_mul(a[:, 1:2], a[:, 2:3], 1.0 / D)
                    e.tensor_scalar_mul(a[:, 4:5], a[:, 3:4], 1.0 / D)
                    e.tensor_mul(a[:, 5:6], a[:, 1:2], a[:, 1:2])
                    e.tensor_copy(dmy[:, 0:1], a[:, 2:3])
                    e.tensor_copy(dmy[:, 1:2], a[:, 3:4])
                    S_["dv"].inc(e.tensor_sub(a[:, 0:1], a[:, 4:5], a[:, 5:6]))
                    e.wait_ge(S_["ac"].h, ac0 + i + 1)
                    if i >= 1:
                        e.wait_ge(S_["tp"].h, tp0 + i)
                    e.reciprocal(b[:, 1:2], b[:, 0:1])
                    e.tensor_copy(dmy[:, 0:1], a[:, 2:3])
                    e.tensor_copy(dmy[:, 1:2], a[:, 3:4])
                    S_["dv"].inc(e.tensor_scalar(lnbf[:, 0, :], src, a[:, 1:2], b[:, 1:2],
                                                 ALU.subtract, ALU.mult))

            def f_s(e):
                def evac(i):
                    e.wait_ge(S_["tp"].h, tp0 + i + 1)
                    if i == 0 and first_evac_wait is not None:
                        e.wait_ge(first_evac_wait[0].h, first_evac_wait[1])
                    o = dst_off + i * 128
                    S_["ev"].inc(e.copy(out=dst_fm[:, :, o:o + 128],
                                        in_=psT[:].rearrange("p (c x) -> p c x", c=DCH)))
                for i in range(n):
                    a = stats[:, (2 * i) % 8, :]
                    b = stats[:, (2 * i) % 8 + 1, :]
                    e.wait_ge(S_["dv"].h, dv0 + 2 * i + 1)
                    S_["ac"].inc(e.activation(out=b[:, 0:1], in_=a[:, 0:1], func=AF.Sqrt, bias=epscol[:, 0:1]))
                    if i >= 1:
                        evac(i - 1)
                evac(n - 1)

            def f_t(e):
                for i in range(n):
                    e.wait_ge(S_["dv"].h, dv0 + 2 * i + 2)
                    e.wait_ge(S_["ev"].h, ev0 + i)
                    for c in range(DCH):
                        ins = e.transpose(psT[:, c * 128:(c + 1) * 128],
                                          lnbf[:, 0, c * 128:(c + 1) * 128], identt[:])
                    S_["tp"].inc(ins)
            blk.vector(f_v)
            blk.scalar(f_s)
            blk.tensor(f_t)

        def std_proj(w_sb, x_fm_sl, dst_slices, oc_list, ts_list, func, bias_base,
                     wait_sems=(), scale=1.0, slot_free=None):
            mm0, ac0 = S_["mm"].val, S_["ac"].val
            slots = [psA[:, 0:512], psA[:, 512:1024], psB[:, 0:512], psB[:, 512:1024]]
            combos = [(oi, oc, ts) for oi, oc in enumerate(oc_list) for ts in ts_list]

            def f_t(e):
                for sem, val in wait_sems:
                    e.wait_ge(sem.h, val)
                for idx, (oi, oc, ts) in enumerate(combos):
                    if idx >= 4:
                        e.wait_ge(S_["ac"].h, ac0 + idx - 3)
                    x = x_fm_sl(ts)
                    for c in range(DCH):
                        ins = e.matmul(slots[idx % 4], w_sb[:, c, oc * 128:(oc + 1) * 128],
                                       x[:, c, :], start=(c == 0), stop=(c == DCH - 1))
                    S_["mm"].inc(ins)

            def f_s(e):
                for idx, (oi, oc, ts) in enumerate(combos):
                    e.wait_ge(S_["mm"].h, mm0 + idx + 1)
                    if slot_free is not None and idx >= 2:
                        semv = slot_free(idx)
                        e.wait_ge(semv[0].h, semv[1])
                    bias = bias1[:, bias_base + oi, 0:1] if bias_base is not None else 0.0
                    S_["ac"].inc(e.activation(out=dst_slices(oi, ts), in_=slots[idx % 4],
                                              func=func, bias=bias, scale=scale))
            blk.tensor(f_t)
            blk.scalar(f_s)

        def swap_proj_residual(src_fm, w_sb, nch, bcast_idx, act_func, extra_mul_col=None,
                               wait_sems=(), gate_fm=None):
            mm0, dv0, ac0 = S_["mm"].val, S_["dv"].val, S_["ac"].val

            def f_t(e):
                for sem, val in wait_sems:
                    e.wait_ge(sem.h, val)
                for tt in range(NT):
                    if tt >= 2:
                        e.wait_ge(S_["dv"].h, dv0 + 2 * tt - 3)
                    ps = (psC, psB)[tt % 2]
                    for c in range(nch):
                        e.matmul(ps[:, 0:512], src_fm[:, c, tt * 128:(tt + 1) * 128],
                                 w_sb[:, c, 0:512], start=(c == 0), stop=(c == nch - 1))
                        ins = e.matmul(ps[:, 512:1024], src_fm[:, c, tt * 128:(tt + 1) * 128],
                                       w_sb[:, c, 512:1024], start=(c == 0), stop=(c == nch - 1))
                    S_["mm"].inc(ins)

            def f_v(e):
                for tt in range(NT):
                    ps = (psC, psB)[tt % 2]
                    e.wait_ge(S_["mm"].h, mm0 + tt + 1)
                    if bcast_idx is not None:
                        S_["dv"].inc(e.tensor_add(scr2[:], ps[:], bcast[:, bcast_idx, :]))
                    else:
                        S_["dv"].inc(e.tensor_copy(out=scr2[:], in_=ps[:]))
                    e.wait_ge(S_["ac"].h, ac0 + tt + 1)
                    if extra_mul_col is not None:
                        S_["dv"].inc(e.scalar_tensor_tensor(x_res[:, tt, :], scr[:], extra_mul_col(tt),
                                                            x_res[:, tt, :], ALU.mult, ALU.add))
                    else:
                        S_["dv"].inc(e.tensor_add(x_res[:, tt, :], x_res[:, tt, :], scr[:]))

            def f_s(e):
                for tt in range(NT):
                    e.wait_ge(S_["dv"].h, dv0 + 2 * tt + 1)
                    S_["ac"].inc(e.activation(out=scr[:], in_=scr2[:], func=act_func))
            blk.tensor(f_t)
            blk.vector(f_v)
            blk.scalar(f_s)

        def f_p0_sync(e):
            S_["ldw"].inc(e.dma_start(out=identt[:], in_=ident_np.ap()), 16)
            S_["ldw"].inc(e.dma_start(out=onesr[:], in_=ones_np.ap()), 16)
            S_["ldw"].inc(e.dma_start(out=onesf[:], in_=onesf_np.ap()), 16)
            for ap_, base in ((l0_bq, BQ), (l0_bk, BK), (l0_bv, BV)):
                S_["ldw"].inc(e.dma_start(out=bias1[:, base:base + 8, 0:1],
                                          in_=ap_.rearrange("(c p) -> p c", p=128).unsqueeze(2), allow_slow_non_contiguous=True), 16)
            S_["ldw"].inc(e.dma_start(out=scr2[0:1, 0:1024], in_=l0_bo[None, :]), 16)
            S_["ldw"].inc(e.dma_start(out=scr2[32:33, 0:1024], in_=l1_bout[None, :]), 16)
            S_["ldw"].inc(e.dma_start(out=scr2[64:65, 0:1024], in_=ff_bsum[0:1, :]), 16)
            S_["ldw"].inc(e.dma_start(out=scr2[96:97, 0:1024], in_=ff_bsum[1:2, :]), 16)
        blk.sync(f_p0_sync)
        LDW_P0 = S_["ldw"].val

        def f_p0_g(e):
            S_["gx"].inc(e.dma_start(out=x_res[:], in_=x_own.rearrange("(t p) d -> p t d", p=128)), 16)
        blk.gpsimd(f_p0_g)
        LDX_RES = S_["gx"].val

        def f_p0_v(e):
            e.wait_ge(S_["ldw"].h, LDW_P0)
            e.memset(epscol[:, 0:1], 1e-5)
            S_["dv"].inc(e.memset(scr[:], 0.0))
        blk.vector(f_p0_v)
        DV_P0 = S_["dv"].val

        def f_p0_g2(e):
            e.wait_ge(S_["dv"].h, DV_P0)
            zb = scr[:].bitcast(BF16)
            total = 4 * (CHK + CHV)
            flat = l1pad.rearrange("a b -> (a b)")
            per = 128 * 2048
            off = 0
            while off < total:
                n = min(per, total - off)
                S_["gz"].inc(e.dma_start(out=flat[off:off + n].rearrange("(p x) -> p x", p=128),
                                         in_=zb[:, 0:n // 128]), 16)
                off += n
        blk.gpsimd(f_p0_g2)
        KV_PAD = S_["gz"].val
        TP_P00 = S_["tp"].val
        DV_P00 = S_["dv"].val

        def f_p0_v2(e):
            e.wait_ge(S_["gz"].h, KV_PAD)
            for i in range(4):
                e.wait_ge(S_["tp"].h, TP_P00 + i)
                S_["dv"].inc(e.tensor_copy(out=scr[0:1, 0:512].bitcast(BF16),
                                           in_=scr2[32 * i:32 * i + 1, 0:1024]))
        blk.vector(f_p0_v2)
        EV_BC0 = S_["ev"].val

        def f_p0_t(e):
            for i in range(4):
                e.wait_ge(S_["dv"].h, DV_P00 + i + 1)
                if i > 0:
                    e.wait_ge(S_["ev"].h, EV_BC0 + i)
                for half in range(2):
                    ins = e.matmul(psA[:, half * 512:(half + 1) * 512], onesr[0:1, :],
                                   scr[0:1, 0:512].bitcast(BF16)[:, half * 512:(half + 1) * 512],
                                   start=True, stop=True)
                S_["tp"].inc(ins)
        blk.tensor(f_p0_t)

        def f_p0_s(e):
            for i in range(4):
                e.wait_ge(S_["tp"].h, TP_P00 + i + 1)
                if i < 3:
                    S_["ev"].inc(e.copy(out=bcast[:, i, :], in_=psA[:]))
                else:
                    S_["ev"].inc(e.copy(out=scr2[:], in_=psA[:]))
        blk.scalar(f_p0_s)
        EV_BC4 = S_["ev"].val

        def f_p0_v3(e):
            e.wait_ge(S_["ev"].h, EV_BC4)
            S_["dv"].inc(e.tensor_add(bcast[:, 2, :], bcast[:, 2, :], scr2[:]))
        blk.vector(f_p0_v3)
        EV_BC = S_["ev"].val

        if "l0qkv" in phases or "l0attn" in phases:
            _esL0 = _es.enter_context(ExitStack())
            qfm = _esL0.enter_context(nc.sbuf_tensor("qfm", [128, DCH, TOK], BF16))
            with ExitStack() as _es2:
                xsrcL = _es2.enter_context(nc.sbuf_tensor("xsrcL", [128, 2, 1024], F32))
                wk_sb = _es2.enter_context(nc.sbuf_tensor("wk_sb", [128, DCH, D], BF16))
                wv_sb = _es2.enter_context(nc.sbuf_tensor("wv_sb", [128, DCH, D], BF16))
                kvo = _es2.enter_context(nc.sbuf_tensor("kvo", [128, 2, 512], BF16))
                vfm = _es2.enter_context(nc.sbuf_tensor("vfm", [128, DCH, 512], BF16))
                vto = _es2.enter_context(nc.sbuf_tensor("vto", [128, 1, VCOL * H], BF16))
                wq_sb = wk_sb
                def f_wl(e):
                    S_["gw"].inc(e.dma_start(out=wk_sb[:], in_=l0_wq.rearrange("(c p) o -> p c o", p=128)), 16)
                    S_["gw"].inc(e.dma_start(out=wv_sb[:], in_=l0_wv.rearrange("(c p) o -> p c o", p=128)), 16)
                blk.gpsimd(f_wl)
                LDW_QKV = S_["gw"].val

                def f_wait_res(e):
                    e.wait_ge(S_["gx"].h, LDX_RES)
                blk.vector(f_wait_res)
                ln_sweep([("sbuf", x_res[:, t, :]) for t in range(NT)], xfm, 0)
                EV_LN1 = S_["ev"].val
                if "dbg_xfm" in dbg:
                    dx = dram_dbg("dbg_xfm", [128, DCH, TOK], U16)
                    blk.gpsimd(lambda e, v=EV_LN1: (e.wait_ge(S_["ev"].h, v),
                                S_["gst"].inc(e.dma_start(out=dx.bitcast(BF16), in_=xfm[:]), 16)))

                std_proj(wq_sb, lambda ts: xfm[:, :, ts * 512:(ts + 1) * 512],
                         lambda oi, ts: qfm[:, oi, ts * 512:(ts + 1) * 512],
                         list(range(DCH)), list(range(4)), AF.Tanh, BQ,
                         wait_sems=[(S_["ev"], EV_LN1), (S_["gw"], LDW_QKV)])
                MM_Q = S_["mm"].val

                def f_wl2(e):
                    e.wait_ge(S_["mm"].h, MM_Q)
                    S_["gw"].inc(e.dma_start(out=wk_sb[:], in_=l0_wk.rearrange("(c p) o -> p c o", p=128)), 16)
                blk.gpsimd(f_wl2)
                LDW_K = S_["gw"].val

                if "dbg_q" in dbg:
                    dq = dram_dbg("dbg_q", [128, DCH, TOK])
                    ac0 = S_["ac"].val
                    blk.gpsimd(lambda e: S_["gst"].inc(
                        (e.wait_ge(S_["ac"].h, ac0), e.dma_start(out=dq, in_=qfm[:]))[1], 16))

                for g in range(NG):
                    fw = (S_["mm"], MM_Q) if g == 0 else None
                    ln_sweep([("hbm", x_seq[g * 512 + t * 128:g * 512 + (t + 1) * 128, :])
                              for t in range(4)], xfm, 0, first_evac_wait=fw, xsrc=xsrcL)
                    EV_LNG = S_["ev"].val
                    KV_K0 = S_["kv"].val
                    AC_K0 = S_["ac"].val
                    std_proj(wk_sb, lambda ts: xfm[:, :, 0:512],
                             lambda oi, ts: kvo[:, oi % 2, :], list(range(DCH)), [0], AF.Tanh, BK,
                             wait_sems=[(S_["ev"], EV_LNG), (S_["gw"], LDW_K)],
                             slot_free=lambda idx: (S_["kv"], KV_K0 + 16 * (idx - 1)))
                    def f_spk(e, g=g, AC_K0=AC_K0):
                        for oc in range(DCH):
                            e.wait_ge(S_["ac"].h, AC_K0 + oc + 1)
                            S_["kv"].inc(e.dma_start(out=l0k[g, oc], in_=kvo[:, oc % 2, :]), 16)
                    blk.sync(f_spk)
                    std_proj(wv_sb, lambda ts: xfm[:, :, 0:512],
                             lambda oi, ts: vfm[:, oi, :], list(range(DCH)), [0], AF.Tanh, BV,
                             wait_sems=[(S_["kv"], S_["kv"].val)])
                    AC_V = S_["ac"].val
                    tp0, ev0, dv0 = S_["tp"].val, S_["ev"].val, S_["dv"].val

                    def f_tv(e, AC_V=AC_V, tp0=tp0, ev0=ev0):
                        e.wait_ge(S_["ac"].h, AC_V)
                        for tt in range(4):
                            e.wait_ge(S_["ev"].h, ev0 + tt)
                            for c in range(DCH):
                                ins = e.transpose(psT[:, c * 128:(c + 1) * 128],
                                                  vfm[:, c, tt * 128:(tt + 1) * 128], identt[:])
                            S_["tp"].inc(ins)
                    blk.tensor(f_tv)

                    if g == 0:
                        def f_ones(e):
                            S_["dv"].inc(e.memset(
                                vto[:, 0, :].rearrange("p (h x) -> p h x", x=VCOL)[:, :, 64:65], 1.0))
                        blk.vector(f_ones)
                    DV_ONES = S_["dv"].val
                    kv0v = S_["kv"].val
                    EV_V0 = S_["ev"].val

                    def f_sv(e, tp0=tp0, kv0v=kv0v):
                        for tt in range(4):
                            e.wait_ge(S_["tp"].h, tp0 + tt + 1)
                            e.wait_ge(S_["kv"].h, kv0v + 16 * tt)
                            dst = vto[:, 0, :].rearrange("p (h x) -> p h x", x=VCOL)[:, :, 0:64]
                            S_["ev"].inc(e.copy(out=dst, in_=psT[:].rearrange("p (h x) -> p h x", x=64)))
                    blk.scalar(f_sv)

                    def f_spv(e, g=g, EV_V0=EV_V0, DV_ONES=DV_ONES):
                        e.wait_ge(S_["dv"].h, DV_ONES)
                        for tt in range(4):
                            e.wait_ge(S_["ev"].h, EV_V0 + tt + 1)
                            S_["kv"].inc(e.dma_start(out=l0v[g, tt], in_=vto[:, 0, :]), 16)
                    blk.sync(f_spv)

                if "dbg_kv" in dbg:
                    dk = dram_dbg("dbg_k", [NG, DCH, 128, 512], U16)
                    dvv = dram_dbg("dbg_v", [NG, 4, 128, VCOL * H], U16)
                    kv0 = S_["kv"].val
                    blk.gpsimd(lambda e: (e.wait_ge(S_["kv"].h, kv0),
                                          S_["gst"].inc(e.dma_start(out=dk.bitcast(BF16), in_=l0k[:]), 16),
                                          S_["gst"].inc(e.dma_start(out=dvv.bitcast(BF16), in_=l0v[:]), 16)))

        if "l0attn" in phases:
            with ExitStack() as _es2:
                wo_sb = _es2.enter_context(nc.sbuf_tensor("wo_sb", [128, DCH, D], BF16))
                winm = _es2.enter_context(nc.sbuf_tensor("winm", [128, 8, 512], BF16))
                kst = _es2.enter_context(nc.sbuf_tensor("kst", [128, 4, 128], BF16))
                vst = _es2.enter_context(nc.sbuf_tensor("vst", [128, 4, 130], BF16))
                pT = _es2.enter_context(nc.sbuf_tensor("pT", [128, 2, 1024], BF16))
                ctxfm = xfm
                KV_ALL = S_["kv"].val
                MM_QKVDONE = S_["mm"].val
                def f_wm(e):
                    e.wait_ge(S_["kv"].h, KV_ALL)
                    e.wait_ge(S_["mm"].h, MM_QKVDONE)
                    S_["ldw"].inc(e.dma_start(out=winm[:], in_=l0_winmask.rearrange("w p t -> p w t").bitcast(BF16)), 16)
                blk.sync(f_wm)
                LDW_WM = S_["ldw"].val

                for hs in range(8):
                    for qc in range(QC):
                        nj = 8 * qc + 8
                        ldk0, mm0, ac0, dv0 = (S_[k].val for k in ("ldk", "mm", "ac", "dv"))
                        QK = lambda j: mm0 + 1 if j == 0 else mm0 + 2 * j
                        AV = lambda j, nj=nj: mm0 + 2 * j + (3 if j < nj - 1 else 2)

                        def f_dma(e, qc=qc, nj=nj, ldk0=ldk0, mm0=mm0, hs=hs, AV=AV):
                            for j in range(nj):
                                if j >= 4:
                                    e.wait_ge(S_["mm"].h, AV(j - 4))
                                else:
                                    e.wait_ge(S_["mm"].h, mm0)
                                if hs == 0 and qc == 0 and j == 0:
                                    e.wait_ge(S_["kv"].h, KV_ALL)
                                g, tt = j // 4, j % 4
                                S_["ldk"].inc(e.dma_start(out=kst[:, j % 4, :],
                                                          in_=l0k[g, hs, :, tt * 128:(tt + 1) * 128]), 16)
                                S_["ldk"].inc(e.dma_start(out=vst[:, j % 4, :],
                                                          in_=l0v[g, tt, :, 130 * hs:130 * hs + 130]), 16)
                        blk.sync(f_dma)

                        def f_pe(e, qc=qc, nj=nj, ldk0=ldk0, mm0=mm0, ac0=ac0, dv0=dv0, hs=hs):
                            if hs == 0 and qc == 0:
                                e.wait_ge(S_["ldw"].h, LDW_WM)

                            def emit_qk(j):
                                strip = (psA, psB)[j % 2]
                                e.wait_ge(S_["ldk"].h, ldk0 + 32 * j + 16)
                                if j >= 2:
                                    e.wait_ge(S_["ac"].h, ac0 + j - 1)
                                elif not (hs == 0 and qc == 0):
                                    e.wait_ge(S_["ac"].h, ac0 - 2)
                                for hp in range(2):
                                    ins = e.matmul(strip[:, hp * 512:(hp + 1) * 512],
                                                   kst[hp * 64:(hp + 1) * 64, j % 4, :],
                                                   qfm[hp * 64:(hp + 1) * 64, hs, qc * 512:(qc + 1) * 512],
                                                   start=True, stop=True)
                                S_["mm"].inc(ins)

                            def emit_av(j):
                                e.wait_ge(S_["ldk"].h, ldk0 + 32 * j + 32)
                                e.wait_ge(S_["ac"].h, ac0 + j + 1)
                                if j == 0:
                                    e.wait_ge(S_["dv"].h, dv0)
                                if j >= 8 * qc:
                                    e.wait_ge(S_["dv"].h, dv0 + 2 * (j - 8 * qc) + 2)
                                for hp in range(2):
                                    avp = psC[0:65, hp * 512:(hp + 1) * 512]
                                    ins = e.matmul(avp, vst[:, j % 4, hp * 65:(hp + 1) * 65],
                                                   pT[:, j % 2, hp * 512:(hp + 1) * 512],
                                                   start=(j == 0), stop=(j == nj - 1))
                                S_["mm"].inc(ins)

                            emit_qk(0)
                            for j in range(1, nj):
                                emit_qk(j)
                                emit_av(j - 1)
                            emit_av(nj - 1)
                        blk.tensor(f_pe)

                        def f_exp(e, qc=qc, nj=nj, mm0=mm0, dv0=dv0, ac0=ac0):
                            for j in range(nj):
                                strip = (psA, psB)[j % 2]
                                e.wait_ge(S_["mm"].h, QK(j))
                                S_["ac"].inc(e.activation(out=pT[:, j % 2, :], in_=strip[:],
                                                          func=AF.Exp, scale=0.125))
                        blk.scalar(f_exp)

                        def f_mask(e, qc=qc, nj=nj, ac0=ac0, dv0=dv0):
                            for j in range(8 * qc, nj):
                                w = j - 8 * qc
                                e.wait_ge(S_["ac"].h, ac0 + j + 1)
                                for hp in range(2):
                                    S_["dv"].inc(e.tensor_mul(pT[:, j % 2, hp * 512:(hp + 1) * 512],
                                                              pT[:, j % 2, hp * 512:(hp + 1) * 512],
                                                              winm[:, w, :]))
                        blk.vector(f_mask)

                        MM_AVE = S_["mm"].val
                        dv1 = S_["dv"].val

                        def f_nv(e, qc=qc, hs=hs, MM_AVE=MM_AVE, dv1=dv1):
                            e.wait_ge(S_["mm"].h, MM_AVE)
                            S_["dv"].inc(e.reciprocal(rrow[0:1, 0:512], psC[64:65, 0:512]))
                            S_["dv"].inc(e.reciprocal(rrow[0:1, 512:1024], psC[64:65, 512:1024]))
                        blk.vector(f_nv)
                        DV_R = S_["dv"].val
                        mm1 = S_["mm"].val

                        def f_nt(e, DV_R=DV_R):
                            e.wait_ge(S_["dv"].h, DV_R)
                            rbdst = (psD[0:64, :], psD[64:128, :])
                            for hp in range(2):
                                S_["mm"].inc(e.matmul(rbdst[hp], onesf[0:1, :],
                                                      rrow[0:1, 512 * hp:512 * hp + 512], start=True, stop=True))
                        blk.tensor(f_nt)
                        AC_RB = S_["ac"].val

                        def f_nrb(e, mm1=mm1):
                            rbdst = (psD[0:64, :], psD[64:128, :])
                            for hp in range(2):
                                e.wait_ge(S_["mm"].h, mm1 + hp + 1)
                                S_["ac"].inc(e.copy(out=scr2[0:64, 512 * hp:512 * hp + 512], in_=rbdst[hp]))
                        blk.scalar(f_nrb)

                        def f_nv2(e, qc=qc, hs=hs, AC_RB=AC_RB):
                            for hp in range(2):
                                e.wait_ge(S_["ac"].h, AC_RB + hp + 1)
                                S_["dv"].inc(e.tensor_mul(
                                    ctxfm[hp * 64:(hp + 1) * 64, hs, qc * 512:(qc + 1) * 512],
                                    psC[0:64, hp * 512:(hp + 1) * 512],
                                    scr2[0:64, 512 * hp:512 * hp + 512]))
                        blk.vector(f_nv2)
                if "dbg_ctx" in dbg:
                    dc = dram_dbg("dbg_ctx", [128, DCH, TOK], U16)
                    dv0 = S_["dv"].val
                    blk.gpsimd(lambda e: (e.wait_ge(S_["dv"].h, dv0),
                                          S_["gst"].inc(e.dma_start(out=dc.bitcast(BF16), in_=ctxfm[:]), 16)))

                if "dbg_winm" in dbg:
                    dwm = dram_dbg("dbg_winm", [128, 8, 512], U16)
                    _ldwv = S_["ldw"].val
                    blk.gpsimd(lambda e: (e.wait_ge(S_["ldw"].h, _ldwv),
                                          S_["gst"].inc(e.dma_start(out=dwm.bitcast(BF16), in_=winm[:]), 16)))
                DV_CTX = S_["dv"].val
                MM_WO = S_["mm"].val
                def f_wo(e):
                    e.wait_ge(S_["mm"].h, MM_WO)
                    S_["gw"].inc(e.dma_start(out=wo_sb[:], in_=l0_wo.rearrange("(c p) o -> p c o", p=128)), 16)
                blk.gpsimd(f_wo)
                swap_proj_residual(ctxfm, wo_sb, DCH, 0, AF.Tanh,
                                   wait_sems=[(S_["dv"], DV_CTX), (S_["gw"], S_["gw"].val),
                                              (S_["ev"], EV_BC)])

        if "l0qkv" in phases or "l0attn" in phases:
            _esL0.close()

        if "moe" in phases:
            with ExitStack() as _es2:
                wgh = _es2.enter_context(nc.sbuf_tensor("wgh", [128, 4, DCH, 128], BF16))
                wuh = _es2.enter_context(nc.sbuf_tensor("wuh", [128, 4, DCH, 128], BF16))
                wde = _es2.enter_context(nc.sbuf_tensor("wde", [128, 16, D], BF16))
                hidb = _es2.enter_context(nc.sbuf_tensor("hidb", [128, 16, 512], BF16))
                sc8 = _es2.enter_context(nc.sbuf_tensor("sc8", [128, NT, 8], F32))
                scrE = _es2.enter_context(nc.sbuf_tensor("scrE", [128, NT, 8], F32))
                scrF = _es2.enter_context(nc.sbuf_tensor("scrF", [128, NT, 8], F32))
                stats2 = _es2.enter_context(nc.sbuf_tensor("stats2", [128, NT, 8], F32))
                wpe = _es2.enter_context(nc.sbuf_tensor("wpe", [128, NT, 9, 8], F32))
                gwsb = _es2.enter_context(nc.sbuf_tensor("gwsb", [128, DCH, 8], BF16))
                gscb = scr2[:].bitcast(BF16)
                gsil = scr[:].bitcast(BF16).rearrange("p (a t) -> p a t", a=4)

                MM_PRE = S_["mm"].val
                ln_sweep([("sbuf", x_res[:, t, :]) for t in range(NT)], xfm, 0,
                         first_evac_wait=(S_["mm"], MM_PRE))
                EV_LN2 = S_["ev"].val

                def f_gwl(e):
                    e.wait_ge(S_["mm"].h, MM_PRE)
                    S_["gw"].inc(e.dma_start(out=gwsb[:], in_=moe_gw.rearrange("(c p) o -> p c o", p=128)), 16)
                blk.gpsimd(f_gwl)
                GW_GATE = S_["gw"].val

                DV_Z0 = S_["dv"].val
                blk.vector(lambda e: S_["dv"].inc(e.memset(gscb[:], 0.0)))
                DV_Z = S_["dv"].val
                mmg0 = S_["mm"].val

                def f_gt(e):
                    e.wait_ge(S_["ev"].h, EV_LN2)
                    e.wait_ge(S_["gw"].h, GW_GATE)
                    for ts in range(4):
                        if ts >= 1:
                            e.wait_ge(S_["ac"].h, acg0 + ts)
                        for c in range(DCH):
                            ins = e.matmul(psD[0:8, :], gwsb[:, c, :], xfm[:, c, ts * 512:(ts + 1) * 512],
                                           start=(c == 0), stop=(c == DCH - 1))
                        S_["mm"].inc(ins)
                acg0 = S_["ac"].val
                blk.tensor(f_gt)

                def f_gs(e):
                    for ts in range(4):
                        e.wait_ge(S_["mm"].h, mmg0 + ts + 1)
                        e.wait_ge(S_["dv"].h, DV_Z)
                        S_["ac"].inc(e.copy(out=gscb[0:8, ts * 512:(ts + 1) * 512], in_=psD[0:8, :]))
                blk.scalar(f_gs)
                ACG_E = S_["ac"].val

                tpg0, evg0 = S_["tp"].val, S_["ev"].val

                def f_gt2(e):
                    e.wait_ge(S_["ac"].h, ACG_E)
                    for rnd in range(2):
                        if rnd == 1:
                            e.wait_ge(S_["ev"].h, evg0 + 1)
                        for c in range(8):
                            ins = e.transpose(psT[:, c * 128:(c + 1) * 128],
                                              gscb[:, rnd * 1024 + c * 128: rnd * 1024 + (c + 1) * 128],
                                              identt[:])
                        S_["tp"].inc(ins)
                blk.tensor(f_gt2)

                def f_gs2(e):
                    for rnd in range(2):
                        e.wait_ge(S_["tp"].h, tpg0 + rnd + 1)
                        S_["ev"].inc(e.copy(out=sc8[:, rnd * 8:(rnd + 1) * 8, :],
                                            in_=psT[:].rearrange("p (c x) -> p c x", c=8)[:, :, 0:8]))
                blk.scalar(f_gs2)
                EV_SC8 = S_["ev"].val

                dvt0, act0 = S_["dv"].val, S_["ac"].val

                def f_top_v(e):
                    e.wait_ge(S_["ev"].h, EV_SC8)
                    SS = lambda tt, k: stats[:, tt, k:k + 1]
                    for tt in range(NT):
                        e.reduce_max(SS(tt, 1), sc8[:, tt, :], axis=AX.X)
                    for tt in range(NT):
                        e.tensor_single_scalar(scrE[:, tt, :], sc8[:, tt, :], SS(tt, 1), ALU.is_equal)
                    for tt in range(NT):
                        e.scalar_tensor_tensor(scrF[:, tt, :], scrE[:, tt, :], -1e9, sc8[:, tt, :],
                                               ALU.mult, ALU.add)
                    for tt in range(NT):
                        e.reduce_max(SS(tt, 2), scrF[:, tt, :], axis=AX.X)
                    for tt in range(NT):
                        S_["dv"].inc(e.tensor_sub(SS(tt, 0), SS(tt, 2), SS(tt, 1)))
                    e.wait_ge(S_["ac"].h, act0 + NT)
                    for tt in range(NT):
                        e.tensor_scalar_add(SS(tt, 3), stats2[:, tt, 0:1], 1.0)
                    for tt in range(NT):
                        e.reciprocal(SS(tt, 4), SS(tt, 3))
                    for tt in range(NT):
                        e.tensor_mul(SS(tt, 5), stats2[:, tt, 0:1], SS(tt, 4))
                    for tt in range(NT):
                        e.tensor_sub(SS(tt, 6), SS(tt, 4), SS(tt, 5))
                    for tt in range(NT):
                        e.tensor_single_scalar(scrF[:, tt, :], sc8[:, tt, :], SS(tt, 2), ALU.is_ge)
                    for tt in range(NT):
                        e.tensor_single_scalar(scrF[:, tt, :], scrF[:, tt, :], SS(tt, 5), ALU.mult)
                    for tt in range(NT):
                        ins = e.scalar_tensor_tensor(
                            wpe[:, tt, 0:8, 0:1].squeeze(), scrE[:, tt, :], SS(tt, 6), scrF[:, tt, :],
                            ALU.mult, ALU.add)
                    S_["dv"].inc(ins, 1)
                blk.vector(f_top_v)

                def f_top_s(e):
                    e.wait_ge(S_["dv"].h, dvt0 + NT)
                    for tt in range(NT):
                        S_["ac"].inc(e.activation(out=stats2[:, tt, 0:1], in_=stats[:, tt, 0:1], func=AF.Exp))
                blk.scalar(f_top_s)
                DV_TOP = S_["dv"].val

                def f_ones2(e):
                    e.wait_ge(S_["dv"].h, DV_TOP)
                    S_["dv"].inc(e.memset(wpe[:, :, 8, 0:1], 1.0))
                blk.vector(f_ones2)
                DV_TOP2 = S_["dv"].val

                MM_LD = [S_["mm"].val]
                for e_i in range(MOE_E + 1):
                    shared = e_i == MOE_E
                    wsrc_g = moe_wsg if shared else moe_wg[e_i]
                    wsrc_u = moe_wsu if shared else moe_wu[e_i]
                    wsrc_d = moe_wsd if shared else moe_wd[e_i]
                    mm0 = S_["mm"].val
                    ac0 = S_["ac"].val
                    dv0 = S_["dv"].val
                    gw0 = S_["gw"].val
                    gslot = [psA[:, 0:512], psA[:, 512:1024]]
                    uslot = [psB[:, 0:512], psB[:, 512:1024]]
                    mm_g = lambda ts, hc, mm0=mm0: mm0 + 36 * ts + 2 * hc + 1
                    mm_u = lambda ts, hc, mm0=mm0: mm0 + 36 * ts + 2 * hc + 2
                    mm_d = lambda ts, tb, mm0=mm0: mm0 + 36 * ts + 33 + tb
                    ac_s = lambda ts, hc, ac0=ac0: ac0 + 16 * ts + hc + 1
                    dv_m = lambda ts, hc, dv0=dv0: dv0 + 20 * ts + hc + 1
                    dv_r = lambda ts, tb, dv0=dv0: dv0 + 20 * ts + 17 + tb
                    def gw_g(ts, hc, gw0=gw0):
                        return gw0 + 512 * ts + 32 * hc + 16 + (16 if ts >= 1 else 0)
                    def gw_u(ts, hc, gw0=gw0):
                        return gw_g(ts, hc) + 16
                    GW_WDE = gw0 + 32 * 16 + 16

                    def f_wl(e, wsrc_g=wsrc_g, wsrc_u=wsrc_u, wsrc_d=wsrc_d,
                             mm_g=mm_g, mm_u=mm_u, e_i=e_i, MM_LDP=MM_LD[0], mm0=mm0):
                        if e_i == 0:
                            e.wait_ge(S_["mm"].h, MM_PRE)
                        for ts in range(4):
                            for hc in range(16):
                                if ts > 0 or hc >= 4:
                                    pts, phc = (ts, hc - 4) if hc >= 4 else (ts - 1, 12 + hc)
                                    e.wait_ge(S_["mm"].h, mm_u(pts, phc))
                                elif e_i > 0:
                                    e.wait_ge(S_["mm"].h, mm0 - 10 + 2 * hc)
                                S_["gw"].inc(e.dma_start(
                                    out=wgh[:, hc % 4, :, :],
                                    in_=wsrc_g[:, hc * 128:(hc + 1) * 128].rearrange("(c p) o -> p c o", p=128)), 16)
                                S_["gw"].inc(e.dma_start(
                                    out=wuh[:, hc % 4, :, :],
                                    in_=wsrc_u[:, hc * 128:(hc + 1) * 128].rearrange("(c p) o -> p c o", p=128)), 16)
                            if ts == 0:
                                e.wait_ge(S_["mm"].h, MM_LDP)
                                S_["gw"].inc(e.dma_start(
                                    out=wde[:], in_=wsrc_d.rearrange("(hc p) o -> p hc o", p=128)), 16)
                    blk.gpsimd(f_wl)

                    def f_et(e, mm0=mm0, ac0=ac0, dv0=dv0, e_i=e_i,
                             mm_g=mm_g, mm_u=mm_u, mm_d=mm_d, ac_s=ac_s, dv_m=dv_m, dv_r=dv_r,
                             gw_g=gw_g, gw_u=gw_u, GW_WDE=GW_WDE):
                        e.wait_ge(S_["dv"].h, dv0 if e_i > 0 else DV_TOP2)
                        for ts in range(4):
                            for hc in range(16):
                                e.wait_ge(S_["gw"].h, gw_g(ts, hc))
                                if ts > 0 or hc >= 2:
                                    pts, phc = (ts, hc - 2) if hc >= 2 else (ts - 1, 14 + hc)
                                    e.wait_ge(S_["ac"].h, ac_s(pts, phc))
                                for c in range(DCH):
                                    ins = e.matmul(gslot[hc % 2], wgh[:, hc % 4, c, :],
                                                   xfm[:, c, ts * 512:(ts + 1) * 512],
                                                   start=(c == 0), stop=(c == DCH - 1))
                                S_["mm"].inc(ins)
                                e.wait_ge(S_["gw"].h, gw_u(ts, hc))
                                if ts > 0 or hc >= 2:
                                    pts, phc = (ts, hc - 2) if hc >= 2 else (ts - 1, 14 + hc)
                                    e.wait_ge(S_["dv"].h, dv_m(pts, phc))
                                for c in range(DCH):
                                    ins = e.matmul(uslot[hc % 2], wuh[:, hc % 4, c, :],
                                                   xfm[:, c, ts * 512:(ts + 1) * 512],
                                                   start=(c == 0), stop=(c == DCH - 1))
                                S_["mm"].inc(ins)
                            if ts == 0:
                                e.wait_ge(S_["gw"].h, GW_WDE)
                            e.wait_ge(S_["dv"].h, dv_m(ts, 15))
                            for tb in range(4):
                                if ts > 0 or tb > 0:
                                    pts, ptb = (ts, tb - 1) if tb > 0 else (ts - 1, 3)
                                    e.wait_ge(S_["dv"].h, dv_r(pts, ptb))
                                for hc in range(16):
                                    e.matmul(psC[:, 0:512], hidb[:, hc, tb * 128:(tb + 1) * 128],
                                             wde[:, hc, 0:512], start=(hc == 0), stop=(hc == 15))
                                    ins = e.matmul(psC[:, 512:1024], hidb[:, hc, tb * 128:(tb + 1) * 128],
                                                   wde[:, hc, 512:1024], start=(hc == 0), stop=(hc == 15))
                                S_["mm"].inc(ins)
                    blk.tensor(f_et)

                    def f_es(e, mm_g=mm_g, dv_m=dv_m):
                        for ts in range(4):
                            for hc in range(16):
                                e.wait_ge(S_["mm"].h, mm_g(ts, hc))
                                if ts > 0 or hc >= 2:
                                    pts, phc = (ts, hc - 2) if hc >= 2 else (ts - 1, 14 + hc)
                                    e.wait_ge(S_["dv"].h, dv_m(pts, phc))
                                S_["ac"].inc(e.activation(out=gsil[:, hc % 2, :], in_=gslot[hc % 2], func=AF.Silu))
                    blk.scalar(f_es)

                    def f_ev(e, mm_u=mm_u, mm_d=mm_d, ac_s=ac_s, e_i=e_i, shared=shared):
                        for ts in range(4):
                            for hc in range(16):
                                e.wait_ge(S_["mm"].h, mm_u(ts, hc))
                                e.wait_ge(S_["ac"].h, ac_s(ts, hc))
                                S_["dv"].inc(e.tensor_mul(hidb[:, hc, :], uslot[hc % 2], gsil[:, hc % 2, :]))
                            for tb in range(4):
                                tile = ts * 4 + tb
                                e.wait_ge(S_["mm"].h, mm_d(ts, tb))
                                wcol = wpe[:, tile, 8 if shared else e_i, 0:1]
                                S_["dv"].inc(e.scalar_tensor_tensor(
                                    x_res[:, tile, :], psC[:], wcol, x_res[:, tile, :],
                                    ALU.mult, ALU.add))
                    blk.vector(f_ev)
                    MM_LD[0] = S_["mm"].val

                if "dbg_wpe" in dbg:
                    dw = dram_dbg("dbg_wpe", [128, NT, 9, 8])
                    dv0 = S_["dv"].val
                    blk.gpsimd(lambda e: (e.wait_ge(S_["dv"].h, dv0),
                                          S_["gst"].inc(e.dma_start(out=dw, in_=wpe[:]), 16)))

        if "l1qkv" in phases:
            with ExitStack() as _es2:
                qfm1g = _es.enter_context(ExitStack())
                qfm1 = qfm1g.enter_context(nc.sbuf_tensor("qfm1", [128, DCH, TOK], BF16))
                w3 = _es2.enter_context(nc.sbuf_tensor("w3", [128, DCH, D], BF16))
                kvo1 = _es2.enter_context(nc.sbuf_tensor("kvo1", [128, 2, 512], BF16))
                vfm1 = _es2.enter_context(nc.sbuf_tensor("vfm1", [128, DCH, 512], BF16))
                vto1 = _es2.enter_context(nc.sbuf_tensor("vto1", [128, 1, VCOL * H], BF16))

                MM_PRE1 = S_["mm"].val
                DV_PRE1 = S_["dv"].val
                ln_sweep([("sbuf", x_res[:, t, :]) for t in range(NT)], xfm, 0,
                         first_evac_wait=(S_["mm"], MM_PRE1))
                EV_LN3 = S_["ev"].val

                def f_b1(e):
                    e.wait_ge(S_["mm"].h, MM_PRE1)
                    e.wait_ge(S_["dv"].h, DV_PRE1)
                    for third in range(3):
                        S_["ldw"].inc(e.dma_start(
                            out=bias1[:, BQK + 8 * third:BQK + 8 * third + 8, 0:1],
                            in_=l1_bqkv[third * D:(third + 1) * D].rearrange("(c p) -> p c", p=128).unsqueeze(2),
                            allow_slow_non_contiguous=True), 16)
                    S_["ldw"].inc(e.dma_start(
                        out=bias1[:, BG1:BG1 + 8, 0:1],
                        in_=l1_bgate.rearrange("(c p) -> p c", p=128).unsqueeze(2),
                        allow_slow_non_contiguous=True), 16)
                blk.sync(f_b1)
                LDW_B1 = S_["ldw"].val
                DV_QB = S_["dv"].val

                def f_qb(e):
                    e.wait_ge(S_["ldw"].h, LDW_B1)
                    S_["dv"].inc(e.tensor_scalar_mul(bias1[:, BQK:BQK + 8, 0:1], bias1[:, BQK:BQK + 8, 0:1], 0.125))
                blk.vector(f_qb)
                DV_QB2 = S_["dv"].val

                def f_w3q(e):
                    e.wait_ge(S_["mm"].h, MM_PRE1)
                    e.wait_ge(S_["dv"].h, DV_PRE1)
                    S_["gw"].inc(e.dma_start(out=w3[:], in_=l1_wqkvT.rearrange("(c p) o -> p c o", p=128)[:, :, 0:D]), 16)
                blk.gpsimd(f_w3q)
                GW_Q1 = S_["gw"].val

                def f_qwait(e):
                    e.wait_ge(S_["dv"].h, DV_QB2)
                blk.scalar(f_qwait)
                std_proj(w3, lambda ts: xfm[:, :, ts * 512:(ts + 1) * 512],
                         lambda oi, ts: qfm1[:, oi, ts * 512:(ts + 1) * 512],
                         list(range(DCH)), list(range(4)), AF.Identity, BQK, scale=0.125,
                         wait_sems=[(S_["ev"], EV_LN3), (S_["gw"], GW_Q1)])
                MM_Q1 = S_["mm"].val

                def f_w3k(e):
                    e.wait_ge(S_["mm"].h, MM_Q1)
                    S_["gw"].inc(e.dma_start(out=w3[:], in_=l1_wqkvT.rearrange("(c p) o -> p c o", p=128)[:, :, D:2 * D]), 16)
                blk.gpsimd(f_w3k)
                GW_K1 = S_["gw"].val

                MM_KDONE = S_["mm"].val
                for qc in range(QC):
                    KV_K0 = S_["kv"].val
                    AC_K0 = S_["ac"].val
                    std_proj(w3, lambda ts, qc=qc: xfm[:, :, qc * 512:(qc + 1) * 512],
                             lambda oi, ts: kvo1[:, oi % 2, :], list(range(DCH)), [0], AF.Identity, BQK + 8,
                             wait_sems=[(S_["gw"], GW_K1)],
                             slot_free=lambda idx, KV_K0=KV_K0: (S_["kv"], KV_K0 + 16 * (idx - 1)))

                    def f_spk1(e, qc=qc, AC_K0=AC_K0):
                        for oc in range(DCH):
                            e.wait_ge(S_["ac"].h, AC_K0 + oc + 1)
                            dst = cc_in[qc, 0:CHK].rearrange("(c p t) -> c p t", c=DCH, p=128)[oc]
                            S_["kv"].inc(e.dma_start(out=dst, in_=kvo1[:, oc % 2, :]), 16)
                    blk.sync(f_spk1)
                MM_KALL = S_["mm"].val

                def f_w3v(e):
                    e.wait_ge(S_["mm"].h, MM_KALL)
                    S_["gw"].inc(e.dma_start(out=w3[:], in_=l1_wqkvT.rearrange("(c p) o -> p c o", p=128)[:, :, 2 * D:3 * D]), 16)
                blk.gpsimd(f_w3v)
                GW_V1 = S_["gw"].val

                first_v = [True]
                for qc in range(QC):
                    std_proj(w3, lambda ts, qc=qc: xfm[:, :, qc * 512:(qc + 1) * 512],
                             lambda oi, ts: vfm1[:, oi, :], list(range(DCH)), [0], AF.Identity, BQK + 16,
                             wait_sems=[(S_["gw"], GW_V1), (S_["kv"], S_["kv"].val)])
                    AC_V1 = S_["ac"].val
                    tp0, ev0 = S_["tp"].val, S_["ev"].val

                    def f_tv1(e, AC_V1=AC_V1, tp0=tp0, ev0=ev0):
                        e.wait_ge(S_["ac"].h, AC_V1)
                        for tt in range(4):
                            e.wait_ge(S_["ev"].h, ev0 + tt)
                            for c in range(DCH):
                                ins = e.transpose(psT[:, c * 128:(c + 1) * 128],
                                                  vfm1[:, c, tt * 128:(tt + 1) * 128], identt[:])
                            S_["tp"].inc(ins)
                    blk.tensor(f_tv1)
                    if first_v[0]:
                        def f_ones1(e):
                            S_["dv"].inc(e.memset(
                                vto1[:, 0, :].rearrange("p (h x) -> p h x", x=VCOL)[:, :, 64:65], 1.0))
                        blk.vector(f_ones1)
                        first_v[0] = False
                    DV_ONES1 = S_["dv"].val
                    kv0v = S_["kv"].val
                    EV_V0 = S_["ev"].val

                    def f_sv1(e, tp0=tp0, kv0v=kv0v):
                        for tt in range(4):
                            e.wait_ge(S_["tp"].h, tp0 + tt + 1)
                            e.wait_ge(S_["kv"].h, kv0v + 16 * tt)
                            dst = vto1[:, 0, :].rearrange("p (h x) -> p h x", x=VCOL)[:, :, 0:64]
                            S_["ev"].inc(e.copy(out=dst, in_=psT[:].rearrange("p (h x) -> p h x", x=64)))
                    blk.scalar(f_sv1)

                    def f_spv1(e, qc=qc, EV_V0=EV_V0, DV_ONES1=DV_ONES1):
                        e.wait_ge(S_["dv"].h, DV_ONES1)
                        for tt in range(4):
                            e.wait_ge(S_["ev"].h, EV_V0 + tt + 1)
                            dst = cc_in[qc, CHK:CHK + CHV].rearrange("(f p t) -> f p t", f=4, p=128)[tt]
                            S_["kv"].inc(e.dma_start(out=dst, in_=vto1[:, 0, :]), 16)
                    blk.sync(f_spv1)

                if "dbg_q1" in dbg:
                    dq1 = dram_dbg("dbg_q1", [128, DCH, TOK], U16)
                    _mmq = S_["mm"].val
                    _acq = S_["ac"].val
                    blk.gpsimd(lambda e: (e.wait_ge(S_["ac"].h, _acq),
                                          S_["gst"].inc(e.dma_start(out=dq1.bitcast(BF16), in_=qfm1[:]), 16)))
                MM_G1 = S_["mm"].val

                def f_wg1(e):
                    e.wait_ge(S_["mm"].h, MM_G1)
                    S_["gw"].inc(e.dma_start(out=w3[:], in_=l1_wgateT.rearrange("(c p) o -> p c o", p=128)), 16)
                blk.gpsimd(f_wg1)
                GW_G1 = S_["gw"].val
                for ts in range(4):
                    KV_G0 = S_["kv"].val
                    AC_G0 = S_["ac"].val
                    std_proj(w3, lambda _ts, ts=ts: xfm[:, :, ts * 512:(ts + 1) * 512],
                             lambda oi, _ts: kvo1[:, oi % 2, :], list(range(DCH)), [0], AF.Sigmoid, BG1,
                             wait_sems=[(S_["gw"], GW_G1)],
                             slot_free=lambda idx, KV_G0=KV_G0: (S_["kv"], KV_G0 + 16 * (idx - 1)))

                    def f_spg(e, ts=ts, AC_G0=AC_G0):
                        for oc in range(DCH):
                            e.wait_ge(S_["ac"].h, AC_G0 + oc + 1)
                            S_["kv"].inc(e.dma_start(out=l1g[oc, :, ts * 512:(ts + 1) * 512],
                                                     in_=kvo1[:, oc % 2, :]), 16)
                    blk.sync(f_spg)

                if "dbg_cc" in dbg:
                    dcc = dram_dbg("dbg_cc", [QC, 2048], U16)
                    _kvc = S_["kv"].val
                    blk.gpsimd(lambda e: (e.wait_ge(S_["kv"].h, _kvc),
                                          S_["gst"].inc(e.dma_start(out=dcc.bitcast(BF16),
                                                                    in_=cc_in[:, 0:2048]), 16)))
                KV_CC = S_["kv"].val

                def f_ag(e):
                    e.wait_ge(S_["kv"].h, KV_CC)
                    e.wait_ge(S_["gz"].h, KV_PAD)
                    S_["cc"].inc(e.collective_compute(
                        "AllGather", ALU.bypass,
                        ins=[cc_in[:]], outs=[cc_out.rearrange("a b c -> (a b) c")],
                        replica_groups=[[0, 1], [2, 3], [4, 5], [6, 7]]), 1)
                blk.gpsimd(f_ag)
                CC_DONE = S_["cc"].val

        if "l1attn" in phases:
            with ExitStack() as _es2:
                kst1 = _es2.enter_context(nc.sbuf_tensor("kst1", [128, 4, 128], BF16))
                vst1 = _es2.enter_context(nc.sbuf_tensor("vst1", [128, 4, 130], BF16))
                btst = _es2.enter_context(nc.sbuf_tensor("btst", [128, 3, 2, 512], BF16))
                pT1 = _es2.enter_context(nc.sbuf_tensor("pT1", [128, 2, 1024], BF16))
                sbcol = _es2.enter_context(nc.sbuf_tensor("sbcol", [128, QC, L1_NJ, 8], F32))
                wog = _es2.enter_context(nc.sbuf_tensor("wog", [128, DCH, D], BF16))
                ctx1 = xfm

                MM_A1 = S_["mm"].val

                def f_sb(e):
                    e.wait_ge(S_["mm"].h, MM_A1)
                    e.wait_ge(S_["cc"].h, CC_DONE)
                    S_["ldw"].inc(e.dma_start(out=sbcol[:], in_=l1_sb[:]), 16)
                blk.sync(f_sb)
                LDW_SB = S_["ldw"].val

                for hs in range(8):
                    for qc in range(QC):
                        nj = L1_NJ
                        ldk0, mm0, ac0, dv0 = (S_[k].val for k in ("ldk", "mm", "ac", "dv"))
                        QK1 = lambda j, mm0=mm0: mm0 + 1 if j == 0 else mm0 + 2 * j
                        AV1 = lambda j, mm0=mm0, nj=nj: mm0 + 2 * j + (3 if j < nj - 1 else 2)

                        gw0b = S_["gw"].val

                        def f_dma1(e, qc=qc, nj=nj, mm0=mm0, hs=hs, AV1=AV1):
                            for j in range(nj):
                                if j >= 4:
                                    e.wait_ge(S_["mm"].h, AV1(j - 4))
                                else:
                                    e.wait_ge(S_["mm"].h, mm0)
                                b13 = 8 * qc + j - 13
                                g, tt = b13 // 4, b13 % 4
                                if g < 0:
                                    ksrc = l1pad[(g % 4), 0:CHK].rearrange("(c p t) -> c p t", c=DCH, p=128)
                                    vsrc = l1pad[(g % 4), CHK:CHK + CHV].rearrange("(f p t) -> f p t", f=4, p=128)
                                else:
                                    ksrc = cc_out[g % 2, g // 2, 0:CHK].rearrange("(c p t) -> c p t", c=DCH, p=128)
                                    vsrc = cc_out[g % 2, g // 2, CHK:CHK + CHV].rearrange("(f p t) -> f p t", f=4, p=128)
                                S_["ldk"].inc(e.dma_start(out=kst1[:, j % 4, :],
                                                          in_=ksrc[hs, :, tt * 128:(tt + 1) * 128]), 16)
                                S_["ldk"].inc(e.dma_start(out=vst1[:, j % 4, :],
                                                          in_=vsrc[tt, :, 130 * hs:130 * hs + 130]), 16)
                        blk.sync(f_dma1)

                        def f_bdma1(e, qc=qc, nj=nj, mm0=mm0, dv0=dv0, hs=hs, AV1=AV1):
                            for j in range(nj):
                                if j >= 3:
                                    e.wait_ge(S_["dv"].h, dv0 + 2 * (j - 3) + 2)
                                else:
                                    e.wait_ge(S_["dv"].h, max(dv0 - 4, 0))
                                S_["gw"].inc(e.dma_start(out=btst[:, j % 3, :, :],
                                                          in_=l1_btile[j, 2 * hs:2 * hs + 2].rearrange("h p t -> p h t").bitcast(BF16)), 16)
                        blk.gpsimd(f_bdma1)

                        def f_pe1(e, qc=qc, nj=nj, ldk0=ldk0, mm0=mm0, ac0=ac0, dv0=dv0, hs=hs):
                            if hs == 0 and qc == 0:
                                e.wait_ge(S_["ldw"].h, LDW_SB)

                            def emit_qk(j):
                                strip = (psA, psB)[j % 2]
                                e.wait_ge(S_["ldk"].h, ldk0 + 32 * j + 32)
                                if j >= 2:
                                    e.wait_ge(S_["ac"].h, ac0 + j - 1)
                                elif not (hs == 0 and qc == 0):
                                    e.wait_ge(S_["ac"].h, ac0 - 2)
                                for hp in range(2):
                                    ins = e.matmul(strip[:, hp * 512:(hp + 1) * 512],
                                                   kst1[hp * 64:(hp + 1) * 64, j % 4, :],
                                                   qfm1[hp * 64:(hp + 1) * 64, hs, qc * 512:(qc + 1) * 512],
                                                   start=True, stop=True)
                                S_["mm"].inc(ins)

                            def emit_av(j):
                                e.wait_ge(S_["ac"].h, ac0 + j + 1)
                                if j == 0:
                                    e.wait_ge(S_["dv"].h, dv0)
                                for hp in range(2):
                                    avp = psC[0:65, hp * 512:(hp + 1) * 512]
                                    ins = e.matmul(avp, vst1[:, j % 4, hp * 65:(hp + 1) * 65],
                                                   pT1[:, j % 2, hp * 512:(hp + 1) * 512],
                                                   start=(j == 0), stop=(j == nj - 1))
                                S_["mm"].inc(ins)

                            emit_qk(0)
                            for j in range(1, nj):
                                emit_qk(j)
                                emit_av(j - 1)
                            emit_av(nj - 1)
                        blk.tensor(f_pe1)

                        def f_badd(e, qc=qc, nj=nj, mm0=mm0, dv0=dv0, gw0b=gw0b, QK1=QK1, AV1=AV1):
                            for j in range(nj):
                                strip = (psA, psB)[j % 2]
                                e.wait_ge(S_["mm"].h, QK1(j))
                                e.wait_ge(S_["gw"].h, gw0b + 16 * (j + 1))
                                for hp in range(2):
                                    ins = e.tensor_add(strip[:, hp * 512:(hp + 1) * 512],
                                                       strip[:, hp * 512:(hp + 1) * 512],
                                                       btst[:, j % 3, hp, :])
                                    S_["dv"].inc(ins)
                        blk.vector(f_badd)

                        def f_exp1(e, qc=qc, nj=nj, mm0=mm0, dv0=dv0, QK1=QK1, AV1=AV1):
                            for j in range(nj):
                                strip = (psA, psB)[j % 2]
                                e.wait_ge(S_["dv"].h, dv0 + 2 * j + 2)
                                if j >= 2:
                                    e.wait_ge(S_["mm"].h, AV1(j - 2))
                                S_["ac"].inc(e.activation(out=pT1[:, j % 2, :], in_=strip[:],
                                                          func=AF.Exp, bias=sbcol[:, qc, j, 0:1]))
                        blk.scalar(f_exp1)

                        MM_AVE1 = S_["mm"].val

                        def f_nv1(e, MM_AVE1=MM_AVE1):
                            e.wait_ge(S_["mm"].h, MM_AVE1)
                            S_["dv"].inc(e.reciprocal(rrow[0:1, 0:512], psC[64:65, 0:512]))
                            S_["dv"].inc(e.reciprocal(rrow[0:1, 512:1024], psC[64:65, 512:1024]))
                        blk.vector(f_nv1)
                        DV_R1 = S_["dv"].val
                        mm1 = S_["mm"].val

                        def f_nt1(e, DV_R1=DV_R1):
                            e.wait_ge(S_["dv"].h, DV_R1)
                            rbdst = (psD[0:64, :], psD[64:128, :])
                            for hp in range(2):
                                S_["mm"].inc(e.matmul(rbdst[hp], onesf[0:1, :],
                                                      rrow[0:1, 512 * hp:512 * hp + 512], start=True, stop=True))
                        blk.tensor(f_nt1)
                        AC_RB1 = S_["ac"].val

                        def f_nrb1(e, mm1=mm1):
                            rbdst = (psD[0:64, :], psD[64:128, :])
                            for hp in range(2):
                                e.wait_ge(S_["mm"].h, mm1 + hp + 1)
                                S_["ac"].inc(e.copy(out=scr2[0:64, 512 * hp:512 * hp + 512], in_=rbdst[hp]))
                        blk.scalar(f_nrb1)

                        def f_nv21(e, qc=qc, hs=hs, AC_RB1=AC_RB1):
                            for hp in range(2):
                                e.wait_ge(S_["ac"].h, AC_RB1 + hp + 1)
                                S_["dv"].inc(e.tensor_mul(
                                    ctx1[hp * 64:(hp + 1) * 64, hs, qc * 512:(qc + 1) * 512],
                                    psC[0:64, hp * 512:(hp + 1) * 512],
                                    scr2[0:64, 512 * hp:512 * hp + 512]))
                        blk.vector(f_nv21)

                DV_CTX1 = S_["dv"].val
                if "dbg_ctx1" in dbg:
                    dc1 = dram_dbg("dbg_ctx1", [128, DCH, TOK], U16)
                    blk.gpsimd(lambda e: (e.wait_ge(S_["dv"].h, DV_CTX1),
                                          S_["gst"].inc(e.dma_start(out=dc1.bitcast(BF16), in_=ctx1[:]), 16)))

                def f_wgl(e):
                    e.wait_ge(S_["mm"].h, MM_A1)
                    S_["gw"].inc(e.dma_start(out=wog[:], in_=l1_woutT.rearrange("(c p) o -> p c o", p=128)), 16)
                blk.gpsimd(f_wgl)
                GW_WOUT = S_["gw"].val
                ldk0g = S_["ldk"].val
                dv0g = S_["dv"].val

                def f_gld(e, ldk0g=ldk0g, dv0g=dv0g):
                    for t in range(NT):
                        if t == 0:
                            e.wait_ge(S_["dv"].h, DV_CTX1)
                        if t >= 2:
                            e.wait_ge(S_["dv"].h, dv0g + t - 1)
                        S_["ldk"].inc(e.dma_start(
                            out=btst[:, t % 2, :, :].rearrange("p a b -> p (a b)").rearrange("p (c x) -> p c x", c=8),
                            in_=l1g[:, :, t * 128:(t + 1) * 128].rearrange("c p t -> p c t")), 16)
                blk.sync(f_gld)

                def f_gmul(e, ldk0g=ldk0g, dv0g=dv0g):
                    for t in range(NT):
                        e.wait_ge(S_["ldk"].h, ldk0g + 16 * (t + 1))
                        S_["dv"].inc(e.tensor_mul(
                            ctx1[:, :, t * 128:(t + 1) * 128],
                            ctx1[:, :, t * 128:(t + 1) * 128],
                            btst[:, t % 2, :, :].rearrange("p a b -> p (a b)").rearrange("p (c x) -> p c x", c=8)))
                blk.vector(f_gmul)
                DV_GM = S_["dv"].val

                swap_proj_residual(ctx1, wog, DCH, 1, AF.Identity,
                                   wait_sems=[(S_["dv"], DV_GM), (S_["gw"], GW_WOUT)])
            qfm1g.close()

        if "l1ffn" in phases:
            with ExitStack() as _es2:
                h1 = _es2.enter_context(nc.sbuf_tensor("h1", [128, FCH, 512], BF16))
                cup = _es2.enter_context(nc.sbuf_tensor("cup", [128, FCH, 128], BF16))
                cdn = _es2.enter_context(nc.sbuf_tensor("cdn", [128, FCH, 128], BF16))
                mstg = _es2.enter_context(nc.sbuf_tensor("mstg", [128, 7, 128], BF16))
                xsc = _es2.enter_context(nc.sbuf_tensor("xsc", [128, 128], BF16))
                btmp = _es2.enter_context(nc.sbuf_tensor("btmp", [128, 4, 3, 512], BF16))
                wexh = _es2.enter_context(nc.sbuf_tensor("wexh", [128, 4, DCH, 128], BF16))
                wconh = _es2.enter_context(nc.sbuf_tensor("wconh", [128, 4, D], BF16))
                bwal = _es2.enter_context(nc.sbuf_tensor("bwal", [128, 2, 5, 16, 2, 2], F32))

                MM_F0 = S_["mm"].val
                def f_fb(e):
                    e.wait_ge(S_["mm"].h, MM_F0)
                    S_["ldw"].inc(e.dma_start(out=bias1[:, 0:32, 0:1],
                                              in_=ff_bexp.rearrange("(c p) -> p c", p=128).unsqueeze(2),
                                              allow_slow_non_contiguous=True), 16)
                    S_["ldw"].inc(e.dma_start(out=bias1[:, 32:64, 0:1],
                                              in_=ff_upb.rearrange("(c p) -> p c", p=128).unsqueeze(2),
                                              allow_slow_non_contiguous=True), 16)
                    for ud, wsrc in ((0, ff_upw), (1, ff_dnw)):
                        for s5 in range(5):
                            sa = s5 + 7
                            for ai, ch in enumerate([c for c in range(FCH) if not (c >> (sa - 7)) & 1]):
                                kbase = (ch >> (sa - 6)) * (2 ** sa) + (ch % (2 ** (sa - 7))) * 128
                                S_["ldw"].inc(e.dma_start(
                                    out=bwal[:, ud, s5, ai, :, :],
                                    in_=wsrc[sa, kbase:kbase + 128, :, :],
                                    allow_slow_non_contiguous=True), 16)
                blk.sync(f_fb)
                LDW_FB = S_["ldw"].val

                def compose(c_dst, stsrc, tag):
                    ldk0, tp0, ev0, mm0, ac0 = (S_[k].val for k in ("ldk", "tp", "ev", "mm", "ac"))

                    def f_cd(e, ldk0=ldk0, ac0=ac0):
                        e.wait_ge(S_["mm"].h, MM_F0)
                        for ch in range(FCH):
                            if ch >= 1:
                                e.wait_ge(S_["ac"].h, ac0 + 7 * ch)
                            S_["gz"].inc(e.dma_start(out=mstg[:],
                                                      in_=stsrc[:, ch, :, :].rearrange("s p t -> p s t")), 16)
                    blk.gpsimd(f_cd)

                    gz0 = S_["gz"].val - 16 * FCH

                    def f_ct(e, gz0=gz0, tp0=tp0, ev0=ev0, mm0=mm0, ac0=ac0):
                        for ch in range(FCH):
                            e.wait_ge(S_["gz"].h, gz0 + 16 * (ch + 1))
                            if ch >= 1:
                                e.wait_ge(S_["ac"].h, ac0 + 7 * ch)
                            ins = e.transpose(psT[:, 0:128], mstg[:, 6, :], identt[:])
                            S_["tp"].inc(ins)
                            for s in range(5, -1, -1):
                                e.wait_ge(S_["ac"].h, ac0 + 7 * ch + (6 - s))
                                ins = e.matmul(psA[:, 0:128], mstg[:, s, :], xsc[:], start=True, stop=True)
                                S_["mm"].inc(ins)
                    blk.tensor(f_ct)

                    def f_cs(e, tp0=tp0, mm0=mm0, ac0=ac0, c_dst=c_dst):
                        for ch in range(FCH):
                            e.wait_ge(S_["tp"].h, tp0 + ch + 1)
                            S_["ac"].inc(e.copy(out=xsc[:], in_=psT[:, 0:128]))
                            for s in range(5, -1, -1):
                                e.wait_ge(S_["mm"].h, mm0 + 6 * ch + (6 - s))
                                dst = xsc[:] if s > 0 else c_dst[:, ch, :]
                                S_["ac"].inc(e.copy(out=dst, in_=psA[:, 0:128]))
                    blk.scalar(f_cs)
                compose(cup, ff_upT, "up")
                compose(cdn, ff_dnT, "dn")
                LDW_C = S_["ldw"].val

                MM_PRE4 = S_["mm"].val
                ln_sweep([("sbuf", x_res[:, t, :]) for t in range(NT)], xfm, 0,
                         first_evac_wait=(S_["mm"], MM_PRE4))
                EV_LN4 = S_["ev"].val

                slots = [psA[:, 0:512], psA[:, 512:1024], psB[:, 0:512], psB[:, 512:1024]]

                def bfly_pairs():
                    out = []
                    for s5 in range(5):
                        step = 2 ** s5
                        achs = [c for c in range(FCH) if not (c >> s5) & 1]
                        for ai, ch in enumerate(achs):
                            out.append((ch, ch + step, s5, ai))
                    return out

                def butterfly_large(ts, ud, cols):
                    pairs = bfly_pairs()
                    dvb = S_["dv"].val
                    acb = S_["ac"].val
                    lk0 = S_["ldk"].val

                    def f_a(e, pairs=pairs, dvb=dvb, ud=ud):
                        e.wait_ge(S_["dv"].h, dvb)
                        for q, (a, b, s5, ai) in enumerate(pairs):
                            u = q % 4
                            if q >= 4:
                                e.wait_ge(S_["dv"].h, dvb + 4 * (q // 2 - 1))
                            if q % 16 < 2 and q >= 16:
                                e.wait_ge(S_["dv"].h, dvb + 32 * (q // 16))
                            S_["ac"].inc(e.activation(out=btmp[:, u, 1, :],
                                                      in_=h1[:, b, :], func=AF.Identity,
                                                      scale=cols[:, ud, s5, ai, 0, 1:2]))
                            S_["ac"].inc(e.activation(out=btmp[:, u, 2, :],
                                                      in_=h1[:, b, :], func=AF.Identity,
                                                      scale=cols[:, ud, s5, ai, 1, 1:2]))
                    blk.scalar(f_a)

                    def f_cp(e, pairs=pairs, dvb=dvb, acb=acb, ud=ud):
                        e.wait_ge(S_["dv"].h, dvb)
                        e.wait_ge(S_["ac"].h, acb)
                        for q, (a, b, s5, ai) in enumerate(pairs):
                            u = q % 4
                            if q >= 4:
                                e.wait_ge(S_["dv"].h, dvb + 4 * (q // 2 - 1))
                            if q % 16 < 2 and q >= 16:
                                e.wait_ge(S_["dv"].h, dvb + 32 * (q // 16))
                            S_["ldk"].inc(e.dma_start(out=btmp[:, u, 0, :], in_=h1[:, a, :]), 16)
                    blk.sync(f_cp)

                    def f_v(e, pairs=pairs, acb=acb, lk0=lk0, ud=ud):
                        for g in range(len(pairs) // 2):
                            e.wait_ge(S_["ac"].h, acb + 4 * (g + 1))
                            e.wait_ge(S_["ldk"].h, lk0 + 16 * (2 * g + 2))
                            P = (pairs[2 * g], pairs[2 * g + 1])
                            U = ((2 * g) % 4, (2 * g + 1) % 4)
                            for u in range(2):
                                a, b, s5, ai = P[u]
                                S_["dv"].inc(e.scalar_tensor_tensor(
                                    h1[:, a, :], btmp[:, U[u], 0, :], cols[:, ud, s5, ai, 0, 0:1],
                                    btmp[:, U[u], 1, :], ALU.mult, ALU.add))
                            for u in range(2):
                                a, b, s5, ai = P[u]
                                S_["dv"].inc(e.scalar_tensor_tensor(
                                    h1[:, b, :], btmp[:, U[u], 0, :], cols[:, ud, s5, ai, 1, 0:1],
                                    btmp[:, U[u], 2, :], ALU.mult, ALU.add))
                    blk.vector(f_v)

                for ts in range(4):
                    mm0, ac0, gw0, lk0 = S_["mm"].val, S_["ac"].val, S_["gw"].val, S_["ldk"].val

                    def xw_gate(e, oc, mm0):
                        if oc < 4:
                            e.wait_ge(S_["mm"].h, mm0)
                        else:
                            e.wait_ge(S_["mm"].h, mm0 + (oc - 4) + 1)

                    def f_xw_g(e, ts=ts, mm0=mm0):
                        for oc in range(FCH):
                            xw_gate(e, oc, mm0)
                            S_["gw"].inc(e.dma_start(
                                out=wexh[:, oc % 4, :, :],
                                in_=ff_wexp[:, oc * 128:(oc + 1) * 128].rearrange("(c p) o -> p c o", p=128)), 16)
                    blk.gpsimd(f_xw_g)

                    DV_PREV = S_["dv"].val

                    def f_xt(e, ts=ts, mm0=mm0, ac0=ac0, gw0=gw0, lk0=lk0, DV_PREV=DV_PREV):
                        if ts == 0:
                            e.wait_ge(S_["ev"].h, EV_LN4)
                            e.wait_ge(S_["ldw"].h, LDW_C)
                        e.wait_ge(S_["dv"].h, DV_PREV)
                        for oc in range(FCH):
                            e.wait_ge(S_["gw"].h, gw0 + 16 * (oc + 1))
                            if oc >= 4:
                                e.wait_ge(S_["ac"].h, ac0 + oc - 3)
                            for c in range(DCH):
                                ins = e.matmul(slots[oc % 4], wexh[:, oc % 4, c, :],
                                               xfm[:, c, ts * 512:(ts + 1) * 512],
                                               start=(c == 0), stop=(c == DCH - 1))
                            S_["mm"].inc(ins)
                    blk.tensor(f_xt)

                    def f_xs(e, mm0=mm0, ac0=ac0):
                        for oc in range(FCH):
                            e.wait_ge(S_["mm"].h, mm0 + oc + 1)
                            S_["ac"].inc(e.activation(out=h1[:, oc, :], in_=slots[oc % 4],
                                                      func=AF.Identity, bias=bias1[:, oc, 0:1]))
                    blk.scalar(f_xs)
                    AC_EXP = S_["ac"].val

                    mm1 = S_["mm"].val
                    ac1 = S_["ac"].val

                    def f_bu_t(e, mm1=mm1, ac1=ac1, AC_EXP=AC_EXP):
                        e.wait_ge(S_["ac"].h, AC_EXP)
                        for ch in range(FCH):
                            if ch >= 4:
                                e.wait_ge(S_["ac"].h, ac1 + ch - 3)
                            ins = e.matmul(slots[ch % 4], cup[:, ch, :], h1[:, ch, :], start=True, stop=True)
                            S_["mm"].inc(ins)
                    blk.tensor(f_bu_t)

                    def f_bu_s(e, mm1=mm1):
                        for ch in range(FCH):
                            e.wait_ge(S_["mm"].h, mm1 + ch + 1)
                            S_["ac"].inc(e.copy(out=h1[:, ch, :], in_=slots[ch % 4]))
                    blk.scalar(f_bu_s)
                    if ts == 0:
                        blk.scalar(lambda e: e.wait_ge(S_["ldw"].h, LDW_FB))
                    butterfly_large(ts, 0, bwal)
                    DV_BU = S_["dv"].val

                    ac2 = S_["ac"].val

                    def f_gel(e, DV_BU=DV_BU):
                        e.wait_ge(S_["dv"].h, DV_BU)
                        for ch in range(FCH):
                            S_["ac"].inc(e.activation(out=h1[:, ch, :], in_=h1[:, ch, :],
                                                      func=AF.Gelu, bias=bias1[:, 32 + ch, 0:1]))
                    blk.scalar(f_gel)
                    AC_GEL = S_["ac"].val

                    mm2 = S_["mm"].val
                    ac3 = S_["ac"].val

                    def f_bd_t(e, mm2=mm2, ac3=ac3, AC_GEL=AC_GEL):
                        e.wait_ge(S_["ac"].h, AC_GEL)
                        for ch in range(FCH):
                            if ch >= 4:
                                e.wait_ge(S_["ac"].h, ac3 + ch - 3)
                            ins = e.matmul(slots[ch % 4], cdn[:, ch, :], h1[:, ch, :], start=True, stop=True)
                            S_["mm"].inc(ins)
                    blk.tensor(f_bd_t)

                    def f_bd_s(e, mm2=mm2):
                        for ch in range(FCH):
                            e.wait_ge(S_["mm"].h, mm2 + ch + 1)
                            S_["ac"].inc(e.copy(out=h1[:, ch, :], in_=slots[ch % 4]))
                    blk.scalar(f_bd_s)
                    butterfly_large(ts, 1, bwal)
                    DV_BD = S_["dv"].val

                    for rnd in range(2):
                        mm3, gw3, dv3 = S_["mm"].val, S_["gw"].val, S_["dv"].val
                        pss = (psC, psB)

                        def f_cw(e, ts=ts, rnd=rnd, mm3=mm3, DV_BD=DV_BD):
                            for hc in range(FCH):
                                if hc >= 4:
                                    e.wait_ge(S_["mm"].h, mm3 + (hc - 4) + 1)
                                else:
                                    e.wait_ge(S_["mm"].h, mm3)
                                S_["gw"].inc(e.dma_start(
                                    out=wconh[:, hc % 4, :],
                                    in_=ff_wcon[hc * 128:(hc + 1) * 128, :]), 16)
                        blk.gpsimd(f_cw)

                        def f_c_t(e, ts=ts, rnd=rnd, mm3=mm3, gw3=gw3, dv3=dv3, DV_BD=DV_BD):
                            e.wait_ge(S_["dv"].h, DV_BD)
                            if rnd == 1:
                                e.wait_ge(S_["dv"].h, dv3)
                            for hc in range(FCH):
                                e.wait_ge(S_["gw"].h, gw3 + 16 * (hc + 1))
                                for t2 in range(2):
                                    tt = ts * 4 + rnd * 2 + t2
                                    lt = tt % 4
                                    e.matmul(pss[t2][:, 0:512], h1[:, hc, lt * 128:(lt + 1) * 128],
                                             wconh[:, hc % 4, 0:512], start=(hc == 0), stop=(hc == FCH - 1))
                                    ins = e.matmul(pss[t2][:, 512:1024], h1[:, hc, lt * 128:(lt + 1) * 128],
                                                   wconh[:, hc % 4, 512:1024], start=(hc == 0), stop=(hc == FCH - 1))
                                S_["mm"].inc(ins, 1)
                        blk.tensor(f_c_t)
                        MM_C = S_["mm"].val

                        def f_c_v(e, ts=ts, rnd=rnd, MM_C=MM_C):
                            e.wait_ge(S_["mm"].h, MM_C)
                            for t2 in range(2):
                                tt = ts * 4 + rnd * 2 + t2
                                e.tensor_add(scr2[:], pss[t2][:], bcast[:, 2, :])
                                S_["dv"].inc(e.tensor_add(x_res[:, tt, :], x_res[:, tt, :], scr2[:]))
                        blk.vector(f_c_v)

        if "dbg_res2" in dbg:
            dr2 = dram_dbg("dbg_res2", [128, NT, D])
            dv0 = S_["dv"].val
            blk.gpsimd(lambda e: (e.wait_ge(S_["dv"].h, dv0),
                                  S_["gst"].inc(e.dma_start(out=dr2, in_=x_res[:]), 16)))

        if "dbg_res1" in dbg:
            dr = dram_dbg("dbg_res1", [128, NT, D])
            dv0 = S_["dv"].val
            blk.gpsimd(lambda e: (e.wait_ge(S_["dv"].h, dv0),
                                  S_["gst"].inc(e.dma_start(out=dr, in_=x_res[:]), 16)))

        DV_END = S_["dv"].val
        MM_END = S_["mm"].val
        AC_END = S_["ac"].val

        def f_out(e):
            e.wait_ge(S_["dv"].h, DV_END)
            e.wait_ge(S_["mm"].h, MM_END)
            e.wait_ge(S_["ac"].h, AC_END)
            S_["st"].inc(e.dma_start(out=out_own.rearrange("(t p) d -> p t d", p=128), in_=x_res[:]), 16)
        blk.sync(f_out)
        ST_END = S_["st"].val
        blk.gpsimd(lambda e: (e.wait_ge(S_["st"].h, ST_END), e.wait_ge(S_["gst"].h, S_["gst"].val)))

    return nc, inputs, dbg_names


OFFSETS = sorted(set(range(0, 65)) | {96, 128, 192, 256, 384, 512, 768, 1024, 1536})


def own_rows(core):
    seq, r = core // 2, core % 2
    rows = []
    for qc in range(QC):
        g = 2 * qc + r
        rows.append(np.arange(seq * S + g * 512, seq * S + g * 512 + 512))
    return np.concatenate(rows)


def make_winmask(r):
    w = np.arange(8)[:, None, None]
    p = np.arange(128)[None, :, None]
    f = np.arange(512)[None, None, :]
    return ((w * 128 + p) <= (r * 512 + f)).astype(bfd).view(np.uint16)


def make_l1_btile(r, pos_bias):
    o2idx = -np.ones(4096, dtype=np.int64)
    for i, o in enumerate(OFFSETS):
        o2idx[o] = i
    jj = np.arange(L1_NJ)[:, None, None]
    p = np.arange(128)[None, :, None]
    f = np.arange(512)[None, None, :]
    o = (r * 512 + (13 - jj) * 128) + f - p
    valid = (o >= 0) & (o <= 1536)
    idx = np.where(valid, o2idx[np.clip(o, 0, 4095)], -1)
    valid = idx >= 0
    pb = pos_bias[np.clip(idx, 0, 73), :]
    out = np.where(valid[..., None], pb, np.float32(-1e9)).astype(bfd)
    return np.ascontiguousarray(out.transpose(0, 3, 1, 2)).view(np.uint16)


def make_l1_sb(r):
    out = np.full((QC, L1_NJ, 8, 128), -1e9, dtype=np.float32)
    for qc in range(QC):
        g = 2 * qc + r
        for jj in range(L1_NJ):
            ab = 8 * qc + jj - 13
            if ab < 4 * g - 13 - 1 or ab > 4 * g + 3:
                pass
            kpos = ab * 128 + np.arange(128)
            ok = (kpos >= 0) & (ab >= 4 * g - 12) & (ab <= 4 * g + 3)
            out[qc, jj, 0, :] = np.where(ok, np.float32(-4.0), np.float32(-1e9))
    return np.ascontiguousarray(out.transpose(3, 0, 1, 2))


def scatter_stageT(wts):
    out = np.zeros((7, FF_H // 128, 128, 128), dtype=np.float32)
    for st in range(7):
        s = 2 ** st
        k = np.arange(FF_H // 2)
        i = (k // s) * 2 * s + (k % s)
        j = i + s
        w = wts[st]
        ch = i // 128
        il, jl = i % 128, j % 128
        assert np.all(j // 128 == ch)
        out[st, ch, il, il] = w[:, 0, 0]
        out[st, ch, il, jl] = w[:, 0, 1]
        out[st, ch, jl, il] = w[:, 1, 0]
        out[st, ch, jl, jl] = w[:, 1, 1]
    return out


def core_inputs(core, inp):
    seq, r = core // 2, core % 2
    x = np.asarray(inp["L0_attn_states"]).reshape(B * S, D)
    rows = own_rows(core)
    d = {
        "x_own": np.ascontiguousarray(x[rows]),
        "x_seq": np.ascontiguousarray(x[seq * S:(seq + 1) * S]),
        "l0_wq": np.asarray(inp["L0_attn_Wq"]), "l0_bq": np.asarray(inp["L0_attn_bq"]),
        "l0_wk": np.asarray(inp["L0_attn_Wk"]), "l0_bk": np.asarray(inp["L0_attn_bk"]),
        "l0_wv": np.asarray(inp["L0_attn_Wv"]), "l0_bv": np.asarray(inp["L0_attn_bv"]),
        "l0_wo": np.asarray(inp["L0_attn_Wo"]), "l0_bo": np.asarray(inp["L0_attn_bo"]),
        "l0_winmask": make_winmask(r),
        "moe_gw": np.asarray(inp["L0_ffn_gate_w"]),
        "moe_wg": np.asarray(inp["L0_ffn_w_gate"]),
        "moe_wu": np.asarray(inp["L0_ffn_w_up"]),
        "moe_wd": np.asarray(inp["L0_ffn_w_down"]),
        "moe_wsg": np.asarray(inp["L0_ffn_ws_gate"]),
        "moe_wsu": np.asarray(inp["L0_ffn_ws_up"]),
        "moe_wsd": np.asarray(inp["L0_ffn_ws_down"]),
        "l1_wqkvT": np.ascontiguousarray(np.asarray(inp["L1_attn_Wqkv"]).T), "l1_bqkv": np.asarray(inp["L1_attn_bqkv"]),
        "l1_wgateT": np.ascontiguousarray(np.asarray(inp["L1_attn_Wgate"]).T), "l1_bgate": np.asarray(inp["L1_attn_bgate"]),
        "l1_woutT": np.ascontiguousarray(np.asarray(inp["L1_attn_Wout"]).T), "l1_bout": np.asarray(inp["L1_attn_bout"]),
        "l1_btile": make_l1_btile(r, np.asarray(inp["L1_attn_pos_bias"])),
        "l1_sb": make_l1_sb(r),
        "ff_wexp": np.asarray(inp["L1_ffn_W_exp"]), "ff_bexp": np.asarray(inp["L1_ffn_b_exp"]),
        "ff_upT": scatter_stageT(np.asarray(inp["L1_ffn_up_weights"])),
        "ff_dnT": scatter_stageT(np.asarray(inp["L1_ffn_down_weights"])),
        "ff_upw": np.asarray(inp["L1_ffn_up_weights"]),
        "ff_dnw": np.asarray(inp["L1_ffn_down_weights"]),
        "ff_upb": np.asarray(inp["L1_ffn_up_bias"]),
        "ff_wcon": np.asarray(inp["L1_ffn_W_con"]),
        "ff_bsum": np.stack([np.asarray(inp["L1_ffn_b_con"]), np.asarray(inp["L1_ffn_down_bias"])]),
    }
    return d


def assemble_output(results):
    out = np.zeros((B * S, D), dtype=np.float32)
    for core in range(8):
        out[own_rows(core)] = results[core]["out_own"]
    return out.reshape(B, S, D)


from concourse.bass_utils import run_bass_kernel_spmd

_CACHE = {}


def kernel(**inputs):
    if "nc" not in _CACHE:
        nc, inspecs, _ = build()
        _CACHE["nc"] = (nc, inspecs)
    nc, inspecs = _CACHE["nc"]
    in_maps = []
    for c in range(8):
        m = core_inputs(c, inputs)
        in_maps.append({k: m[k] for k in inspecs})
    res = run_bass_kernel_spmd(nc, in_maps, core_ids=list(range(8)))
    return assemble_output(res.results)
